# version 02
# baseline (speedup 1.0000x reference)
import math
from contextlib import ExitStack

import numpy as np
import concourse.bass as bass
import concourse.mybir as mybir
from concourse.bass_utils import run_bass_kernel_spmd

F32 = mybir.dt.float32
BF16 = mybir.dt.bfloat16
AF = mybir.ActivationFunctionType
ALU = mybir.AluOpType
AX = mybir.AxisListType

N_CORES = 8
GELU_C = math.sqrt(2.0 / math.pi)
GELU_A = 0.044715


class Ctr:
    __slots__ = ("sem", "n", "name")

    def __init__(self, sem, name):
        self.sem = sem
        self.n = 0
        self.name = name

    @property
    def now(self):
        return (self, self.n)


class Em:

    AUTO = ("vector", "scalar")

    def __init__(self, nc, es):
        self.nc = nc
        self.es = es
        self.streams = {k: [] for k in ("sync", "scalar", "vector", "tensor", "gpsimd")}
        self._names = {}
        self.pc = {}
        for e in ("vector", "scalar"):
            self.pc[e] = self.ctr(f"{e}_pc")

    def ctr(self, name):
        if name in self._names:
            self._names[name] += 1
            name = f"{name}_{self._names[name]}"
        else:
            self._names[name] = 0
        return Ctr(self.es.enter_context(self.nc.semaphore(name)), name)

    def on(self, eng, fn, wait=(), inc=None, k=1):
        waits = []
        if eng in self.AUTO:
            p = self.pc[eng]
            if p.n > 0:
                waits.append((p, p.n))
            if inc is None:
                inc = p
        for c, v in wait:
            if v > 0:
                waits.append((c, v))
        if inc is not None:
            inc.n += k
        self.streams[eng].append((fn, tuple(waits), inc, k))
        return inc.n if inc is not None else None

    def dma(self, eng, out, in_, inc, wait=(), **kw):
        return self.on(
            eng,
            (lambda e, out=out, in_=in_, kw=kw: e.dma_start(out=out, in_=in_, **kw)),
            wait=wait, inc=inc, k=16,
        )

    def finalize(self, block):
        for eng_name, items in self.streams.items():
            if not items:
                continue

            def make(items):
                def emitfn(eng):
                    for fn, waits, inc, k in items:
                        for c, v in waits:
                            eng.wait_ge(c.sem, v)
                        inst = fn(eng)
                        if inc is not None:
                            inst.then_inc(inc.sem, k)
                return emitfn

            getattr(block, eng_name)(make(items))


def sb(em, name, shape, dtype):
    return em.es.enter_context(em.nc.sbuf_tensor(name, shape, dtype))


def ps(em, name, shape, dtype):
    return em.es.enter_context(em.nc.psum_tensor(name, shape, dtype))



def bcast_row(em, name, src_row_ap, ncols, ones_row, psum_pool, dtype=F32,
              wait=(), scale=None):
    dst = sb(em, name, [128, ncols], dtype)
    tb = em.ctr(f"{name}_t")
    nch = (ncols + 511) // 512
    spc = em.pc["scalar"]
    for c in range(nch):
        c0, c1 = c * 512, min(ncols, (c + 1) * 512)
        em.on("tensor",
              (lambda t, c0=c0, c1=c1: t.matmul(
                  out=psum_pool[:, 0:c1 - c0], lhsT=ones_row[:, 0:128],
                  rhs=src_row_ap[:, c0:c1], start=True, stop=True)),
              wait=(list(wait) + [(spc, spc.n)]), inc=tb)
        em.on("scalar", (lambda s, c0=c0, c1=c1: s.copy(
            out=dst[:, c0:c1], in_=psum_pool[:, 0:c1 - c0])),
            wait=[(tb, tb.n)])
    return dst, em.pc["scalar"].n


def emit_ln_tile(em, out_ap, in_ap, D, scr_col, scr_col2, scr_big, eps=1e-5,
                 out2_ap=None):
    v, s = em.pc["vector"], em.pc["scalar"]
    em.on("vector", lambda e: e.reduce_sum(out=scr_col[:], in_=in_ap, axis=AX.X))
    em.on("vector", lambda e: e.tensor_scalar(
        out=scr_col[:], in0=scr_col[:], scalar1=-1.0 / D, scalar2=None, op0=ALU.mult))
    em.on("scalar", lambda e: e.activation(
        out=scr_big[:], in_=in_ap, func=AF.Square, bias=scr_col[:], scale=1.0,
        accum_out=scr_col2[:]), wait=[v.now])
    em.on("vector", lambda e: e.tensor_scalar(
        out=scr_col2[:], in0=scr_col2[:], scalar1=1.0 / D, scalar2=eps,
        op0=ALU.mult, op1=ALU.add), wait=[s.now])
    em.on("scalar", lambda e: e.activation(
        out=scr_col2[:], in_=scr_col2[:], func=AF.Ln), wait=[v.now])
    em.on("scalar", lambda e: e.activation(
        out=scr_col2[:], in_=scr_col2[:], func=AF.Exp, scale=-0.5))
    em.on("vector", lambda e: e.tensor_tensor(
        out=scr_col[:], in0=scr_col[:], in1=scr_col2[:], op=ALU.mult), wait=[s.now])
    em.on("scalar", lambda e: e.activation(
        out=out_ap, in_=in_ap, func=AF.Identity, scale=scr_col2[:], bias=scr_col[:]),
        wait=[v.now])
    if out2_ap is not None:
        em.on("scalar", lambda e: e.activation(
            out=out2_ap, in_=in_ap, func=AF.Identity, scale=scr_col2[:], bias=scr_col[:]))
    return em.pc["scalar"].n



def emit_attn(em, g, io, mode):
    B_loc, T, D, H = g["B_loc"], g["T"], g["D"], g["H"]
    pfx = "a0" if mode == "l0" else "a1"
    dtile = D // 128
    ntile = T // 128
    NCH = (T + 511) // 512
    HP = H // 2
    vpc, spc = em.pc["vector"], em.pc["scalar"]
    l0 = mode == "l0"

    xT = [sb(em, f"{pfx}_xT{k}", [128, T], BF16) for k in range(dtile)]
    xload = [sb(em, f"{pfx}_xld{j}", [128, D], BF16) for j in range(2)]
    wslab = [sb(em, f"{pfx}_w{j}", [128, dtile * 3 * 128], BF16) for j in range(2)]
    bcols = sb(em, f"{pfx}_bc", [128, 3 * dtile], F32)
    qkvT = [sb(em, f"{pfx}_qkv{j}", [128, T], BF16) for j in range(3)]
    v_tm = sb(em, f"{pfx}_vtm", [128, ntile * 64], BF16)
    lcol = sb(em, f"{pfx}_l", [128, 1], F32)
    rbuf = sb(em, f"{pfx}_r", [128, T], BF16)
    rT = [sb(em, f"{pfx}_rT{j}", [128, 128], BF16) for j in range(2)]
    ytb = [sb(em, f"{pfx}_yt{k}", [128, T], BF16) for k in range(dtile)]
    wproj = [sb(em, f"{pfx}_wp{k}", [128, D], BF16) for k in range(dtile)]
    xres = [sb(em, f"{pfx}_xr{j}", [128, D], F32) for j in range(2)]
    t1 = sb(em, f"{pfx}_t1", [128, D], F32)
    scr_ln = sb(em, f"{pfx}_lnscr", [128, D], F32)
    c1 = sb(em, f"{pfx}_c1", [128, 1], F32)
    c2 = sb(em, f"{pfx}_c2", [128, 1], F32)
    x1t = sb(em, f"{pfx}_x1t", [128, D], F32)
    x3Tt = sb(em, f"{pfx}_x3Tt", [128, D], F32)
    gwT_s = sb(em, f"{pfx}_gwT", [128, 8 * dtile], F32)
    sc8 = sb(em, f"{pfx}_sc8", [128, 8], F32)
    g1 = sb(em, f"{pfx}_g1", [128, 1], F32)
    gd = sb(em, f"{pfx}_gd", [128, 1], F32)
    idn32_s = sb(em, f"{pfx}_idn32", [128, 128], F32)
    xoff = max(512, D)
    if l0:
        kbar = sb(em, f"{pfx}_kbar", [128, 1], F32)
        kbar_b = sb(em, f"{pfx}_kbarb", [128, 1], BF16)
        negm = sb(em, f"{pfx}_negm", [128, ntile], F32)
        madc = sb(em, f"{pfx}_mad", [128, 1], F32)
        gcol = sb(em, f"{pfx}_gp", [128, 1], F32)
        bias2 = sb(em, f"{pfx}_b2", [128, 1], F32)
        scr_abs = sb(em, f"{pfx}_scr", [128, T], F32)
        ubuf = sb(em, f"{pfx}_u", [128, T], F32)
        wbuf = sb(em, f"{pfx}_w2", [128, T], F32)
    else:
        qf = sb(em, f"{pfx}_qf", [128, T], F32)
        kf = sb(em, f"{pfx}_kf", [128, T], F32)
        swp = sb(em, f"{pfx}_swp", [128, T], F32)
        ncol = sb(em, f"{pfx}_ncol", [128, 1], F32)
        trig = {}
        for nm in ("cosq", "sinq"):
            trig[nm] = sb(em, f"{pfx}_{nm}", [128, T], F32)
        bck = sb(em, f"{pfx}_bck", [128, dtile], F32)
        msk = sb(em, f"{pfx}_msk", [128, 8], F32)
        w8 = sb(em, f"{pfx}_w8", [128, 8], F32)
        g2 = sb(em, f"{pfx}_g2", [128, 1], F32)

    psum_s = ps(em, f"{pfx}_ps_s", [128, max(T, xoff + 128)], F32)
    psum_aux = ps(em, f"{pfx}_ps_aux", [128, 512], F32)
    psum_rT = [ps(em, f"{pfx}_ps_rT{j}", [128, 1024], BF16) for j in range(2)]

    ld_w = [em.ctr(f"{pfx}_ldw0"), em.ctr(f"{pfx}_ldw1")]
    ld_wp = em.ctr(f"{pfx}_ldwp")
    ld_x = [em.ctr(f"{pfx}_ldx0"), em.ctr(f"{pfx}_ldx1")]
    ld_r = [em.ctr(f"{pfx}_ldr0"), em.ctr(f"{pfx}_ldr1")]
    ld_c = em.ctr(f"{pfx}_ldc")
    ld_sw = em.ctr(f"{pfx}_ldsw")
    st_o = em.ctr(f"{pfx}_sto")
    tqkv = em.ctr(f"{pfx}_tqkv")
    tx = em.ctr(f"{pfx}_tx")
    tv = em.ctr(f"{pfx}_tv")
    tmm = em.ctr(f"{pfx}_tmm")
    tav = em.ctr(f"{pfx}_tav")
    ttr = em.ctr(f"{pfx}_ttr")
    tm = em.ctr(f"{pfx}_tm")
    tp = em.ctr(f"{pfx}_tp")
    tgt = em.ctr(f"{pfx}_tgt")

    em.dma("sync", bcols[:, 0:3 * dtile], io["b_attn_cols"][:, :], ld_c)
    for k in range(dtile):
        em.dma("gpsimd", wproj[k][:, :], io["W_proj"][k * 128:(k + 1) * 128, :], ld_wp)
    ones_s = sb(em, f"{pfx}_ones", [1, 128], F32)
    bprow = sb(em, f"{pfx}_bprow", [1, D], F32)
    em.dma("sync", ones_s[:, :], io["ones_f32"][:, :], ld_c)
    em.dma("sync", bprow[:, :], io["b_proj_row"][:, :], ld_c)
    idn_s = sb(em, f"{pfx}_idn", [128, 128], BF16)
    em.dma("sync", idn_s[:, :], io["idn"][:, :], ld_c)
    if l0:
        gain_s = sb(em, f"{pfx}_gain11", [1, 1], F32)
        em.dma("sync", gain_s[:, :], io["gain11"][:, :], ld_c)
        tri_s = sb(em, f"{pfx}_tri", [128, 128], F32)
        em.dma("sync", tri_s[:, :], io["tri"][:, :], ld_c)
    else:
        for nm in ("cosq", "sinq"):
            em.dma("sync", trig[nm][:, :], io[nm][:, :], ld_c)
    em.dma("sync", idn32_s[:, :], io["idn32"][:, :], ld_c)
    em.dma("sync",
           gwT_s.rearrange("p (k e) -> p k e", e=8),
           io["gwT"].rearrange("(k p) e -> p k e", p=128), ld_c)
    n_const = ld_c.n
    if not l0:
        em.on("vector", (lambda v: v.tensor_scalar(
            out=bck[:, :], in0=bcols[:, dtile:2 * dtile], scalar1=0.125,
            scalar2=None, op0=ALU.mult)), wait=[(ld_c, n_const)])
    bb, _ = bcast_row(em, f"{pfx}_bb", bprow, D, ones_s, psum_s, F32,
                      wait=[(ld_c, n_const)])
    if l0:
        gain_col, _ = bcast_row(em, f"{pfx}_gaincol", gain_s, 1, ones_s, psum_aux,
                                F32, wait=[(ld_c, n_const)])

    xslot_last = [0, 0]
    wslot_last = [0, 0]
    xslot_uses = [0, 0]
    wslot_uses = [0, 0]
    rope_v_last = 0
    for b in range(B_loc):
        tok0 = b * T
        for i in range(ntile):
            slot = i % 2
            em.dma("gpsimd", xload[slot][:, :],
                   io["x_in"][tok0 + i * 128: tok0 + (i + 1) * 128, :], ld_x[slot],
                   wait=[(tx, xslot_last[slot])] if xslot_last[slot] else ())
            xslot_uses[slot] += 1
            for k in range(dtile):
                pslot = k % 2
                w0 = [(spc, spc.n)]
                if k == 0:
                    w0 += [(ld_x[slot], xslot_uses[slot] * 16), (ld_c, n_const)]
                if b > 0 and i == 0 and k == 0:
                    w0.append((tqkv, b * HP * 3 * NCH))
                em.on("tensor",
                      (lambda t, slot=slot, k=k, pslot=pslot: t.transpose(
                          out=psum_rT[pslot][:, 0:128],
                          in_=xload[slot][:, k * 128:(k + 1) * 128],
                          identity=idn_s[:, :])),
                      wait=w0, inc=tx)
                em.on("scalar",
                      (lambda s, k=k, i=i, pslot=pslot: s.copy(
                          out=xT[k][:, i * 128:(i + 1) * 128],
                          in_=psum_rT[pslot][:, 0:128])),
                      wait=[(tx, tx.n)])
            xslot_last[slot] = tx.n
        xT_ready_s = em.pc["scalar"].n

        for hp in range(HP):
            wslot = hp % 2
            wa4 = io["W_attn"].rearrange("(kt p) (g c) -> p kt g c", p=128, g=3)[
                :, :, :, hp * 128:(hp + 1) * 128]
            hp_glob = b * HP + hp
            for gq in range(3):
                em.dma("gpsimd",
                       wslab[wslot][:, gq * dtile * 128:(gq + 1) * dtile * 128]
                       .rearrange("p (kt c) -> p kt c", kt=dtile),
                       wa4[:, :, gq, :],
                       ld_w[wslot],
                       wait=([(tqkv, wslot_last[wslot])]
                             if (wslot_last[wslot] and gq == 0) else ()))
            wslot_uses[wslot] += 1
            wready = (ld_w[wslot], ld_w[wslot].n)
            for n in range(NCH):
                n0, n1 = n * 512, min(T, (n + 1) * 512)
                for j in range(3):
                    for kt in range(dtile):
                        em.on("tensor",
                              (lambda t, j=j, kt=kt, n0=n0, n1=n1, wslot=wslot:
                               t.matmul(
                                   out=psum_aux[:, 0:n1 - n0],
                                   lhsT=wslab[wslot][:, (j * dtile + kt) * 128:
                                                     (j * dtile + kt + 1) * 128],
                                   rhs=xT[kt][:, n0:n1],
                                   start=(kt == 0), stop=(kt == dtile - 1))),
                              wait=([wready, (spc, spc.n)] if kt == 0 else ()),
                              inc=tqkv if kt == dtile - 1 else None)
                    if l0 or j == 2:
                        dst = qkvT[j]
                    else:
                        dst = qf if j == 0 else kf
                    ewait = [(tqkv, tqkv.n), (ld_c, n_const)]
                    if (not l0) and j < 2 and n == 0 and rope_v_last:
                        ewait += [(vpc, rope_v_last), (ld_sw, ld_sw.n)]
                    if (not l0) and j == 1:
                        bias_ap = bck[:, hp:hp + 1]
                        sc_ = 0.125
                    else:
                        bias_ap = bcols[:, j * dtile + hp:j * dtile + hp + 1]
                        sc_ = 1.0
                    if (not l0) and j < 2 and n == 0:
                        ewait.append((vpc, vpc.n))
                    em.on("scalar",
                          (lambda s, n0=n0, n1=n1, dst=dst, bias_ap=bias_ap,
                           sc_=sc_: s.activation(
                               out=dst[:, n0:n1], in_=psum_aux[:, 0:n1 - n0],
                               func=AF.Identity, bias=bias_ap, scale=sc_)),
                          wait=ewait)
            wslot_last[wslot] = tqkv.n
            qkv_ready_s = em.pc["scalar"].n

            if not l0:
                for (srcb, dstj) in ((qf, 0), (kf, 1)):
                    for (d0, s0) in ((0, 32), (32, 0), (64, 96), (96, 64)):
                        em.dma("sync", swp[d0:d0 + 32, :], srcb[s0:s0 + 32, :],
                               ld_sw,
                               wait=([(spc, qkv_ready_s), (vpc, vpc.n)]
                                     if d0 == 0 else ()))
                    em.on("vector",
                          (lambda v: v.tensor_tensor(
                              out=swp[:, :], in0=swp[:, :], in1=trig["sinq"][:, :],
                              op=ALU.mult)),
                          wait=[(ld_sw, ld_sw.n), (ld_c, n_const)])
                    em.on("vector",
                          (lambda v, srcb=srcb: v.tensor_tensor(
                              out=srcb[:, :], in0=srcb[:, :], in1=trig["cosq"][:, :],
                              op=ALU.mult)),
                          wait=[(ld_sw, ld_sw.n)])
                    em.on("vector",
                          (lambda v, srcb=srcb, dstj=dstj: v.tensor_tensor(
                              out=qkvT[dstj][:, :], in0=srcb[:, :], in1=swp[:, :],
                              op=ALU.add)))
                rope_v_last = vpc.n
                qkv_ready_s = em.pc["scalar"].n

            for hh in range(2):
                base = hh * 64
                if l0:
                    em.on("vector", (lambda v, base=base: v.reduce_sum(
                        out=kbar[base:base + 64, :],
                        in_=qkvT[1][base:base + 64, :], axis=AX.X)),
                        wait=[(spc, qkv_ready_s)])
                    em.on("vector", (lambda v, base=base: v.tensor_scalar(
                        out=kbar_b[base:base + 64, :], in0=kbar[base:base + 64, :],
                        scalar1=1.0 / T, scalar2=None, op0=ALU.mult)))
                    kbar_v = vpc.n
                for kt in range(ntile):
                    pslot = kt % 2
                    vw = [(spc, spc.n)]
                    if (not l0) and kt == 0:
                        vw.append((vpc, rope_v_last))
                    em.on("tensor",
                          (lambda t, kt=kt, base=base, pslot=pslot: t.transpose(
                              out=psum_rT[pslot][:, 0:64],
                              in_=qkvT[2][base:base + 64, kt * 128:(kt + 1) * 128],
                              identity=idn_s[base:base + 64, base:base + 64])),
                          wait=vw, inc=tv)
                    em.on("scalar",
                          (lambda s, kt=kt, pslot=pslot: s.copy(
                              out=v_tm[:, kt * 64:(kt + 1) * 64],
                              in_=psum_rT[pslot][:, 0:64])),
                          wait=[(tv, tv.n)])
                if l0:
                    for i in range(ntile):
                        em.on("tensor",
                              (lambda t, i=i, base=base: t.matmul(
                                  out=psum_aux[:, i:i + 1],
                                  lhsT=qkvT[0][base:base + 64, i * 128:(i + 1) * 128],
                                  rhs=kbar_b[base:base + 64, :],
                                  start=True, stop=True)),
                              wait=([(vpc, kbar_v), (spc, spc.n)] if i == 0 else ()),
                              inc=tm if i == ntile - 1 else None)
                    em.on("scalar", (lambda s: s.activation(
                        out=negm[:, 0:ntile], in_=psum_aux[:, 0:ntile],
                        func=AF.Identity, scale=-1.0)), wait=[(tm, tm.n)])
                    negm_s = em.pc["scalar"].n

                for i in range(ntile):
                    pre = (i + 1) * 128 if l0 else T
                    nkt = (i + 1) if l0 else ntile
                    for n in range(NCH):
                        n0, n1 = n * 512, min(T, (n + 1) * 512)
                        sw = [(spc, spc.n)]
                        if (not l0) and i == 0 and n == 0:
                            sw.append((vpc, rope_v_last))
                        em.on("tensor",
                              (lambda t, i=i, base=base, n0=n0, n1=n1: t.matmul(
                                  out=psum_s[:, n0:n1],
                                  lhsT=qkvT[0][base:base + 64, i * 128:(i + 1) * 128],
                                  rhs=qkvT[1][base:base + 64, n0:n1],
                                  start=True, stop=True)),
                              wait=(sw if n == 0 else ()),
                              inc=tmm if n == NCH - 1 else None)
                    if l0:
                        em.on("scalar",
                              (lambda s, i=i: s.activation(
                                  out=scr_abs[:, 0:T], in_=psum_s[:, 0:T],
                                  func=AF.Abs, bias=negm[:, i:i + 1], scale=1.0,
                                  accum_out=madc[:])),
                              wait=[(tmm, tmm.n), (spc, negm_s)])
                        em.on("vector", (lambda v: v.tensor_scalar(
                            out=madc[:], in0=madc[:], scalar1=1.0 / T, scalar2=1e-6,
                            op0=ALU.mult, op1=ALU.add)), wait=[(spc, spc.n)])
                        em.on("vector", (lambda v: v.reciprocal(out=madc[:], in_=madc[:])))
                        em.on("vector", (lambda v: v.tensor_tensor(
                            out=gcol[:], in0=madc[:], in1=gain_col[:, 0:1],
                            op=ALU.mult)))
                        em.on("vector", (lambda v, i=i: v.tensor_tensor(
                            out=bias2[:], in0=negm[:, i:i + 1], in1=gcol[:],
                            op=ALU.mult)))
                        em.on("scalar",
                              (lambda s, pre=pre: s.activation(
                                  out=ubuf[:, 0:pre], in_=psum_s[:, 0:pre],
                                  func=AF.Relu, scale=gcol[:], bias=bias2[:])),
                              wait=[(vpc, vpc.n)])
                        relu_s = em.pc["scalar"].n
                        em.on("vector", (lambda v, i=i, pre=pre: v.tensor_tensor(
                            out=ubuf[:, i * 128:pre], in0=ubuf[:, i * 128:pre],
                            in1=tri_s[:, :], op=ALU.mult)), wait=[(spc, relu_s)])
                        em.on("vector", (lambda v, pre=pre: v.tensor_tensor(
                            out=wbuf[:, 0:pre], in0=ubuf[:, 0:pre],
                            in1=ubuf[:, 0:pre], op=ALU.mult)))
                        em.on("scalar",
                              (lambda s, pre=pre: s.activation(
                                  out=rbuf[:, 0:pre], in_=wbuf[:, 0:pre],
                                  func=AF.Square, accum_out=lcol[:])),
                              wait=[(vpc, vpc.n)])
                        em.on("vector", (lambda v: v.tensor_scalar(
                            out=lcol[:], in0=lcol[:], scalar1=1e-6, scalar2=None,
                            op0=ALU.add)), wait=[(spc, spc.n)])
                    else:
                        em.on("vector", (lambda v: v.reduce_max(
                            out=ncol[:], in_=psum_s[:, 0:T], axis=AX.X,
                            negate=True)), wait=[(tmm, tmm.n)])
                        em.on("scalar",
                              (lambda s: s.activation(
                                  out=rbuf[:, 0:T], in_=psum_s[:, 0:T], func=AF.Exp,
                                  bias=ncol[:], scale=1.0, accum_out=lcol[:])),
                              wait=[(vpc, vpc.n)])
                    em.on("vector", (lambda v: v.reciprocal(out=lcol[:], in_=lcol[:])),
                          wait=[(spc, spc.n)])
                    em.on("vector", (lambda v, pre=pre: v.tensor_scalar(
                        out=rbuf[:, 0:pre], in0=rbuf[:, 0:pre], scalar1=lcol[:],
                        scalar2=None, op0=ALU.mult)))
                    att_v = vpc.n
                    for kt in range(nkt):
                        slot = kt % 2
                        em.on("tensor",
                              (lambda t, kt=kt, slot=slot: t.transpose(
                                  out=psum_rT[slot][:, 0:128],
                                  in_=rbuf[:, kt * 128:(kt + 1) * 128],
                                  identity=idn_s[:, :])),
                              wait=([(vpc, att_v)] if kt == 0 else
                                    [(spc, spc.n)] if kt >= 2 else ()),
                              inc=ttr)
                        em.on("scalar",
                              (lambda s, slot=slot: s.copy(
                                  out=rT[slot][:, :], in_=psum_rT[slot][:, 0:128])),
                              wait=[(ttr, ttr.n)])
                        rt_s = em.pc["scalar"].n
                        em.on("tensor",
                              (lambda t, kt=kt, slot=slot, last=(kt == nkt - 1):
                               t.matmul(
                                   out=psum_aux[0:64, 128:256],
                                   lhsT=v_tm[:, kt * 64:(kt + 1) * 64],
                                   rhs=rT[slot][:, :],
                                   start=(kt == 0), stop=last)),
                              wait=[(spc, rt_s)],
                              inc=tav if kt == nkt - 1 else None)
                    em.on("scalar",
                          (lambda s, hp=hp, base=base, i=i: s.copy(
                              out=ytb[hp][base:base + 64, i * 128:(i + 1) * 128],
                              in_=psum_aux[0:64, 128:256])),
                          wait=[(tav, tav.n)])

        for i in range(ntile):
            slot = i % 2
            em.dma("sync", xres[slot][:, :],
                   io["x_in"][tok0 + i * 128: tok0 + (i + 1) * 128, :], ld_r[slot],
                   wait=[(vpc, vpc.n)] if i >= 2 else ())
            NDC = (D + 511) // 512
            for nc_ in range(NDC):
                n0, n1 = nc_ * 512, min(D, (nc_ + 1) * 512)
                for kt in range(dtile):
                    em.on("tensor",
                          (lambda t, i=i, kt=kt, n0=n0, n1=n1: t.matmul(
                              out=psum_s[:, n0:n1],
                              lhsT=ytb[kt][:, i * 128:(i + 1) * 128],
                              rhs=wproj[kt][:, n0:n1],
                              start=(kt == 0), stop=(kt == dtile - 1))),
                          wait=([(ld_wp, dtile * 16), (spc, spc.n)]
                                if (kt == 0 and nc_ == 0) else ()),
                          inc=tp if (kt == dtile - 1 and nc_ == NDC - 1) else None)
            em.on("scalar",
                  (lambda s: s.copy(out=t1[:, :], in_=psum_s[:, 0:D])),
                  wait=[(tp, tp.n), (vpc, vpc.n)])
            em.on("vector", (lambda v, slot=slot: v.tensor_tensor(
                out=t1[:, :], in0=t1[:, :], in1=xres[slot][:, :], op=ALU.add)),
                wait=[(spc, spc.n), (ld_r[slot], ld_r[slot].n)])
            em.on("vector", (lambda v: v.tensor_tensor(
                out=t1[:, :], in0=t1[:, :], in1=bb[:, 0:D], op=ALU.add)))
            emit_ln_tile(em, x1t[:, :], t1[:, :], D, c1, c2, scr_ln)
            em.dma("sync", io["x1_out"][tok0 + i * 128: tok0 + (i + 1) * 128, :],
                   x1t[:, :], st_o, wait=[(spc, spc.n)])
            if True:
                for k in range(dtile):
                    em.on("tensor",
                          (lambda t, k=k: t.transpose(
                              out=psum_s[:, xoff:xoff + 128],
                              in_=x1t[:, k * 128:(k + 1) * 128],
                              identity=idn32_s[:, :])),
                          wait=[(spc, spc.n)], inc=tgt)
                    em.on("scalar",
                          (lambda s, k=k: s.copy(
                              out=x3Tt[:, k * 128:(k + 1) * 128],
                              in_=psum_s[:, xoff:xoff + 128])),
                          wait=[(tgt, tgt.n)])
                for k in range(dtile):
                    em.on("tensor",
                          (lambda t, k=k: t.matmul(
                              out=psum_aux[:, 256:264],
                              lhsT=x3Tt[:, k * 128:(k + 1) * 128],
                              rhs=gwT_s[:, k * 8:(k + 1) * 8],
                              start=(k == 0), stop=(k == dtile - 1))),
                          wait=([(spc, spc.n)] if k == 0 else ()),
                          inc=tgt if k == dtile - 1 else None)
                em.on("vector", (lambda v: v.reduce_max(
                    out=g1[:], in_=psum_aux[:, 256:264], axis=AX.X, negate=True)),
                    wait=[(tgt, tgt.n)])
                em.on("scalar",
                      (lambda s: s.activation(
                          out=sc8[:, :], in_=psum_aux[:, 256:264], func=AF.Exp,
                          bias=g1[:], scale=1.0, accum_out=gd[:])),
                      wait=[(vpc, vpc.n)])
                em.on("vector", (lambda v: v.reciprocal(out=gd[:], in_=gd[:])),
                      wait=[(spc, spc.n)])
                em.on("vector", (lambda v: v.tensor_scalar(
                    out=sc8[:, :], in0=sc8[:, :], scalar1=gd[:], scalar2=None,
                    op0=ALU.mult)))
            if l0:
                em.dma("sync", io["scores_out"][tok0 + i * 128: tok0 + (i + 1) * 128, :],
                       sc8[:, :], st_o, wait=[(vpc, vpc.n)])
                em.on("vector", (lambda v: v.tensor_copy(out=g1[:], in_=g1[:])),
                      wait=[(st_o, st_o.n)])
            else:
                em.on("vector", (lambda v: v.reduce_max(
                    out=g1[:], in_=sc8[:, :], axis=AX.X)))
                em.on("vector", (lambda v: v.tensor_scalar(
                    out=msk[:, :], in0=sc8[:, :], scalar1=g1[:], scalar2=None,
                    op0=ALU.is_ge)))
                em.on("vector", (lambda v: v.scalar_tensor_tensor(
                    out=w8[:, :], in0=msk[:, :], scalar=-1e9, in1=sc8[:, :],
                    op0=ALU.mult, op1=ALU.add)))
                em.on("vector", (lambda v: v.reduce_max(
                    out=g2[:], in_=w8[:, :], axis=AX.X)))
                em.on("vector", (lambda v: v.tensor_scalar(
                    out=msk[:, :], in0=sc8[:, :], scalar1=g2[:], scalar2=None,
                    op0=ALU.is_ge)))
                em.on("vector", (lambda v: v.tensor_tensor(
                    out=g1[:], in0=g1[:], in1=g2[:], op=ALU.add)))
                em.on("vector", (lambda v: v.tensor_scalar(
                    out=g1[:], in0=g1[:], scalar1=1e-20, scalar2=None, op0=ALU.add)))
                em.on("vector", (lambda v: v.reciprocal(out=g1[:], in_=g1[:])))
                em.on("vector", (lambda v: v.tensor_tensor(
                    out=w8[:, :], in0=sc8[:, :], in1=msk[:, :], op=ALU.mult)))
                em.on("vector", (lambda v: v.tensor_scalar(
                    out=w8[:, :], in0=w8[:, :], scalar1=g1[:], scalar2=None,
                    op0=ALU.mult)))
                em.dma("sync", io["wdense_out"][tok0 + i * 128: tok0 + (i + 1) * 128, :],
                       w8[:, :], st_o, wait=[(vpc, vpc.n)])
                em.on("vector", (lambda v: v.tensor_copy(out=g2[:], in_=g2[:])),
                      wait=[(st_o, st_o.n)])
            em.on("scalar", (lambda s: s.nop()),
                  wait=[(st_o, st_o.n), (tgt, tgt.n)])
    return st_o


def np_bf16(x):
    import ml_dtypes
    return np.asarray(x, dtype=ml_dtypes.bfloat16)






def emit_gelu_chain(em, psum_in, dst_bf16_ap, n, h, hsq, b1col, wait_t, ld_c, nldc):
    spc, vpc = em.pc["scalar"], em.pc["vector"]
    em.on("scalar",
          (lambda s: s.activation(out=h[:, 0:n], in_=psum_in[:, 0:n],
                                  func=AF.Identity, bias=b1col, scale=1.0)),
          wait=[wait_t, (ld_c, nldc), (vpc, vpc.n)])
    em.on("vector", (lambda v: v.tensor_tensor(
        out=hsq[:, 0:n], in0=h[:, 0:n], in1=h[:, 0:n], op=ALU.mult)),
        wait=[(spc, spc.n)])
    em.on("vector", (lambda v: v.tensor_scalar(
        out=hsq[:, 0:n], in0=hsq[:, 0:n], scalar1=GELU_A, scalar2=1.0,
        op0=ALU.mult, op1=ALU.add)))
    em.on("vector", (lambda v: v.tensor_tensor(
        out=hsq[:, 0:n], in0=hsq[:, 0:n], in1=h[:, 0:n], op=ALU.mult)))
    em.on("scalar",
          (lambda s: s.activation(out=hsq[:, 0:n], in_=hsq[:, 0:n], func=AF.Tanh,
                                  scale=GELU_C)),
          wait=[(vpc, vpc.n)])
    em.on("vector", (lambda v: v.scalar_tensor_tensor(
        out=dst_bf16_ap, in0=hsq[:, 0:n], scalar=1.0, in1=h[:, 0:n],
        op0=ALU.add, op1=ALU.mult)),
        wait=[(spc, spc.n)])


def emit_glu_chain(em, psum_g, psum_u, dst_bf16_ap, n, h, wait_t):
    spc, vpc = em.pc["scalar"], em.pc["vector"]
    em.on("scalar",
          (lambda s: s.activation(out=h[:, 0:n], in_=psum_g[:, 0:n], func=AF.Silu)),
          wait=[wait_t, (vpc, vpc.n)])
    em.on("vector", (lambda v: v.tensor_tensor(
        out=dst_bf16_ap, in0=h[:, 0:n], in1=psum_u[:, 0:n], op=ALU.mult)),
        wait=[(spc, spc.n)])


def emit_bcast_into(em, dst, src_row_ap, ncols, ones_row, psum_pool, tbc, wait=()):
    spc, vpc = em.pc["scalar"], em.pc["vector"]
    nch = (ncols + 511) // 512
    for c in range(nch):
        c0, c1 = c * 512, min(ncols, (c + 1) * 512)
        em.on("tensor",
              (lambda t, c0=c0, c1=c1: t.matmul(
                  out=psum_pool[:, 0:c1 - c0], lhsT=ones_row[:, 0:128],
                  rhs=src_row_ap[:, c0:c1], start=True, stop=True)),
              wait=(list(wait) + [(spc, spc.n), (vpc, vpc.n)]), inc=tbc)
        em.on("scalar", (lambda s, c0=c0, c1=c1: s.copy(
            out=dst[:, c0:c1], in_=psum_pool[:, 0:c1 - c0])),
            wait=[(tbc, tbc.n)])
    return em.pc["scalar"].n


def emit_moe_ffn(em, g, io, mode):
    Tl, E, D = g["Tl"], g["E"], g["D"]
    F_e, F_s, TBLK = g["F_e"], g["F_s"], g["TBLK"]
    S_tot = g["S_tot"]
    NHALF = g.get("NHALF", 2 if mode == "l0" else 1)
    l0 = mode == "l0"
    pfx = "f0" if l0 else "f1"
    dtile = D // 128
    ntile_l = Tl // 128
    NB = Tl // TBLK
    TPB = TBLK // 128
    Fh = F_e // NHALF
    fth = Fh // 128
    fts = F_s // 128
    NDC = (D + 511) // 512
    vpc, spc = em.pc["vector"], em.pc["scalar"]
    SPL = fth if NHALF > 1 else (fts + 1) // 2
    assert SPL >= fth and fts - SPL <= SPL and fts - SPL >= 0
    widthA = max(Fh, SPL * 128)
    widthB = max(Fh, (fts - SPL) * 128)

    upA = [sb(em, f"{pfx}_upA{k}", [128, widthA], BF16) for k in range(dtile)]
    upB = [sb(em, f"{pfx}_upB{k}", [128, widthB], BF16) for k in range(dtile)]
    if not l0:
        up2A = [sb(em, f"{pfx}_u2A{k}", [128, widthA], BF16) for k in range(dtile)]
        up2B = [sb(em, f"{pfx}_u2B{k}", [128, widthB], BF16) for k in range(dtile)]
    dnA = [sb(em, f"{pfx}_dnA{k}", [128, D], BF16) for k in range(SPL)]
    dnB = [sb(em, f"{pfx}_dnB{k}", [128, D], BF16)
           for k in range(max(fth, fts - SPL))]
    hT = sb(em, f"{pfx}_hT", [128, fts * TBLK], BF16)
    xblk = [sb(em, f"{pfx}_xb{j}", [128, dtile * TBLK], BF16) for j in range(2)]
    h_scr = sb(em, f"{pfx}_hscr", [128, max(TBLK, 512)], F32)
    q_scr = sb(em, f"{pfx}_qscr", [128, max(TBLK, 512)], F32)
    gyst = [sb(em, f"{pfx}_gyst{j}", [128, D], F32) for j in range(2)]
    glp = sb(em, f"{pfx}_glp", [128, ntile_l * E], F32)
    glp_h = sb(em, f"{pfx}_glph", [128, ntile_l * E], F32) if l0 else None
    bcols = sb(em, f"{pfx}_b1c", [128, max(E * F_e, F_s) // 128], F32) if l0 else None
    b2b = sb(em, f"{pfx}_b2b", [128, D], F32) if l0 else None
    ones_s = sb(em, f"{pfx}_ones", [1, 128], F32)
    onesc_s = sb(em, f"{pfx}_onesc", [128, 1], F32)
    idn_s = sb(em, f"{pfx}_idn", [128, 128], BF16)
    cA = sb(em, f"{pfx}_cA", [128, 1], F32)
    cB = sb(em, f"{pfx}_cB", [128, 1], F32)
    cC = sb(em, f"{pfx}_cC", [128, 1], F32)
    cD = sb(em, f"{pfx}_cD", [128, 1], F32)
    cE = sb(em, f"{pfx}_cE", [128, 1], F32)
    c11 = sb(em, f"{pfx}_c11", [1, 1], F32)
    b2row_s = sb(em, f"{pfx}_b2row", [1, D], F32) if l0 else None

    psum_h = ps(em, f"{pfx}_ph", [128, 512], F32)
    psum_h2 = ps(em, f"{pfx}_ph2", [128, 512], F32)
    psum_y = [ps(em, f"{pfx}_py{j}", [128, 512], F32) for j in range(2)]
    psum_b = ps(em, f"{pfx}_pb", [128, 512], F32)
    psum_tb = [ps(em, f"{pfx}_pt{j}", [128, 1024], BF16) for j in range(2)]

    ld_c = em.ctr(f"{pfx}_ldc")
    ld_w = [em.ctr(f"{pfx}_ldw0"), em.ctr(f"{pfx}_ldw1")]
    ld_xb = [em.ctr(f"{pfx}_lxb0"), em.ctr(f"{pfx}_lxb1")]
    ld_xt = [em.ctr(f"{pfx}_lxt0"), em.ctr(f"{pfx}_lxt1")]
    ld_res = [em.ctr(f"{pfx}_lr0"), em.ctr(f"{pfx}_lr1")]
    ld_gy = [em.ctr(f"{pfx}_lg0"), em.ctr(f"{pfx}_lg1")]
    ld_sc = em.ctr(f"{pfx}_lsc")
    ld_bs = em.ctr(f"{pfx}_lbs")
    st_t = em.ctr(f"{pfx}_stt")
    st_acc = [em.ctr(f"{pfx}_sac0"), em.ctr(f"{pfx}_sac1")]
    st_o = em.ctr(f"{pfx}_sto")
    tx = em.ctr(f"{pfx}_tx")
    tup = em.ctr(f"{pfx}_tup")
    tdn = em.ctr(f"{pfx}_tdn")
    tb_ = em.ctr(f"{pfx}_tb")
    ld_b2 = em.ctr(f"{pfx}_lb2")

    em.dma("sync", ones_s[:, :], io["ones_f32"][:, :], ld_c)
    em.dma("sync", onesc_s[:, :], io["ones_col"][:, :], ld_c)
    em.dma("sync", idn_s[:, :], io["idn"][:, :], ld_c)
    if l0:
        em.dma("sync", bcols[:, 0:E * F_e // 128], io["b1_all_cols"][:, :], ld_c)
    n_const0 = ld_c.n

    esA = ExitStack()
    xld = [esA.enter_context(em.nc.sbuf_tensor(f"{pfx}_xld{j}", [128, D], BF16))
           for j in range(2)]
    xtc = [esA.enter_context(em.nc.sbuf_tensor(f"{pfx}_xtc{j}", [128, dtile * 128],
                                               BF16)) for j in range(2)]
    xt_last = [0, 0]
    xt_uses = [0, 0]
    for i in range(ntile_l):
        slot = i % 2
        em.dma("gpsimd", xld[slot][:, :],
               io["x1_in"][i * 128:(i + 1) * 128, :], ld_xt[slot],
               wait=[(tx, xt_last[slot])] if xt_last[slot] else ())
        xt_uses[slot] += 1
        for k in range(dtile):
            pslot = k % 2
            w0 = [(spc, spc.n)]
            if k == 0:
                w0 += [(ld_xt[slot], xt_uses[slot] * 16), (ld_c, n_const0)]
                if i >= 2:
                    w0.append((st_t, (i - 1) * 16))
            em.on("tensor",
                  (lambda t, slot=slot, k=k, pslot=pslot: t.transpose(
                      out=psum_tb[pslot][:, 0:128],
                      in_=xld[slot][:, k * 128:(k + 1) * 128],
                      identity=idn_s[:, :])),
                  wait=w0, inc=tx)
            em.on("scalar",
                  (lambda s, k=k, slot=slot, pslot=pslot: s.copy(
                      out=xtc[slot][:, k * 128:(k + 1) * 128],
                      in_=psum_tb[pslot][:, 0:128])),
                  wait=[(tx, tx.n)])
        xt_last[slot] = tx.n
        em.dma("sync",
               io["x1T_d"].rearrange("(dt p) t -> p dt t", p=128)[
                   :, :, i * 128:(i + 1) * 128],
               xtc[slot].rearrange("p (dt c) -> p dt c", dt=dtile),
               st_t, wait=[(spc, spc.n)])
    n_xt = st_t.n

    em.dma("sync", glp[:, :], io["g_loc_packed"][:, :], ld_sc)
    if l0:
        esB = ExitStack()
        PCK = S_tot * E // 128
        sc_pack = esB.enter_context(em.nc.sbuf_tensor(f"{pfx}_scp", [128, PCK], F32))
        msk_p = esB.enter_context(em.nc.sbuf_tensor(f"{pfx}_mskp", [128, PCK], F32))
        em.dma("sync", sc_pack[:, :], io["sc_packed"][:, :], ld_sc)
        KCAP = float(g["CAP"] * S_tot)
        lo, hi, mid, cnt = cB, cC, cD, cE
        em.on("vector", (lambda v: v.memset(lo[:], 0.0)),
              wait=[(ld_sc, 32), (ld_c, n_const0)])
        em.on("vector", (lambda v: v.memset(hi[:], 2.0)))
        for it in range(50):
            em.on("vector", (lambda v: v.tensor_tensor(
                out=mid[:], in0=lo[:], in1=hi[:], op=ALU.add)))
            em.on("vector", (lambda v: v.tensor_scalar(
                out=mid[:], in0=mid[:], scalar1=0.5, scalar2=None, op0=ALU.mult)))
            em.on("vector", (lambda v: v.tensor_scalar(
                out=msk_p[:, :], in0=sc_pack[:, :], scalar1=mid[:], scalar2=None,
                op0=ALU.is_ge)))
            em.on("vector", (lambda v: v.reduce_sum(
                out=cnt[:], in_=msk_p[:, :], axis=AX.X)))
            em.on("tensor",
                  (lambda t: t.matmul(out=psum_b[0:1, 0:1], lhsT=onesc_s[:, :],
                                      rhs=cnt[:, :], start=True, stop=True)),
                  wait=[(vpc, vpc.n), (spc, spc.n)], inc=tb_)
            em.on("scalar", (lambda s: s.copy(out=c11[:, :], in_=psum_b[0:1, 0:1])),
                  wait=[(tb_, tb_.n)])
            em.on("tensor",
                  (lambda t: t.matmul(out=psum_b[:, 1:2], lhsT=ones_s[:, :],
                                      rhs=c11[:, :], start=True, stop=True)),
                  wait=[(spc, spc.n)], inc=tb_)
            em.on("vector", (lambda v: v.tensor_copy(out=cnt[:], in_=psum_b[:, 1:2])),
                  wait=[(tb_, tb_.n)])
            em.on("vector", (lambda v: v.tensor_scalar(
                out=cnt[:], in0=cnt[:], scalar1=KCAP, scalar2=None, op0=ALU.is_ge)))
            em.on("vector", (lambda v: v.tensor_tensor(
                out=cA[:], in0=mid[:], in1=lo[:], op=ALU.subtract)))
            em.on("vector", (lambda v: v.tensor_tensor(
                out=cA[:], in0=cA[:], in1=cnt[:], op=ALU.mult)))
            em.on("vector", (lambda v: v.tensor_tensor(
                out=lo[:], in0=lo[:], in1=cA[:], op=ALU.add)))
            em.on("vector", (lambda v: v.tensor_tensor(
                out=cA[:], in0=hi[:], in1=mid[:], op=ALU.subtract)))
            em.on("vector", (lambda v: v.tensor_tensor(
                out=cA[:], in0=cA[:], in1=cnt[:], op=ALU.mult)))
            em.on("vector", (lambda v: v.tensor_tensor(
                out=hi[:], in0=mid[:], in1=cA[:], op=ALU.add)))
        em.on("vector", (lambda v: v.tensor_scalar(
            out=msk_p[:, 0:ntile_l * E], in0=glp[:, :], scalar1=lo[:],
            scalar2=None, op0=ALU.is_ge)))
        em.on("vector", (lambda v: v.tensor_tensor(
            out=glp[:, :], in0=glp[:, :], in1=msk_p[:, 0:ntile_l * E],
            op=ALU.mult)))
        em.on("vector", (lambda v: v.tensor_scalar(
            out=glp_h[:, :], in0=glp[:, :], scalar1=0.5, scalar2=None,
            op0=ALU.mult)))
        esB.close()
    else:
        em.on("vector", (lambda v: v.memset(cA[:], 0.0)),
              wait=[(ld_sc, 16), (ld_c, n_const0)])
    gate_v = vpc.n

    state = dict(gy_i=0, gy_last=[0, 0], xb_guard=[0, 0],
                 ev=[[0, 0], [0, 0]], wfree=[0, 0], acc_guard=(0, 0))

    def expert_sets(slot):
        if l0:
            return (upA if slot == 0 else upB), None, (dnA if slot == 0 else dnB)
        return ((upA if slot == 0 else upB),
                (up2A if slot == 0 else up2B),
                (dnA if slot == 0 else dnB))

    def shared_up(ft):
        return (upA, ft) if ft < SPL else (upB, ft - SPL)

    def shared_up2(ft):
        return (up2A, ft) if ft < SPL else (up2B, ft - SPL)

    def shared_dn(ft):
        return (dnA, ft) if ft < SPL else (dnB, ft - SPL)

    def emit_block(tb, ups, up2s, dns, ft_n, wl, bias_of, shared):
        slot = tb % 2
        src = io["x1T_d"].rearrange("(dt p) t -> p dt t", p=128)[
            :, :, tb * TBLK:(tb + 1) * TBLK]
        em.dma("gpsimd",
               xblk[slot].rearrange("p (dt c) -> p dt c", dt=dtile),
               src, ld_xb[slot],
               wait=[(st_t, n_xt)] + ([(tup, state["xb_guard"][slot])]
                                      if state["xb_guard"][slot] else []))
        state.setdefault("xb_n", [0, 0])
        state["xb_n"][slot] += 1
        xb_w = (ld_xb[slot], state["xb_n"][slot] * 16)
        for ft in range(ft_n):
            par = ft % 2
            banks = (psum_h, psum_h2) if par == 0 else (psum_y[0], psum_y[1])
            evs, evv = state["ev"][par]
            w_first = [xb_w, wl]
            if evs:
                w_first.append((spc, evs))
            if evv:
                w_first.append((vpc, evv))
            if shared:
                ups_, lf = shared_up(ft)
            else:
                ups_, lf = ups, ft
            for kt in range(dtile):
                em.on("tensor",
                      (lambda t, kt=kt, slot=slot, banks=banks, ups_=ups_,
                       lf=lf: t.matmul(
                           out=banks[0][:, 0:TBLK],
                           lhsT=ups_[kt][:, lf * 128:(lf + 1) * 128],
                           rhs=xblk[slot][:, kt * TBLK:(kt + 1) * TBLK],
                           start=(kt == 0), stop=(kt == dtile - 1))),
                      wait=(w_first if kt == 0 else ()),
                      inc=(tup if (l0 and kt == dtile - 1) else None))
            if not l0:
                if shared:
                    u2s_, lf2 = shared_up2(ft)
                else:
                    u2s_, lf2 = up2s, ft
                for kt in range(dtile):
                    em.on("tensor",
                          (lambda t, kt=kt, slot=slot, banks=banks, u2s_=u2s_,
                           lf2=lf2: t.matmul(
                               out=banks[1][:, 0:TBLK],
                               lhsT=u2s_[kt][:, lf2 * 128:(lf2 + 1) * 128],
                               rhs=xblk[slot][:, kt * TBLK:(kt + 1) * TBLK],
                               start=(kt == 0), stop=(kt == dtile - 1))),
                          inc=(tup if kt == dtile - 1 else None))
            dst = hT[:, ft * TBLK:(ft + 1) * TBLK]
            if l0:
                bc, bn = bias_of(ft)
                emit_gelu_chain(em, banks[0], dst, TBLK, h_scr, q_scr,
                                bc, (tup, tup.n), ld_c if not shared else ld_bs,
                                bn)
            else:
                emit_glu_chain(em, banks[0], banks[1], dst, TBLK, h_scr,
                               (tup, tup.n))
            state["ev"][par] = [em.pc["scalar"].n, vpc.n]
        state["xb_guard"][slot] = tup.n

    def emit_down(tt, tb, dns, ft_n, shared, v_i=None, e=None, hf=None):
        a_loc = tb * TPB + tt
        for dc in range(NDC):
            n0, n1 = dc * 512, min(D, (dc + 1) * 512)
            ybank = psum_h if dc % 2 == 0 else psum_h2
            for f in range(ft_n):
                if shared:
                    dns_, lf = shared_dn(f)
                else:
                    dns_, lf = dns, f
                em.on("tensor",
                      (lambda t, tt=tt, f=f, n0=n0, n1=n1, ybank=ybank,
                       dns_=dns_, lf=lf: t.matmul(
                           out=ybank[:, 0:n1 - n0],
                           lhsT=hT[:, f * TBLK + tt * 128:
                                   f * TBLK + (tt + 1) * 128],
                           rhs=dns_[lf][:, n0:n1],
                           start=(f == 0), stop=(f == ft_n - 1))),
                      wait=([(spc, spc.n), (vpc, state["act_v"])]
                            if f == 0 else ()),
                      inc=tdn if f == ft_n - 1 else None)
            if shared:
                em.on("scalar",
                      (lambda s, ybank=ybank, n0=n0, n1=n1: s.activation(
                          out=gyst[0][:, n0:n1], in_=ybank[:, 0:n1 - n0],
                          func=AF.Identity, scale=0.5 if l0 else 1.0)),
                      wait=[(tdn, tdn.n), (vpc, vpc.n),
                            (st_acc[0], state["acc_total"][0]),
                            (st_acc[1], state["acc_total"][1])])
            else:
                gslot = state["gy_i"]
                gc = glp_h if l0 else glp
                gc_col = a_loc * E + e
                gw = [(tdn, tdn.n), (vpc, max(gate_v, vpc.n))]
                if state["gy_last"][gslot]:
                    gw.append((st_acc[gslot], state["gy_last"][gslot]))
                em.on("scalar",
                      (lambda s, ybank=ybank, n0=n0, n1=n1, gc_col=gc_col,
                       gc=gc, gslot=gslot: s.activation(
                           out=gyst[gslot][:, n0:n1],
                           in_=ybank[:, 0:n1 - n0],
                           func=AF.Identity,
                           scale=gc[:, gc_col:gc_col + 1])),
                      wait=gw)
                state["gy_last"][gslot] = 0

    NV = E * NHALF
    for v_i in range(NV):
        e = v_i // NHALF
        hf = v_i % NHALF
        wslot = v_i % 2
        ups, up2s, dns = expert_sets(wslot)
        wwait = ([(tup, state["wfree"][wslot]), (tdn, tdn.n)]
                 if state["wfree"][wslot] else [])
        for k in range(dtile):
            em.dma("gpsimd", ups[k][:, 0:Fh],
                   io["W_up_all"][e, k * 128:(k + 1) * 128,
                                  hf * Fh:(hf + 1) * Fh],
                   ld_w[wslot], wait=(wwait if k == 0 else ()))
            if not l0:
                em.dma("gpsimd", up2s[k][:, 0:Fh],
                       io["W_up2_all"][e, k * 128:(k + 1) * 128,
                                       hf * Fh:(hf + 1) * Fh],
                       ld_w[wslot])
        for f in range(fth):
            em.dma("gpsimd", dns[f][:, :],
                   io["W_dn_all"][e, hf * Fh + f * 128: hf * Fh + (f + 1) * 128, :],
                   ld_w[wslot])
        wl = (ld_w[wslot], ld_w[wslot].n)
        if l0 and hf == 0:
            em.dma("sync", b2row_s[:, :], io["b2_all"][e:e + 1, :], ld_b2,
                   wait=[(tb_, tb_.n)])
            emit_bcast_into(em, b2b, b2row_s[0:1, :], D, ones_s,
                            psum_b, tb_, wait=[(ld_c, n_const0),
                                               (ld_b2, ld_b2.n)])
        b2_s = em.pc["scalar"].n

        def bias_of(ft, e=e, hf=hf):
            return (bcols[:, e * (F_e // 128) + hf * fth + ft:
                          e * (F_e // 128) + hf * fth + ft + 1], n_const0)

        for tb in range(NB):
            emit_block(tb, ups, up2s, dns, fth, wl, bias_of, shared=False)
            state["act_v"] = vpc.n
            for tt in range(TPB):
                a_loc = tb * TPB + tt
                gc_col = a_loc * E + e
                state["gy_i"] ^= 1
                gslot = state["gy_i"]
                emit_down(tt, tb, dns, fth, shared=False, v_i=v_i, e=e, hf=hf)
                if l0 and hf == 0:
                    em.on("vector",
                          (lambda v, gc_col=gc_col, gslot=gslot:
                           v.scalar_tensor_tensor(
                               out=gyst[gslot][:, 0:D], in0=b2b[:, 0:D],
                               scalar=glp[:, gc_col:gc_col + 1],
                               in1=gyst[gslot][:, 0:D],
                               op0=ALU.mult, op1=ALU.add)),
                          wait=[(spc, max(spc.n, b2_s))])
                    dma_wait = [(vpc, vpc.n)]
                else:
                    dma_wait = [(spc, spc.n)]
                if v_i > 0:
                    dma_wait += [(st_acc[0], state["acc_guard"][0]),
                                 (st_acc[1], state["acc_guard"][1])]
                em.dma("gpsimd",
                       io["gy_acc"][a_loc * 128:(a_loc + 1) * 128, :],
                       gyst[gslot][:, 0:D], st_acc[gslot],
                       wait=dma_wait,
                       **({} if v_i == 0 else {"accum_op": ALU.add}))
                state["gy_last"][gslot] = st_acc[gslot].n
            state["ev"][0] = [em.pc["scalar"].n, vpc.n]
            state["ev"][1] = [em.pc["scalar"].n, vpc.n]
        state["wfree"][wslot] = tup.n
        state["acc_guard"] = (st_acc[0].n, st_acc[1].n)
        state["ev"][0] = [em.pc["scalar"].n, vpc.n]
        state["ev"][1] = [em.pc["scalar"].n, vpc.n]
    esA.close()

    wgate = [(tup, tup.n), (tdn, tdn.n)]
    for k in range(dtile):
        em.dma("gpsimd", upA[k][:, 0:SPL * 128],
               io["Ws_up"][k * 128:(k + 1) * 128, 0:SPL * 128],
               ld_w[0], wait=(wgate if k == 0 else ()))
        em.dma("gpsimd", upB[k][:, 0:(fts - SPL) * 128],
               io["Ws_up"][k * 128:(k + 1) * 128, SPL * 128:F_s], ld_w[0])
        if not l0:
            em.dma("gpsimd", up2A[k][:, 0:SPL * 128],
                   io["Ws_up2"][k * 128:(k + 1) * 128, 0:SPL * 128], ld_w[0])
            em.dma("gpsimd", up2B[k][:, 0:(fts - SPL) * 128],
                   io["Ws_up2"][k * 128:(k + 1) * 128, SPL * 128:F_s], ld_w[0])
    for f in range(fts):
        dns_, lf = shared_dn(f)
        em.dma("gpsimd", dns_[lf][:, :],
               io["Ws_dn"][f * 128:(f + 1) * 128, :], ld_w[0])
    if l0:
        em.dma("sync", bcols[:, 0:fts], io["bs1_cols"][:, :], ld_bs,
               wait=[(spc, spc.n)])
        bsrow = sb(em, f"{pfx}_bsrow", [1, D], F32)
        em.dma("sync", bsrow[:, :], io["bs2_row"][:, :], ld_bs)
        emit_bcast_into(em, b2b, bsrow[0:1, :], D, ones_s, psum_b, tb_,
                        wait=[(ld_bs, ld_bs.n)])
    wl_sh = (ld_w[0], ld_w[0].n)
    n_bs = ld_bs.n
    bs_done = em.pc["scalar"].n

    xres = [sb(em, f"{pfx}_xres{j}", [128, D], F32) for j in range(2)]
    gyres = [sb(em, f"{pfx}_gyres{j}", [128, D], F32) for j in range(2)]
    lnscr = sb(em, f"{pfx}_lnscr", [128, D], F32)
    xout_t = sb(em, f"{pfx}_xout", [128, D], F32)

    def bias_sh(ft):
        return (bcols[:, ft:ft + 1], n_bs)

    n_acc = (st_acc[0].n, st_acc[1].n)
    state["acc_total"] = n_acc
    for tb in range(NB):
        emit_block(tb, None, None, None, fts, wl_sh, bias_sh, shared=True)
        state["act_v"] = vpc.n
        for tt in range(TPB):
            i_loc = tb * TPB + tt
            rslot = i_loc % 2
            emit_down(tt, tb, None, fts, shared=True)
            em.dma("sync", xres[rslot][:, :],
                   io["x1_in"][i_loc * 128:(i_loc + 1) * 128, :],
                   ld_res[rslot], wait=[(st_t, n_xt), (vpc, vpc.n)])
            em.dma("gpsimd", gyres[rslot][:, :],
                   io["gy_acc"][i_loc * 128:(i_loc + 1) * 128, :],
                   ld_gy[rslot],
                   wait=[(st_acc[0], n_acc[0]), (st_acc[1], n_acc[1]),
                         (vpc, vpc.n)])
            if l0:
                em.on("vector", (lambda v: v.tensor_tensor(
                    out=gyst[0][:, 0:D], in0=gyst[0][:, 0:D], in1=b2b[:, 0:D],
                    op=ALU.add)), wait=[(spc, max(spc.n, bs_done))])
            em.on("vector",
                  (lambda v, rslot=rslot: v.tensor_tensor(
                      out=gyst[0][:, 0:D], in0=gyst[0][:, 0:D],
                      in1=gyres[rslot][:, :], op=ALU.add)),
                  wait=[(spc, spc.n), (ld_gy[rslot], ld_gy[rslot].n)])
            em.on("vector",
                  (lambda v, rslot=rslot: v.tensor_tensor(
                      out=gyst[0][:, 0:D], in0=gyst[0][:, 0:D],
                      in1=xres[rslot][:, :], op=ALU.add)),
                  wait=[(ld_res[rslot], ld_res[rslot].n)])
            emit_ln_tile(em, xout_t[:, :], gyst[0][:, 0:D], D, cA, cB, lnscr)
            em.dma("sync",
                   io["x2_out"][i_loc * 128:(i_loc + 1) * 128, :],
                   xout_t[:, :], st_o, wait=[(spc, spc.n)])
            em.on("scalar", (lambda s: s.nop()), wait=[(st_o, st_o.n)])
        state["ev"][0] = [em.pc["scalar"].n, vpc.n]
        state["ev"][1] = [em.pc["scalar"].n, vpc.n]
    return st_o


def build_attn_program(g, mode):
    nc = bass.Bass("TRN2", target_bir_lowering=False, debug=False)
    B_loc, T, D = g["B_loc"], g["T"], g["D"]
    io = {}
    io["x_in"] = nc.dram_tensor("x_in", [B_loc * T, D], F32, kind="ExternalInput").ap()
    io["W_attn"] = nc.dram_tensor("W_attn", [D, 3 * D], F32, kind="ExternalInput").ap()
    io["b_attn_cols"] = nc.dram_tensor("b_attn_cols", [128, 3 * D // 128], F32,
                                       kind="ExternalInput").ap()
    io["W_proj"] = nc.dram_tensor("W_proj", [D, D], F32, kind="ExternalInput").ap()
    io["b_proj_row"] = nc.dram_tensor("b_proj_row", [1, D], F32, kind="ExternalInput").ap()
    io["idn"] = nc.dram_tensor("idn", [128, 128], BF16, kind="ExternalInput").ap()
    io["ones_f32"] = nc.dram_tensor("ones_f32", [1, 128], F32, kind="ExternalInput").ap()
    io["idn32"] = nc.dram_tensor("idn32", [128, 128], F32, kind="ExternalInput").ap()
    io["gwT"] = nc.dram_tensor("gwT", [D, 8], F32, kind="ExternalInput").ap()
    if mode == "l0":
        io["gain11"] = nc.dram_tensor("gain11", [1, 1], F32, kind="ExternalInput").ap()
        io["tri"] = nc.dram_tensor("tri", [128, 128], F32, kind="ExternalInput").ap()
        io["scores_out"] = nc.dram_tensor("scores_out", [B_loc * T, 8], F32,
                                          kind="ExternalOutput").ap()
    else:
        for nm in ("cosq", "sinq"):
            io[nm] = nc.dram_tensor(nm, [128, T], F32, kind="ExternalInput").ap()
        io["wdense_out"] = nc.dram_tensor("wdense_out", [B_loc * T, 8], F32,
                                          kind="ExternalOutput").ap()
    io["x1_out"] = nc.dram_tensor("x1_out", [B_loc * T, D], F32, kind="ExternalOutput").ap()
    es = ExitStack()
    with es:
        em = Em(nc, es)
        block = es.enter_context(nc.Block())
        st = emit_attn(em, g, io, mode)
        em.on("gpsimd", (lambda e: e.nop()), wait=[(st, st.n)])
        em.finalize(block)
    return nc


def attn_consts(D):
    tri = np.tril(np.ones((128, 128), np.float32))
    idn = np_bf16(np.eye(128, dtype=np.float32))
    idn32 = np.eye(128, dtype=np.float32)
    ones_f32 = np.ones((1, 128), np.float32)
    return dict(tri=tri, idn=idn, idn32=idn32, ones_f32=ones_f32)


def rope_consts(T, half=32, theta=500.0, kscale=0.125):
    freqs = 1.0 / theta ** (np.arange(half, dtype=np.float64) / half)
    ang = np.arange(T, dtype=np.float64)[:, None] * freqs[None, :]
    cos = np.cos(ang).astype(np.float32)
    sin = np.sin(ang).astype(np.float32)
    cosT = np.empty((128, T), np.float32)
    sinT = np.empty((128, T), np.float32)
    for r in range(128):
        f = r % half
        cosT[r] = cos[:, f]
        sgn = -1.0 if (r % 64) < half else 1.0
        sinT[r] = sgn * sin[:, f]
    return dict(cosq=cosT, sinq=sinT,
                cosk=np.float32(kscale) * cosT, sink=np.float32(kscale) * sinT)


def build_ffn_program(g, mode):
    nc = bass.Bass("TRN2", target_bir_lowering=False, debug=False)
    Tl, E, D = g["Tl"], g["E"], g["D"]
    F_e, F_s, S_tot = g["F_e"], g["F_s"], g["S_tot"]
    l0 = mode == "l0"
    ntile_l = Tl // 128
    io = {}
    def ein(name, shape):
        io[name] = nc.dram_tensor(name, shape, F32, kind="ExternalInput").ap()
    ein("x1_in", [Tl, D])
    ein("g_loc_packed", [128, ntile_l * E])
    if l0:
        ein("sc_packed", [128, S_tot * E // 128])
        ein("b1_all_cols", [128, E * F_e // 128]); ein("b2_all", [E, D])
        ein("bs1_cols", [128, F_s // 128]); ein("bs2_row", [1, D])
    ein("W_up_all", [E, D, F_e]); ein("W_dn_all", [E, F_e, D])
    ein("Ws_up", [D, F_s]); ein("Ws_dn", [F_s, D])
    if not l0:
        ein("W_up2_all", [E, D, F_e]); ein("Ws_up2", [D, F_s])
    ein("ones_f32", [1, 128]); ein("ones_col", [128, 1])
    io["idn"] = nc.dram_tensor("idn", [128, 128], BF16, kind="ExternalInput").ap()
    io["x2_out"] = nc.dram_tensor("x2_out", [Tl, D], F32, kind="ExternalOutput").ap()
    io["x1T_d"] = nc.dram_tensor("x1T_d", [D, Tl], BF16).ap()
    io["gy_acc"] = nc.dram_tensor("gy_acc", [Tl, D], F32).ap()
    es = ExitStack()
    with es:
        em = Em(nc, es)
        block = es.enter_context(nc.Block())
        st = emit_moe_ffn(em, g, io, mode)
        em.on("gpsimd", (lambda e: e.nop()), wait=[(st, st.n)])
        em.finalize(block)
    return nc



_B, _T, _D, _H, _E = 16, 2048, 1024, 16, 8


def _cols(b):
    return np.ascontiguousarray(np.asarray(b).reshape(-1, 128).T)


def _pack_gate(s, E):
    a = s.shape[0] // 128
    return np.ascontiguousarray(
        np.asarray(s).reshape(a, 128, E).transpose(1, 0, 2).reshape(128, a * E))


def kernel(**inputs):
    ii = {k: np.asarray(v) for k, v in inputs.items()}
    B, T, D, H, E = _B, _T, _D, _H, _E
    R = N_CORES
    B_loc = B // R
    Tl = B_loc * T
    S_tot = B * T
    cores = list(range(R))
    cst = attn_consts(D)
    ones_col = np.ones((128, 1), np.float32)
    g_at = dict(B_loc=B_loc, T=T, D=D, H=H)

    x_sh = [np.ascontiguousarray(ii["L0_attn_x"][2 * c:2 * c + 2].reshape(Tl, D))
            for c in range(R)]

    nc1 = build_attn_program(g_at, "l0")
    common1 = dict(
        W_attn=ii["L0_attn_W_attn"],
        b_attn_cols=_cols(ii["L0_attn_b_attn"]),
        W_proj=ii["L0_attn_W_proj"],
        b_proj_row=ii["L0_attn_b_proj"].reshape(1, D),
        gain11=ii["L0_attn_score_gain"].reshape(1, 1),
        gwT=np.ascontiguousarray(ii["L0_ffn_gate_weight"].T),
        tri=cst["tri"], idn=cst["idn"], idn32=cst["idn32"],
        ones_f32=cst["ones_f32"],
    )
    res1 = run_bass_kernel_spmd(
        nc1, [dict(common1, x_in=x_sh[c]) for c in range(R)], core_ids=cores)
    x1_sh = [res1.results[c]["x1_out"] for c in range(R)]
    sc_all = np.concatenate([res1.results[c]["scores_out"] for c in range(R)], 0)
    sc_packed = _pack_gate(sc_all, E)

    g_f0 = dict(Tl=Tl, E=E, D=D, F_e=4096, F_s=4096, TBLK=256, CAP=2,
                S_tot=S_tot)
    nc2 = build_ffn_program(g_f0, "l0")
    maps2 = []
    for c in range(R):
        maps2.append(dict(
            x1_in=x1_sh[c],
            g_loc_packed=_pack_gate(sc_all[c * Tl:(c + 1) * Tl], E),
            sc_packed=sc_packed,
            W_up_all=ii["L0_ffn_W1"], W_dn_all=ii["L0_ffn_W2"],
            b1_all_cols=_cols(ii["L0_ffn_b1"]),
            b2_all=ii["L0_ffn_b2"],
            Ws_up=ii["L0_ffn_Ws1"], Ws_dn=ii["L0_ffn_Ws2"],
            bs1_cols=_cols(ii["L0_ffn_bs1"]),
            bs2_row=ii["L0_ffn_bs2"].reshape(1, D),
            ones_f32=cst["ones_f32"], ones_col=ones_col, idn=cst["idn"]))
    res2 = run_bass_kernel_spmd(nc2, maps2, core_ids=cores)
    x2_sh = [res2.results[c]["x2_out"] for c in range(R)]

    nc3 = build_attn_program(g_at, "l1")
    rc = rope_consts(T, 32, 500.0, 1.0)
    common3 = dict(
        W_attn=ii["L1_attn_w_qkv"],
        b_attn_cols=_cols(ii["L1_attn_b_qkv"]),
        W_proj=ii["L1_attn_w_out"],
        b_proj_row=ii["L1_attn_b_out"].reshape(1, D),
        gwT=np.ascontiguousarray(ii["L1_ffn_gate_w"].T),
        cosq=rc["cosq"], sinq=rc["sinq"],
        idn=cst["idn"], idn32=cst["idn32"], ones_f32=cst["ones_f32"],
    )
    res3 = run_bass_kernel_spmd(
        nc3, [dict(common3, x_in=x2_sh[c]) for c in range(R)], core_ids=cores)
    x3_sh = [res3.results[c]["x1_out"] for c in range(R)]
    wd_sh = [res3.results[c]["wdense_out"] for c in range(R)]

    g_f1 = dict(Tl=Tl, E=E, D=D, F_e=1408, F_s=2816, TBLK=256, CAP=0,
                S_tot=S_tot)
    nc4 = build_ffn_program(g_f1, "l1")
    maps4 = []
    for c in range(R):
        maps4.append(dict(
            x1_in=x3_sh[c],
            g_loc_packed=_pack_gate(wd_sh[c], E),
            W_up_all=ii["L1_ffn_wg"], W_up2_all=ii["L1_ffn_wu"],
            W_dn_all=ii["L1_ffn_wd"],
            Ws_up=ii["L1_ffn_swg"], Ws_up2=ii["L1_ffn_swu"],
            Ws_dn=ii["L1_ffn_swd"],
            ones_f32=cst["ones_f32"], ones_col=ones_col, idn=cst["idn"]))
    res4 = run_bass_kernel_spmd(nc4, maps4, core_ids=cores)
    out = np.concatenate([res4.results[c]["x2_out"] for c in range(R)], 0)
    return out.reshape(B, T, D).astype(np.float32)


# version 04 (final)
# speedup vs baseline: 1.2930x; 1.2930x over previous
import math
from contextlib import ExitStack

import numpy as np
import concourse.bass as bass
import concourse.mybir as mybir
from concourse.bass_utils import run_bass_kernel_spmd

F32 = mybir.dt.float32
BF16 = mybir.dt.bfloat16
AF = mybir.ActivationFunctionType
ALU = mybir.AluOpType
AX = mybir.AxisListType

N_CORES = 8
GELU_C = math.sqrt(2.0 / math.pi)
GELU_A = 0.044715


class Ctr:
    __slots__ = ("sem", "n", "name")

    def __init__(self, sem, name):
        self.sem = sem
        self.n = 0
        self.name = name

    @property
    def now(self):
        return (self, self.n)


class Em:

    AUTO = ("vector", "scalar")

    def __init__(self, nc, es):
        self.nc = nc
        self.es = es
        self.streams = {k: [] for k in ("sync", "scalar", "vector", "tensor", "gpsimd")}
        self._names = {}
        self.pc = {}
        for e in ("vector", "scalar"):
            self.pc[e] = self.ctr(f"{e}_pc")

    def ctr(self, name):
        if name in self._names:
            self._names[name] += 1
            name = f"{name}_{self._names[name]}"
        else:
            self._names[name] = 0
        return Ctr(self.es.enter_context(self.nc.semaphore(name)), name)

    def on(self, eng, fn, wait=(), inc=None, k=1):
        waits = []
        if eng in self.AUTO:
            p = self.pc[eng]
            if p.n > 0:
                waits.append((p, p.n))
            if inc is None:
                inc = p
        for c, v in wait:
            if v > 0:
                waits.append((c, v))
        if inc is not None:
            inc.n += k
        self.streams[eng].append((fn, tuple(waits), inc, k))
        return inc.n if inc is not None else None

    def dma(self, eng, out, in_, inc, wait=(), **kw):
        return self.on(
            eng,
            (lambda e, out=out, in_=in_, kw=kw: e.dma_start(out=out, in_=in_, **kw)),
            wait=wait, inc=inc, k=16,
        )

    def finalize(self, block):
        for eng_name, items in self.streams.items():
            if not items:
                continue

            def make(items):
                def emitfn(eng):
                    for fn, waits, inc, k in items:
                        for c, v in waits:
                            eng.wait_ge(c.sem, v)
                        inst = fn(eng)
                        if inc is not None:
                            inst.then_inc(inc.sem, k)
                return emitfn

            getattr(block, eng_name)(make(items))


def sb(em, name, shape, dtype):
    return em.es.enter_context(em.nc.sbuf_tensor(name, shape, dtype))


def ps(em, name, shape, dtype):
    return em.es.enter_context(em.nc.psum_tensor(name, shape, dtype))



def bcast_row(em, name, src_row_ap, ncols, ones_row, psum_pool, dtype=F32,
              wait=(), scale=None):
    dst = sb(em, name, [128, ncols], dtype)
    tb = em.ctr(f"{name}_t")
    nch = (ncols + 511) // 512
    spc = em.pc["scalar"]
    for c in range(nch):
        c0, c1 = c * 512, min(ncols, (c + 1) * 512)
        em.on("tensor",
              (lambda t, c0=c0, c1=c1: t.matmul(
                  out=psum_pool[:, 0:c1 - c0], lhsT=ones_row[:, 0:128],
                  rhs=src_row_ap[:, c0:c1], start=True, stop=True)),
              wait=(list(wait) + [(spc, spc.n)]), inc=tb)
        em.on("scalar", (lambda s, c0=c0, c1=c1: s.copy(
            out=dst[:, c0:c1], in_=psum_pool[:, 0:c1 - c0])),
            wait=[(tb, tb.n)])
    return dst, em.pc["scalar"].n


def emit_ln_tile(em, out_ap, in_ap, D, scr_col, scr_col2, scr_big, eps=1e-5,
                 out2_ap=None, out_wait=()):
    v, s = em.pc["vector"], em.pc["scalar"]
    em.on("vector", lambda e: e.reduce_sum(out=scr_col[:], in_=in_ap, axis=AX.X))
    em.on("vector", lambda e: e.tensor_scalar(
        out=scr_col[:], in0=scr_col[:], scalar1=-1.0 / D, scalar2=None, op0=ALU.mult))
    em.on("scalar", lambda e: e.activation(
        out=scr_big[:], in_=in_ap, func=AF.Square, bias=scr_col[:], scale=1.0,
        accum_out=scr_col2[:]), wait=[v.now])
    em.on("vector", lambda e: e.tensor_scalar(
        out=scr_col2[:], in0=scr_col2[:], scalar1=1.0 / D, scalar2=eps,
        op0=ALU.mult, op1=ALU.add), wait=[s.now])
    em.on("scalar", lambda e: e.activation(
        out=scr_col2[:], in_=scr_col2[:], func=AF.Ln), wait=[v.now])
    em.on("scalar", lambda e: e.activation(
        out=scr_col2[:], in_=scr_col2[:], func=AF.Exp, scale=-0.5))
    em.on("vector", lambda e: e.tensor_tensor(
        out=scr_col[:], in0=scr_col[:], in1=scr_col2[:], op=ALU.mult), wait=[s.now])
    em.on("scalar", lambda e: e.activation(
        out=out_ap, in_=in_ap, func=AF.Identity, scale=scr_col2[:], bias=scr_col[:]),
        wait=[v.now] + list(out_wait))
    if out2_ap is not None:
        em.on("scalar", lambda e: e.activation(
            out=out2_ap, in_=in_ap, func=AF.Identity, scale=scr_col2[:], bias=scr_col[:]))
    return em.pc["scalar"].n



def emit_attn(em, g, io, mode):
    B_loc, T, D, H = g["B_loc"], g["T"], g["D"], g["H"]
    pfx = "a0" if mode == "l0" else "a1"
    dtile = D // 128
    ntile = T // 128
    NCH = (T + 511) // 512
    HP = H // 2
    vpc, spc = em.pc["vector"], em.pc["scalar"]
    l0 = mode == "l0"

    xT = [sb(em, f"{pfx}_xT{k}", [128, T], BF16) for k in range(dtile)]
    xload = [sb(em, f"{pfx}_xld{j}", [128, D], BF16) for j in range(2)]
    wslab = [sb(em, f"{pfx}_w{j}", [128, dtile * 3 * 128], BF16) for j in range(2)]
    bcols = sb(em, f"{pfx}_bc", [128, 3 * dtile], F32)
    qkvT = [sb(em, f"{pfx}_qkv{j}", [128, T], BF16) for j in range(3)]
    v_tm2 = [sb(em, f"{pfx}_vtm{j}", [128, ntile * 65], BF16) for j in range(2)]
    pT_buf = sb(em, f"{pfx}_pT", [128, 4 * 512], BF16)
    ub = [sb(em, f"{pfx}_ub{j}", [128, 512], F32) for j in range(2)]
    bias_sb = [sb(em, f"{pfx}_bsb{j}", [128, 512], BF16) for j in range(2)]
    rb_sb = [sb(em, f"{pfx}_rbsb{j}", [128, 512], BF16) for j in range(2)]
    rrow_s = sb(em, f"{pfx}_rrow", [128, 512], F32)
    onesP = sb(em, f"{pfx}_onesP", [128, 64], F32)
    ytb = [sb(em, f"{pfx}_yt{k}", [128, T], BF16) for k in range(dtile)]
    wproj = [sb(em, f"{pfx}_wp{k}", [128, D], BF16) for k in range(dtile)]
    xres = [sb(em, f"{pfx}_xr{j}", [128, D], F32) for j in range(2)]
    t1 = sb(em, f"{pfx}_t1", [128, D], F32)
    scr_ln = sb(em, f"{pfx}_lnscr", [128, D], F32)
    c1 = sb(em, f"{pfx}_c1", [128, 1], F32)
    c2 = sb(em, f"{pfx}_c2", [128, 1], F32)
    x1t = [sb(em, f"{pfx}_x1t{j}", [128, D], F32) for j in range(2)]
    ones_b = sb(em, f"{pfx}_onesb", [1, 128], BF16)
    bprow_b = sb(em, f"{pfx}_bprowb", [1, D], BF16)
    x3Tt = sb(em, f"{pfx}_x3Tt", [128, D], F32)
    gwT_s = sb(em, f"{pfx}_gwT", [128, 8 * dtile], F32)
    sc8 = sb(em, f"{pfx}_sc8", [128, 8], F32)
    g1 = sb(em, f"{pfx}_g1", [128, 1], F32)
    gd = sb(em, f"{pfx}_gd", [128, 1], F32)
    idn32_s = sb(em, f"{pfx}_idn32", [128, 128], F32)
    nTb = sb(em, f"{pfx}_nTb", [16, 128], BF16)
    brow_b = sb(em, f"{pfx}_brow", [1, T], BF16)
    if l0:
        kbar = sb(em, f"{pfx}_kbar", [128, 1], F32)
        kbar_b = sb(em, f"{pfx}_kbarb", [128, 1], BF16)
        negm2 = [sb(em, f"{pfx}_negm{j}", [128, ntile], F32) for j in range(2)]
        eps2 = [sb(em, f"{pfx}_eps{j}", [128, ntile], F32) for j in range(2)]
        mads = sb(em, f"{pfx}_mads", [128, ntile], F32)
        madc4 = sb(em, f"{pfx}_madc4", [128, 4 * ntile], F32)
        nTe = sb(em, f"{pfx}_nTe", [16, 128], F32)
        eps65 = sb(em, f"{pfx}_eps65", [65, T], F32)
        sgn = sb(em, f"{pfx}_sgn", [128, 1], F32)
        rga = sb(em, f"{pfx}_rga", [128, 1], F32)
        trimT_s = sb(em, f"{pfx}_trimT", [128, 4 * 512], BF16)
    else:
        qf = sb(em, f"{pfx}_qf", [128, T], F32)
        kf = sb(em, f"{pfx}_kf", [128, T], F32)
        swp = sb(em, f"{pfx}_swp", [128, T], F32)
        ncol4 = sb(em, f"{pfx}_ncol4", [128, 4 * ntile], F32)
        nneg2 = [sb(em, f"{pfx}_nneg{j}", [128, ntile], F32) for j in range(2)]
        trig = {}
        for nm in ("cosq", "sinq"):
            trig[nm] = sb(em, f"{pfx}_{nm}", [128, T], F32)
        bck = sb(em, f"{pfx}_bck", [128, dtile], F32)
        msk = sb(em, f"{pfx}_msk", [128, 8], F32)
        w8 = sb(em, f"{pfx}_w8", [128, 8], F32)
        g2 = sb(em, f"{pfx}_g2", [128, 1], F32)

    xoff = max(512, D)
    psum_s = ps(em, f"{pfx}_ps_s", [128, max(2048, T, xoff + 512 + 128)], F32)
    psum_av = [ps(em, f"{pfx}_ps_av{j}", [128, 512], F32) for j in range(2)]
    psum_rT = [ps(em, f"{pfx}_ps_rT{j}", [128, 512], BF16) for j in range(2)]
    NEGC = 1536
    CVC = 1568
    CVC2 = 1696
    BIA = 1536
    BCA = 1024

    ld_w = [em.ctr(f"{pfx}_ldw0"), em.ctr(f"{pfx}_ldw1")]
    ld_wp = em.ctr(f"{pfx}_ldwp")
    ld_x = [em.ctr(f"{pfx}_ldx0"), em.ctr(f"{pfx}_ldx1")]
    ld_r = [em.ctr(f"{pfx}_ldr0"), em.ctr(f"{pfx}_ldr1")]
    ld_c = em.ctr(f"{pfx}_ldc")
    ld_sw = em.ctr(f"{pfx}_ldsw")
    ld_cv = em.ctr(f"{pfx}_ldcv")
    st_o = em.ctr(f"{pfx}_sto")
    tqkv = em.ctr(f"{pfx}_tqkv")
    tx = em.ctr(f"{pfx}_tx")
    tv = em.ctr(f"{pfx}_tv")
    tmm = em.ctr(f"{pfx}_tmm")
    tsm = em.ctr(f"{pfx}_tsm")
    tbias = em.ctr(f"{pfx}_tbias")
    tav = em.ctr(f"{pfx}_tav")
    tbc = em.ctr(f"{pfx}_tbc")
    ttr = em.ctr(f"{pfx}_ttr")
    tm = em.ctr(f"{pfx}_tm")
    tp = em.ctr(f"{pfx}_tp")
    tgt = em.ctr(f"{pfx}_tgt")

    em.dma("sync", bcols[:, 0:3 * dtile], io["b_attn_cols"][:, :], ld_c)
    for k in range(dtile):
        em.dma("gpsimd", wproj[k][:, :], io["W_proj"][k * 128:(k + 1) * 128, :], ld_wp)
    ones_s = sb(em, f"{pfx}_ones", [1, 128], F32)
    bprow = sb(em, f"{pfx}_bprow", [1, D], F32)
    em.dma("sync", ones_s[:, :], io["ones_f32"][:, :], ld_c)
    em.dma("sync", bprow[:, :], io["b_proj_row"][:, :], ld_c)
    idn_s = sb(em, f"{pfx}_idn", [128, 128], BF16)
    em.dma("sync", idn_s[:, :], io["idn"][:, :], ld_c)
    if l0:
        gain_s = sb(em, f"{pfx}_gain11", [1, 1], F32)
        em.dma("sync", gain_s[:, :], io["gain11"][:, :], ld_c)
        em.dma("sync", trimT_s[:, :], io["trimT"][:, :], ld_c)
    else:
        for nm in ("cosq", "sinq"):
            em.dma("sync", trig[nm][:, :], io[nm][:, :], ld_c)
    em.dma("sync", idn32_s[:, :], io["idn32"][:, :], ld_c)
    em.dma("sync",
           gwT_s.rearrange("p (k e) -> p k e", e=8),
           io["gwT"].rearrange("(k p) e -> p k e", p=128), ld_c)
    n_const = ld_c.n
    bck_v = 0
    if not l0:
        em.on("vector", (lambda v: v.tensor_scalar(
            out=bck[:, :], in0=bcols[:, dtile:2 * dtile], scalar1=0.125,
            scalar2=None, op0=ALU.mult)), wait=[(ld_c, n_const)])
        bck_v = em.pc["vector"].n
    em.on("vector", (lambda v: v.tensor_copy(out=ones_b[:, :], in_=ones_s[:, :])),
          wait=[(ld_c, n_const)])
    em.on("vector", (lambda v: v.tensor_copy(out=bprow_b[:, :], in_=bprow[:, :])))
    bias_b_v = em.pc["vector"].n
    em.on("tensor",
          (lambda t: t.matmul(out=psum_av[0][:, 0:80],
                              lhsT=ones_s[0:1, 0:128], rhs=ones_s[0:1, 0:80],
                              start=True, stop=True)),
          wait=[(ld_c, n_const)], inc=tm)
    em.on("scalar", (lambda s: s.copy(out=onesP[:, :],
                                      in_=psum_av[0][:, 16:80])),
          wait=[(tm, tm.n)])
    for j in range(2):
        em.on("scalar",
              (lambda s, j=j: s.copy(
                  out=v_tm2[j].rearrange("p (k c) -> p k c", c=65)[:, :, 64:65],
                  in_=psum_av[0][:, 0:ntile])))
    ones_pre_s = em.pc["scalar"].n
    if l0:
        gain_col, gain_rdy = bcast_row(em, f"{pfx}_gaincol", gain_s, 1, ones_s,
                                       psum_av[1], F32, wait=[(ld_c, n_const),
                                                              (spc, ones_pre_s)])
        em.on("vector", (lambda v: v.tensor_scalar(
            out=sgn[:, :], in0=gain_col[:, 0:1], scalar1=0.0, scalar2=2.0,
            op0=ALU.is_ge, op1=ALU.mult)), wait=[(spc, gain_rdy)])
        em.on("vector", (lambda v: v.tensor_scalar(
            out=sgn[:, :], in0=sgn[:, :], scalar1=-1.0, scalar2=None,
            op0=ALU.add)))
        em.on("vector", (lambda v: v.tensor_tensor(
            out=rga[:, :], in0=gain_col[:, 0:1], in1=sgn[:, :], op=ALU.mult)))
        em.on("vector", (lambda v: v.reciprocal(out=rga[:, :], in_=rga[:, :])))
        sgn_v = em.pc["vector"].n

    xslot_last = [0, 0]
    wslot_last = [0, 0]
    xslot_uses = [0, 0]
    wslot_uses = [0, 0]
    rope_v_last = 0
    qkv_evac_s = 0
    epi_guard = [(0, 0), (0, 0)]
    epi_done = [(0, 0), (0, 0)]
    ln_done = [0, 0]
    t1_free = 0
    epi_s = 0
    epi_v = 0
    pb_v = 0
    pb_s = 0
    ytb_v = 0
    for b in range(B_loc):
        tok0 = b * T
        for i in range(ntile):
            slot = i % 2
            em.dma("gpsimd", xload[slot][:, :],
                   io["x_in"][tok0 + i * 128: tok0 + (i + 1) * 128, :], ld_x[slot],
                   wait=[(tx, xslot_last[slot])] if xslot_last[slot] else ())
            xslot_uses[slot] += 1
            for k in range(dtile):
                pslot = k % 2
                w0 = [(spc, spc.n)]
                if k == 0:
                    w0 += [(ld_x[slot], xslot_uses[slot] * 16), (ld_c, n_const)]
                if b > 0 and i == 0 and k == 0:
                    w0.append((tqkv, b * HP * 3))
                em.on("tensor",
                      (lambda t, slot=slot, k=k, pslot=pslot: t.transpose(
                          out=psum_rT[pslot][:, 0:128],
                          in_=xload[slot][:, k * 128:(k + 1) * 128],
                          identity=idn_s[:, :])),
                      wait=w0, inc=tx)
                em.on("scalar",
                      (lambda s, k=k, i=i, pslot=pslot: s.copy(
                          out=xT[k][:, i * 128:(i + 1) * 128],
                          in_=psum_rT[pslot][:, 0:128])),
                      wait=[(tx, tx.n)])
            xslot_last[slot] = tx.n
        xT_ready_s = em.pc["scalar"].n

        for hp in range(HP):
            wslot = hp % 2
            wa4 = io["W_attn"].rearrange("(kt p) (g c) -> p kt g c", p=128, g=3)[
                :, :, :, hp * 128:(hp + 1) * 128]
            for gq in range(3):
                em.dma("gpsimd",
                       wslab[wslot][:, gq * dtile * 128:(gq + 1) * dtile * 128]
                       .rearrange("p (kt c) -> p kt c", kt=dtile),
                       wa4[:, :, gq, :],
                       ld_w[wslot],
                       wait=([(tqkv, wslot_last[wslot])]
                             if (wslot_last[wslot] and gq == 0) else ()))
            wslot_uses[wslot] += 1
            wready = (ld_w[wslot], ld_w[wslot].n)
            for j in range(3):
                for kt in range(dtile):
                    for n in range(NCH):
                        n0, n1 = n * 512, min(T, (n + 1) * 512)
                        w = []
                        if kt == 0 and n == 0:
                            sv = max(xT_ready_s, qkv_evac_s, epi_s, ones_pre_s)
                            w = [wready, (spc, sv if sv else spc.n)]
                            vv = max(epi_v, pb_v)
                            if vv:
                                w.append((vpc, vv))
                        em.on("tensor",
                              (lambda t, j=j, kt=kt, n0=n0, n1=n1, wslot=wslot:
                               t.matmul(
                                   out=psum_s[:, n0:n1],
                                   lhsT=wslab[wslot][:, (j * dtile + kt) * 128:
                                                     (j * dtile + kt + 1) * 128],
                                   rhs=xT[kt][:, n0:n1],
                                   start=(kt == 0), stop=(kt == dtile - 1),
                                   skip_group_check=True)),
                              wait=w,
                              inc=tqkv if (kt == dtile - 1 and n == NCH - 1)
                              else None)
                dst = qkvT[j] if (l0 or j == 2) else (qf if j == 0 else kf)
                if (not l0) and j == 1:
                    bias_ap = bck[:, hp:hp + 1]
                    sc_ = 0.125
                else:
                    bias_ap = bcols[:, j * dtile + hp:j * dtile + hp + 1]
                    sc_ = 1.0
                for n in range(NCH):
                    n0, n1 = n * 512, min(T, (n + 1) * 512)
                    ewait = [(tqkv, tqkv.n), (ld_c, n_const)]
                    if (not l0) and j == 1 and n == 0:
                        ewait.append((vpc, bck_v))
                    if (not l0) and j < 2 and n == 0 and rope_v_last:
                        ewait += [(vpc, rope_v_last), (ld_sw, ld_sw.n)]
                    em.on("scalar",
                          (lambda s, n0=n0, n1=n1, dst=dst, bias_ap=bias_ap,
                           sc_=sc_: s.activation(
                               out=dst[:, n0:n1], in_=psum_s[:, n0:n1],
                               func=AF.Identity, bias=bias_ap, scale=sc_)),
                          wait=ewait)
                qkv_evac_s = em.pc["scalar"].n
            wslot_last[wslot] = tqkv.n
            qkv_ready_s = em.pc["scalar"].n

            if not l0:
                for (srcb, dstj) in ((qf, 0), (kf, 1)):
                    for (d0, s0) in ((0, 32), (32, 0), (64, 96), (96, 64)):
                        em.dma("sync", swp[d0:d0 + 32, :], srcb[s0:s0 + 32, :],
                               ld_sw,
                               wait=([(spc, qkv_ready_s), (vpc, vpc.n)]
                                     if d0 == 0 else ()))
                    em.on("vector",
                          (lambda v: v.tensor_tensor(
                              out=swp[:, :], in0=swp[:, :], in1=trig["sinq"][:, :],
                              op=ALU.mult)),
                          wait=[(ld_sw, ld_sw.n), (ld_c, n_const)])
                    em.on("vector",
                          (lambda v, srcb=srcb: v.tensor_tensor(
                              out=srcb[:, :], in0=srcb[:, :], in1=trig["cosq"][:, :],
                              op=ALU.mult)),
                          wait=[(ld_sw, ld_sw.n)])
                    em.on("vector",
                          (lambda v, srcb=srcb, dstj=dstj: v.tensor_tensor(
                              out=qkvT[dstj][:, :], in0=srcb[:, :], in1=swp[:, :],
                              op=ALU.add)))
                rope_v_last = vpc.n
            elif l0:
                em.on("scalar",
                      (lambda s: s.activation(
                          out=qkvT[0][:, :], in_=qkvT[0][:, :],
                          func=AF.Identity, scale=sgn[:, 0:1])),
                      wait=[(vpc, sgn_v), (spc, qkv_ready_s)])
                qkv_ready_s = em.pc["scalar"].n

            kbar_v = 0
            if l0:
                for base in (0, 64):
                    em.on("vector", (lambda v, base=base: v.reduce_sum(
                        out=kbar[base:base + 64, :],
                        in_=qkvT[1][base:base + 64, :], axis=AX.X)),
                        wait=[(spc, qkv_ready_s)])
                    em.on("vector", (lambda v, base=base: v.tensor_scalar(
                        out=kbar_b[base:base + 64, :], in0=kbar[base:base + 64, :],
                        scalar1=1.0 / T, scalar2=None, op0=ALU.mult)))
                kbar_v = vpc.n
            vtm_s = [0, 0]
            for hh in range(2):
                base = hh * 64
                for kt in range(ntile):
                    pslot = kt % 2
                    vw = [(spc, spc.n)]
                    if (not l0) and hh == 0 and kt == 0:
                        vw.append((vpc, rope_v_last))
                    em.on("tensor",
                          (lambda t, kt=kt, base=base, pslot=pslot: t.transpose(
                              out=psum_rT[pslot][:, 0:64],
                              in_=qkvT[2][base:base + 64, kt * 128:(kt + 1) * 128],
                              identity=idn_s[base:base + 64, base:base + 64])),
                          wait=vw, inc=tv)
                    em.on("scalar",
                          (lambda s, kt=kt, pslot=pslot, hh=hh: s.copy(
                              out=v_tm2[hh][:, kt * 65:kt * 65 + 64],
                              in_=psum_rT[pslot][:, 0:64])),
                          wait=[(tv, tv.n)])
                vtm_s[hh] = em.pc["scalar"].n
            negm_s = [0, 0]
            if l0:
                for hh in range(2):
                    base = hh * 64
                    for i in range(ntile):
                        em.on("tensor",
                              (lambda t, i=i, base=base, hh=hh: t.matmul(
                                  out=psum_s[:, NEGC + hh * 16 + i:
                                             NEGC + hh * 16 + i + 1],
                                  lhsT=qkvT[0][base:base + 64,
                                               i * 128:(i + 1) * 128],
                                  rhs=kbar_b[base:base + 64, :],
                                  start=True, stop=True)),
                              wait=([(vpc, kbar_v), (spc, spc.n)]
                                    if i == 0 else ()),
                              inc=tm if i == ntile - 1 else None)
                    em.on("scalar", (lambda s, hh=hh: s.activation(
                        out=negm2[hh][:, 0:ntile],
                        in_=psum_s[:, NEGC + hh * 16:NEGC + hh * 16 + ntile],
                        func=AF.Identity, scale=-1.0)), wait=[(tm, tm.n)])
                    negm_s[hh] = em.pc["scalar"].n

            stat_last = [0, 0]
            comb_v = [0, 0]
            conv_s_mark = 0
            conv_dma_n = 0
            bank_mark = {}
            for hh in range(2):
                base = hh * 64
                for i in range(ntile):
                    for c in range(NCH):
                        gidx = i * NCH + c
                        bank = gidx % 3
                        col0 = bank * 512
                        kc0, kc1 = c * 512, min(T, (c + 1) * 512)
                        kcw = kc1 - kc0
                        w = []
                        if hh == 0 and gidx == 0:
                            w = [(spc, max(qkv_ready_s, negm_s[0], negm_s[1]))]
                            if not l0:
                                w.append((vpc, rope_v_last))
                            if pb_v:
                                w.append((vpc, pb_v))
                        bm = bank_mark.get(bank)
                        if bm:
                            w.append(bm)
                        if l0 and hh == 1 and gidx == 0:
                            w.append((vpc, comb_v[0]))
                        em.on("tensor",
                              (lambda t, i=i, base=base, kc0=kc0, kc1=kc1,
                               col0=col0, kcw=kcw: t.matmul(
                                   out=psum_s[:, col0:col0 + kcw],
                                   lhsT=qkvT[0][base:base + 64,
                                                i * 128:(i + 1) * 128],
                                   rhs=qkvT[1][base:base + 64, kc0:kc1],
                                   start=True, stop=True)),
                              wait=w, inc=tmm)
                        if l0:
                            em.on("scalar",
                                  (lambda s, col0=col0, gidx=gidx, i=i, hh=hh,
                                   kcw=kcw: s.activation(
                                       out=ub[gidx % 2][:, 0:kcw],
                                       in_=psum_s[:, col0:col0 + kcw],
                                       func=AF.Abs,
                                       bias=negm2[hh][:, i:i + 1], scale=1.0,
                                       accum_out=madc4[:, gidx:gidx + 1])),
                                  wait=[(tmm, tmm.n)])
                            bank_mark[bank] = (spc, em.pc["scalar"].n)
                        else:
                            em.on("vector",
                                  (lambda v, col0=col0, gidx=gidx, kcw=kcw:
                                   v.reduce_max(
                                      out=ncol4[:, gidx:gidx + 1],
                                      in_=psum_s[:, col0:col0 + kcw],
                                      axis=AX.X)),
                                  wait=[(tmm, tmm.n)])
                            bank_mark[bank] = (vpc, em.pc["vector"].n)
                if l0:
                    stat_last[hh] = em.pc["scalar"].n
                else:
                    stat_last[hh] = em.pc["vector"].n
                if l0:
                    for i in range(ntile):
                        em.on("vector", (lambda v, i=i: v.reduce_sum(
                            out=mads[:, i:i + 1],
                            in_=madc4[:, i * NCH:(i + 1) * NCH], axis=AX.X)),
                            wait=([(spc, stat_last[hh])] if i == 0 else ()))
                    em.on("vector", (lambda v: v.tensor_scalar(
                        out=mads[:, 0:ntile], in0=mads[:, 0:ntile],
                        scalar1=1.0 / T, scalar2=1e-6, op0=ALU.mult,
                        op1=ALU.add)))
                    em.on("vector", (lambda v: v.tensor_scalar(
                        out=mads[:, 0:ntile], in0=mads[:, 0:ntile],
                        scalar1=rga[:, 0:1], scalar2=None, op0=ALU.mult)))
                    em.on("vector", (lambda v, hh=hh: v.tensor_tensor(
                        out=eps2[hh][:, 0:ntile], in0=mads[:, 0:ntile],
                        in1=mads[:, 0:ntile], op=ALU.mult)))
                    em.on("vector", (lambda v, hh=hh: v.tensor_tensor(
                        out=eps2[hh][:, 0:ntile], in0=eps2[hh][:, 0:ntile],
                        in1=eps2[hh][:, 0:ntile], op=ALU.mult)))
                    em.on("vector", (lambda v, hh=hh: v.tensor_scalar(
                        out=eps2[hh][:, 0:ntile], in0=eps2[hh][:, 0:ntile],
                        scalar1=1e-6, scalar2=None, op0=ALU.mult)))
                    comb_v[hh] = em.pc["vector"].n
                    em.on("tensor",
                          (lambda t, hh=hh: t.transpose(
                              out=psum_s[0:ntile, CVC:CVC + 128],
                              in_=negm2[hh][:, 0:ntile],
                              identity=idn32_s[:, :])),
                          wait=[(spc, max(stat_last[hh], conv_s_mark)),
                                (vpc, comb_v[hh])], inc=ttr)
                    em.on("tensor",
                          (lambda t, hh=hh: t.transpose(
                              out=psum_s[0:ntile, CVC2:CVC2 + 128],
                              in_=eps2[hh][:, 0:ntile],
                              identity=idn32_s[:, :])), inc=ttr)
                    em.on("scalar", (lambda s: s.copy(
                        out=nTb[0:ntile, :], in_=psum_s[0:ntile, CVC:CVC + 128])),
                        wait=[(ttr, ttr.n), (ld_cv, ld_cv.n)])
                    em.on("scalar", (lambda s: s.copy(
                        out=nTe[0:ntile, :],
                        in_=psum_s[0:ntile, CVC2:CVC2 + 128])))
                    conv_s_mark = em.pc["scalar"].n
                    if hh == 0:
                        for a in range(ntile):
                            wv = [(spc, conv_s_mark), (tbias, tbias.n)]
                            if pb_v:
                                wv.append((vpc, pb_v))
                            wv = wv if a == 0 else ()
                            em.dma("sync",
                                   brow_b[0:1, a * 128:(a + 1) * 128],
                                   nTb[a:a + 1, 0:128], ld_cv, wait=wv)
                            em.dma("sync",
                                   eps65[0:1, a * 128:(a + 1) * 128],
                                   nTe[a:a + 1, 0:128], ld_cv)
                            em.dma("sync",
                                   eps65[64:65, a * 128:(a + 1) * 128],
                                   nTe[a:a + 1, 0:128], ld_cv)
                else:
                    for i in range(ntile):
                        em.on("vector", (lambda v, i=i, hh=hh: v.reduce_max(
                            out=nneg2[hh][:, i:i + 1],
                            in_=ncol4[:, i * NCH:(i + 1) * NCH], axis=AX.X,
                            negate=True)))
                    comb_v[hh] = em.pc["vector"].n
                    em.on("tensor",
                          (lambda t, hh=hh: t.transpose(
                              out=psum_s[0:ntile, CVC:CVC + 128],
                              in_=nneg2[hh][:, 0:ntile],
                              identity=idn32_s[:, :])),
                          wait=[(spc, max(qkv_ready_s, conv_s_mark)),
                                (vpc, comb_v[hh])], inc=ttr)
                    em.on("scalar", (lambda s: s.copy(
                        out=nTb[0:ntile, :], in_=psum_s[0:ntile, CVC:CVC + 128])),
                        wait=[(ttr, ttr.n), (ld_cv, ld_cv.n)])
                    conv_s_mark = em.pc["scalar"].n
                    if hh == 0:
                        for a in range(ntile):
                            wv = ([(spc, conv_s_mark), (tbias, tbias.n)]
                                  if a == 0 else ())
                            em.dma("sync",
                                   brow_b[0:1, a * 128:(a + 1) * 128],
                                   nTb[a:a + 1, 0:128], ld_cv, wait=wv)
                conv_dma_n = ld_cv.n

            sfin = stat_last[1] if l0 else 0
            vfin = stat_last[1] if not l0 else comb_v[1]
            ttadd = [0, 0]
            sq_mark = [0, 0]
            pT_av = [0, 0, 0, 0]
            evac_v = [0, 0]
            bias_ev = [0, 0]
            bias_rd = [0, 0]
            bc_read_s = 0
            first_pb = True
            pready_map = {}

            def emit_avmm(kt, kts, hh, qg, qw):
                aslot = qg % 2
                idx = kts.index(kt)
                p4 = kt % 4
                peng, pcnt = pready_map[kt]
                w = [(vpc if peng == "vector" else spc, pcnt)]
                if idx == 0:
                    if evac_v[aslot]:
                        w.append((vpc, evac_v[aslot]))
                    w.append((spc, vtm_s[hh]))
                last = idx == len(kts) - 1
                if hh == 0:
                    em.on("tensor",
                          (lambda t, kt=kt, p4=p4, aslot=aslot, idx=idx,
                           last=last, qw=qw: t.matmul(
                              out=psum_av[aslot][0:65, 0:qw],
                              lhsT=v_tm2[0][:, kt * 65:kt * 65 + 65],
                              rhs=pT_buf[:, p4 * 512:p4 * 512 + qw],
                              start=(idx == 0), stop=last,
                              skip_group_check=True)),
                          wait=w, inc=tav)
                else:
                    em.on("tensor",
                          (lambda t, kt=kt, p4=p4, aslot=aslot, idx=idx,
                           last=last, qw=qw: t.matmul(
                              out=psum_av[aslot][64:128, 0:qw],
                              lhsT=v_tm2[1][:, kt * 65:kt * 65 + 64],
                              rhs=pT_buf[:, p4 * 512:p4 * 512 + qw],
                              start=(idx == 0), stop=last,
                              skip_group_check=True)),
                          wait=w)
                    em.on("tensor",
                          (lambda t, kt=kt, p4=p4, aslot=aslot, idx=idx,
                           last=last, qw=qw: t.matmul(
                              out=psum_av[aslot][0:1, 0:qw],
                              lhsT=v_tm2[1][:, kt * 65 + 64:kt * 65 + 65],
                              rhs=pT_buf[:, p4 * 512:p4 * 512 + qw],
                              start=(idx == 0), stop=last,
                              skip_group_check=True)),
                          inc=tav)
                pT_av[p4] = tav.n

            for hh in range(2):
                base = hh * 64
                hoff = 0
                if hh == 1:
                    for a in range(ntile):
                        wv = ([(spc, conv_s_mark), (tbias, tbias.n),
                               (vpc, vpc.n)] if a == 0 else ())
                        em.dma("sync", brow_b[0:1, a * 128:(a + 1) * 128],
                               nTb[a:a + 1, 0:128], ld_cv, wait=wv)
                        if l0:
                            em.dma("sync", eps65[0:1, a * 128:(a + 1) * 128],
                                   nTe[a:a + 1, 0:128], ld_cv)
                            em.dma("sync", eps65[64:65, a * 128:(a + 1) * 128],
                                   nTe[a:a + 1, 0:128], ld_cv)
                    conv_dma_n = ld_cv.n
                for qg in range(NCH):
                    q0, q1 = qg * 512, min(T, (qg + 1) * 512)
                    qw = q1 - q0
                    kts = list(range(min(ntile, (q1 + 127) // 128))) if l0 \
                        else list(range(ntile))
                    bslot = qg % 2
                    bw = [(ld_cv, conv_dma_n), (spc, max(bias_ev))]
                    if first_pb:
                        bw.append((spc, max(conv_s_mark, 1)))
                    em.on("tensor",
                          (lambda t, hoff=hoff, q0=q0, q1=q1, qw=qw: t.matmul(
                              out=psum_s[:, BIA:BIA + qw],
                              lhsT=ones_b[0:1, 0:128],
                              rhs=brow_b[0:1, hoff + q0:hoff + q1],
                              start=True, stop=True)),
                          wait=bw, inc=tbias)
                    nb = tbias.n
                    bev = [(tbias, nb)]
                    if bias_rd[bslot]:
                        bev.append((vpc, bias_rd[bslot]))
                    em.on("scalar",
                          (lambda s, qw=qw, bslot=bslot: s.copy(
                              out=bias_sb[bslot][:, 0:qw],
                              in_=psum_s[:, BIA:BIA + qw])),
                          wait=bev)
                    bias_ev[bslot] = em.pc["scalar"].n
                    for idx, kt in enumerate(kts):
                        s2 = kt % 2
                        col0 = s2 * 512
                        w = []
                        if ttadd[s2]:
                            w.append((vpc, ttadd[s2]))
                        elif vfin and first_pb:
                            w.append((vpc, vfin))
                        if first_pb and idx == 0 and sfin:
                            w.append((spc, sfin))
                        em.on("tensor",
                              (lambda t, kt=kt, base=base, q0=q0, q1=q1,
                               col0=col0, qw=qw: t.matmul(
                                   out=psum_s[:, col0:col0 + qw],
                                   lhsT=qkvT[1][base:base + 64,
                                                kt * 128:(kt + 1) * 128],
                                   rhs=qkvT[0][base:base + 64, q0:q1],
                                   start=True, stop=True)),
                              wait=w, inc=tsm)
                        cw = [(tsm, tsm.n), (spc, bias_ev[bslot])]
                        if sq_mark[kt % 2]:
                            cw.append((spc, sq_mark[kt % 2]))
                        em.on("vector",
                              (lambda v, col0=col0, kt=kt, qw=qw, bslot=bslot:
                               v.tensor_tensor(
                                  out=ub[kt % 2][:, 0:qw],
                                  in0=psum_s[:, col0:col0 + qw],
                                  in1=bias_sb[bslot][:, 0:qw], op=ALU.add)),
                              wait=cw)
                        ttadd[s2] = em.pc["vector"].n
                        bias_rd[bslot] = em.pc["vector"].n
                        p4 = kt % 4
                        if l0:
                            em.on("vector",
                                  (lambda v, kt=kt, qw=qw:
                                   v.scalar_tensor_tensor(
                                      out=ub[kt % 2][:, 0:qw],
                                      in0=ub[kt % 2][:, 0:qw],
                                      scalar=0.0, in1=ub[kt % 2][:, 0:qw],
                                      op0=ALU.max, op1=ALU.mult)))
                            pw = [(vpc, vpc.n)]
                            if pT_av[p4]:
                                pw.append((tav, pT_av[p4]))
                            em.on("scalar",
                                  (lambda s, kt=kt, p4=p4, qw=qw: s.activation(
                                      out=pT_buf[:, p4 * 512:p4 * 512 + qw],
                                      in_=ub[kt % 2][:, 0:qw], func=AF.Square)),
                                  wait=pw)
                            sq_mark[kt % 2] = em.pc["scalar"].n
                            pready = ("scalar", em.pc["scalar"].n)
                            if kt >= 4 * qg:
                                jj = kt - 4 * qg
                                em.on("vector",
                                      (lambda v, p4=p4, jj=jj, qw=qw:
                                       v.tensor_tensor(
                                          out=pT_buf[:, p4 * 512:p4 * 512 + qw],
                                          in0=pT_buf[:, p4 * 512:p4 * 512 + qw],
                                          in1=trimT_s[:, jj * 512:jj * 512 + qw],
                                          op=ALU.mult)),
                                      wait=[(spc, spc.n)])
                                pready = ("vector", em.pc["vector"].n)
                        else:
                            pw = [(vpc, vpc.n)]
                            if pT_av[p4]:
                                pw.append((tav, pT_av[p4]))
                            em.on("scalar",
                                  (lambda s, kt=kt, p4=p4, qw=qw: s.activation(
                                      out=pT_buf[:, p4 * 512:p4 * 512 + qw],
                                      in_=ub[kt % 2][:, 0:qw], func=AF.Exp)),
                                  wait=pw)
                            sq_mark[kt % 2] = em.pc["scalar"].n
                            pready = ("scalar", em.pc["scalar"].n)
                        pready_map[kt] = pready
                        if idx >= 1:
                            emit_avmm(kts[idx - 1], kts, hh, qg, qw)
                    emit_avmm(kts[-1], kts, hh, qg, qw)
                    aslot = qg % 2
                    lrow = 64 if hh == 0 else 0
                    ew = [(tav, tav.n)]
                    if l0:
                        em.on("vector",
                              (lambda v, aslot=aslot, lrow=lrow, hoff=hoff,
                               q0=q0, q1=q1, qw=qw: v.tensor_tensor(
                                  out=rrow_s[lrow:lrow + 1, 0:qw],
                                  in0=psum_av[aslot][lrow:lrow + 1, 0:qw],
                                  in1=eps65[lrow:lrow + 1, hoff + q0:hoff + q1],
                                  op=ALU.add)),
                              wait=ew + ([(tbc, tbc.n)] if tbc.n else []))
                        em.on("vector",
                              (lambda v, lrow=lrow, qw=qw: v.reciprocal(
                                  out=rrow_s[lrow:lrow + 1, 0:qw],
                                  in_=rrow_s[lrow:lrow + 1, 0:qw])))
                    else:
                        em.on("vector",
                              (lambda v, aslot=aslot, lrow=lrow, qw=qw:
                               v.reciprocal(
                                  out=rrow_s[lrow:lrow + 1, 0:qw],
                                  in_=psum_av[aslot][lrow:lrow + 1, 0:qw])),
                              wait=ew + ([(tbc, tbc.n)] if tbc.n else []))
                        em.on("vector",
                              (lambda v, lrow=lrow, qw=qw: v.tensor_copy(
                                  out=rrow_s[lrow:lrow + 1, 0:qw],
                                  in_=rrow_s[lrow:lrow + 1, 0:qw])))
                    rr_v = em.pc["vector"].n
                    bcw = [(vpc, rr_v)]
                    if bc_read_s:
                        bcw.append((spc, bc_read_s))
                    if hh == 0:
                        em.on("tensor",
                              (lambda t, qw=qw: t.matmul(
                                  out=psum_s[0:64, BCA:BCA + qw],
                                  lhsT=onesP[64:65, 0:64],
                                  rhs=rrow_s[64:65, 0:qw],
                                  start=True, stop=True)),
                              wait=bcw, inc=tbc)
                        ob, oe = 0, 64
                    else:
                        em.on("tensor",
                              (lambda t, qw=qw: t.matmul(
                                  out=psum_s[64:128, BCA:BCA + qw],
                                  lhsT=ones_s[0:1, 0:64],
                                  rhs=rrow_s[0:1, 0:qw],
                                  start=True, stop=True)),
                              wait=bcw, inc=tbc)
                        ob, oe = 64, 128
                    rbw = [(tbc, tbc.n)]
                    if evac_v[aslot]:
                        rbw.append((vpc, evac_v[aslot]))
                    em.on("scalar",
                          (lambda s, aslot=aslot, ob=ob, oe=oe, qw=qw: s.copy(
                              out=rb_sb[aslot][ob:oe, 0:qw],
                              in_=psum_s[ob:oe, BCA:BCA + qw])),
                          wait=rbw)
                    bc_read_s = em.pc["scalar"].n
                    em.on("vector",
                          (lambda v, aslot=aslot, ob=ob, oe=oe, hp=hp, q0=q0,
                           q1=q1, qw=qw: v.tensor_tensor(
                              out=ytb[hp][ob:oe, q0:q1],
                              in0=psum_av[aslot][ob:oe, 0:qw],
                              in1=rb_sb[aslot][ob:oe, 0:qw], op=ALU.mult)),
                          wait=[(spc, bc_read_s)])
                    evac_v[aslot] = em.pc["vector"].n
                    first_pb = False
            pb_v = em.pc["vector"].n
            pb_s = em.pc["scalar"].n
            ytb_v = pb_v

        def emit_epilogue(i):
            x1s = x1t[i % 2]
            for kk in range(dtile):
                eslot = kk % 2
                off = xoff + eslot * 512
                tw = [(spc, epi_guard[eslot][0])] if epi_guard[eslot][0] else []
                tw += [(vpc, epi_guard[eslot][1])] if epi_guard[eslot][1] else []
                em.on("tensor",
                      (lambda t, kk=kk, off=off, x1s=x1s: t.transpose(
                          out=psum_s[:, off:off + 128],
                          in_=x1s[:, kk * 128:(kk + 1) * 128],
                          identity=idn32_s[:, :])),
                      wait=tw + ([(spc, ln_done[i % 2])] if kk == 0 else []),
                      inc=tgt)
                ceng = "scalar" if kk % 2 == 0 else "vector"
                fn = ((lambda e, kk=kk, off=off: e.copy(
                          out=x3Tt[:, kk * 128:(kk + 1) * 128],
                          in_=psum_s[:, off:off + 128]))
                      if ceng == "scalar" else
                      (lambda e, kk=kk, off=off: e.tensor_copy(
                          out=x3Tt[:, kk * 128:(kk + 1) * 128],
                          in_=psum_s[:, off:off + 128])))
                em.on(ceng, fn, wait=[(tgt, tgt.n)])
                cp = em.pc[ceng]
                epi_guard[eslot] = ((cp.n, 0) if ceng == "scalar"
                                    else (0, cp.n))
            ecs = em.pc["scalar"].n
            ecv = em.pc["vector"].n
            for kk in range(dtile):
                em.on("tensor",
                      (lambda t, kk=kk: t.matmul(
                          out=psum_av[0][:, 256:264],
                          lhsT=x3Tt[:, kk * 128:(kk + 1) * 128],
                          rhs=gwT_s[:, kk * 8:(kk + 1) * 8],
                          start=(kk == 0), stop=(kk == dtile - 1))),
                      wait=([(spc, ecs), (vpc, ecv)] if kk == 0 else ()),
                      inc=tgt if kk == dtile - 1 else None)
            em.on("vector", (lambda v: v.reduce_max(
                out=g1[:], in_=psum_av[0][:, 256:264], axis=AX.X, negate=True)),
                wait=[(tgt, tgt.n)])
            em.on("scalar",
                  (lambda s: s.activation(
                      out=sc8[:, :], in_=psum_av[0][:, 256:264], func=AF.Exp,
                      bias=g1[:], scale=1.0, accum_out=gd[:])),
                  wait=[(vpc, vpc.n)])
            em.on("vector", (lambda v: v.reciprocal(out=gd[:], in_=gd[:])),
                  wait=[(spc, spc.n)])
            em.on("vector", (lambda v: v.tensor_scalar(
                out=sc8[:, :], in0=sc8[:, :], scalar1=gd[:], scalar2=None,
                op0=ALU.mult)))
            if l0:
                em.dma("sync",
                       io["scores_out"][tok0 + i * 128: tok0 + (i + 1) * 128, :],
                       sc8[:, :], st_o, wait=[(vpc, vpc.n)])
                em.on("vector", (lambda v: v.tensor_copy(out=g1[:], in_=g1[:])),
                      wait=[(st_o, st_o.n)])
            else:
                em.on("vector", (lambda v: v.reduce_max(
                    out=g1[:], in_=sc8[:, :], axis=AX.X)))
                em.on("vector", (lambda v: v.tensor_scalar(
                    out=msk[:, :], in0=sc8[:, :], scalar1=g1[:], scalar2=None,
                    op0=ALU.is_ge)))
                em.on("vector", (lambda v: v.scalar_tensor_tensor(
                    out=w8[:, :], in0=msk[:, :], scalar=-1e9, in1=sc8[:, :],
                    op0=ALU.mult, op1=ALU.add)))
                em.on("vector", (lambda v: v.reduce_max(
                    out=g2[:], in_=w8[:, :], axis=AX.X)))
                em.on("vector", (lambda v: v.tensor_scalar(
                    out=msk[:, :], in0=sc8[:, :], scalar1=g2[:], scalar2=None,
                    op0=ALU.is_ge)))
                em.on("vector", (lambda v: v.tensor_tensor(
                    out=g1[:], in0=g1[:], in1=g2[:], op=ALU.add)))
                em.on("vector", (lambda v: v.tensor_scalar(
                    out=g1[:], in0=g1[:], scalar1=1e-20, scalar2=None,
                    op0=ALU.add)))
                em.on("vector", (lambda v: v.reciprocal(out=g1[:], in_=g1[:])))
                em.on("vector", (lambda v: v.tensor_tensor(
                    out=w8[:, :], in0=sc8[:, :], in1=msk[:, :], op=ALU.mult)))
                em.on("vector", (lambda v: v.tensor_scalar(
                    out=w8[:, :], in0=w8[:, :], scalar1=g1[:], scalar2=None,
                    op0=ALU.mult)))
                em.dma("sync",
                       io["wdense_out"][tok0 + i * 128: tok0 + (i + 1) * 128, :],
                       w8[:, :], st_o, wait=[(vpc, vpc.n)])
                em.on("vector", (lambda v: v.tensor_copy(out=g2[:], in_=g2[:])),
                      wait=[(st_o, st_o.n)])
            epi_done[i % 2] = (tgt.n, em.pc["vector"].n)

        pend_epi = None
        for i in range(ntile):
            slot = i % 2
            em.dma("sync", xres[slot][:, :],
                   io["x_in"][tok0 + i * 128: tok0 + (i + 1) * 128, :], ld_r[slot],
                   wait=[(vpc, vpc.n)] if i >= 2 else ())
            NDC = (D + 511) // 512
            for kt in range(dtile):
                for nc_ in range(NDC):
                    n0, n1 = nc_ * 512, min(D, (nc_ + 1) * 512)
                    em.on("tensor",
                          (lambda t, i=i, kt=kt, n0=n0, n1=n1: t.matmul(
                              out=psum_s[:, n0:n1],
                              lhsT=ytb[kt][:, i * 128:(i + 1) * 128],
                              rhs=wproj[kt][:, n0:n1],
                              start=(kt == 0), stop=False,
                              skip_group_check=True)),
                          wait=([(ld_wp, dtile * 16), (spc, spc.n),
                                 (vpc, max(t1_free, ytb_v))]
                                if (kt == 0 and nc_ == 0) else ()),
                          inc=None)
            for nc_ in range(NDC):
                n0, n1 = nc_ * 512, min(D, (nc_ + 1) * 512)
                em.on("tensor",
                      (lambda t, n0=n0, n1=n1: t.matmul(
                          out=psum_s[:, n0:n1],
                          lhsT=ones_b[:, 0:128],
                          rhs=bprow_b[:, n0:n1],
                          start=False, stop=True,
                          skip_group_check=True)),
                      wait=([(vpc, bias_b_v)] if (i == 0 and nc_ == 0) else ()),
                      inc=tp if nc_ == NDC - 1 else None)
            em.on("vector",
                  (lambda v, slot=slot: v.scalar_tensor_tensor(
                      out=t1[:, :], in0=psum_s[:, 0:D], scalar=1.0,
                      in1=xres[slot][:, :], op0=ALU.mult, op1=ALU.add)),
                  wait=[(tp, tp.n), (ld_r[slot], ld_r[slot].n)])
            t1_free = vpc.n
            x1s = x1t[i % 2]
            lw = [(st_o, st_o.n)]
            if epi_done[i % 2][0]:
                lw.append((tgt, epi_done[i % 2][0]))
            emit_ln_tile(em, x1s[:, :], t1[:, :], D, c1, c2, scr_ln,
                         out_wait=lw)
            ln_done[i % 2] = em.pc["scalar"].n
            em.dma("sync", io["x1_out"][tok0 + i * 128: tok0 + (i + 1) * 128, :],
                   x1s[:, :], st_o, wait=[(spc, spc.n)])
            if pend_epi is not None:
                emit_epilogue(pend_epi)
            pend_epi = i
        emit_epilogue(pend_epi)
        epi_s = em.pc["scalar"].n
        epi_v = em.pc["vector"].n
    return st_o



def np_bf16(x):
    import ml_dtypes
    return np.asarray(x, dtype=ml_dtypes.bfloat16)






def emit_gelu_chain(em, psum_in, dst_bf16_ap, n, h, hsq, b1col, wait_t, ld_c, nldc,
                    war=()):
    spc, vpc = em.pc["scalar"], em.pc["vector"]
    em.on("scalar",
          (lambda s: s.activation(out=h[:, 0:n], in_=psum_in[:, 0:n],
                                  func=AF.Identity, bias=b1col, scale=1.0)),
          wait=[wait_t, (ld_c, nldc), (vpc, vpc.n)] + list(war))
    hfree = spc.n
    em.on("vector", (lambda v: v.tensor_tensor(
        out=hsq[:, 0:n], in0=h[:, 0:n], in1=h[:, 0:n], op=ALU.mult)),
        wait=[(spc, spc.n)])
    em.on("vector", (lambda v: v.tensor_scalar(
        out=hsq[:, 0:n], in0=hsq[:, 0:n], scalar1=GELU_A, scalar2=1.0,
        op0=ALU.mult, op1=ALU.add)))
    em.on("vector", (lambda v: v.tensor_tensor(
        out=hsq[:, 0:n], in0=hsq[:, 0:n], in1=h[:, 0:n], op=ALU.mult)))
    em.on("scalar",
          (lambda s: s.activation(out=hsq[:, 0:n], in_=hsq[:, 0:n], func=AF.Tanh,
                                  scale=GELU_C)),
          wait=[(vpc, vpc.n)])
    em.on("vector", (lambda v: v.scalar_tensor_tensor(
        out=dst_bf16_ap, in0=hsq[:, 0:n], scalar=1.0, in1=h[:, 0:n],
        op0=ALU.add, op1=ALU.mult)),
        wait=[(spc, spc.n)])
    return (hfree, 0)


def emit_glu_chain(em, psum_g, psum_u, dst_bf16_ap, n, h, wait_t, war=()):
    spc, vpc = em.pc["scalar"], em.pc["vector"]
    em.on("scalar",
          (lambda s: s.activation(out=h[:, 0:n], in_=psum_g[:, 0:n], func=AF.Silu)),
          wait=[wait_t, (vpc, vpc.n)] + list(war))
    gfree = spc.n
    em.on("vector", (lambda v: v.tensor_tensor(
        out=dst_bf16_ap, in0=h[:, 0:n], in1=psum_u[:, 0:n], op=ALU.mult)),
        wait=[(spc, spc.n)])
    return (gfree, vpc.n)


def emit_bcast_into(em, dst, src_row_ap, ncols, ones_row, psum_pool, tbc, wait=()):
    spc, vpc = em.pc["scalar"], em.pc["vector"]
    nch = (ncols + 511) // 512
    for c in range(nch):
        c0, c1 = c * 512, min(ncols, (c + 1) * 512)
        em.on("tensor",
              (lambda t, c0=c0, c1=c1: t.matmul(
                  out=psum_pool[:, 0:c1 - c0], lhsT=ones_row[:, 0:128],
                  rhs=src_row_ap[:, c0:c1], start=True, stop=True)),
              wait=(list(wait) + [(spc, spc.n), (vpc, vpc.n)]), inc=tbc)
        em.on("scalar", (lambda s, c0=c0, c1=c1: s.copy(
            out=dst[:, c0:c1], in_=psum_pool[:, 0:c1 - c0])),
            wait=[(tbc, tbc.n)])
    return em.pc["scalar"].n


def emit_moe_ffn(em, g, io, mode):
    Tl, E, D = g["Tl"], g["E"], g["D"]
    F_e, F_s, TBLK = g["F_e"], g["F_s"], g["TBLK"]
    S_tot = g["S_tot"]
    NHALF = g.get("NHALF", 2 if mode == "l0" else 1)
    l0 = mode == "l0"
    pfx = "f0" if l0 else "f1"
    dtile = D // 128
    ntile_l = Tl // 128
    NB = Tl // TBLK
    assert NB % 2 == 0
    TPB = TBLK // 128
    Fh = F_e // NHALF
    fth = Fh // 128
    fts = F_s // 128
    NDC = (D + 511) // 512
    assert NDC == 2
    vpc, spc = em.pc["vector"], em.pc["scalar"]
    SPL = fth if NHALF > 1 else (fts + 1) // 2
    assert SPL >= fth and fts - SPL <= SPL and fts - SPL >= 0
    widthA = max(Fh, SPL * 128)
    widthB = max(Fh, (fts - SPL) * 128)

    upA = [sb(em, f"{pfx}_upA{k}", [128, widthA], BF16) for k in range(dtile)]
    upB = [sb(em, f"{pfx}_upB{k}", [128, widthB], BF16) for k in range(dtile)]
    if not l0:
        up2A = [sb(em, f"{pfx}_u2A{k}", [128, widthA], BF16) for k in range(dtile)]
        up2B = [sb(em, f"{pfx}_u2B{k}", [128, widthB], BF16) for k in range(dtile)]
    dnA = [sb(em, f"{pfx}_dnA{k}", [128, D], BF16) for k in range(SPL)]
    dnB = [sb(em, f"{pfx}_dnB{k}", [128, D], BF16)
           for k in range(max(fth, fts - SPL))]
    hT = sb(em, f"{pfx}_hT", [128, max(2 * fth, fts) * TBLK], BF16)
    xblk = [sb(em, f"{pfx}_xb{j}", [128, dtile * TBLK], BF16) for j in range(3)]
    h_scr = sb(em, f"{pfx}_hscr", [128, max(TBLK, 512)], BF16)
    q_scr = sb(em, f"{pfx}_qscr", [128, max(TBLK, 512)], BF16)
    gyst = [sb(em, f"{pfx}_gyst{j}", [128, D], F32) for j in range(2)]
    glp = sb(em, f"{pfx}_glp", [128, ntile_l * E], F32)
    glp_h = sb(em, f"{pfx}_glph", [128, ntile_l * E], F32) if l0 else None
    bcols = sb(em, f"{pfx}_b1c", [128, max(E * F_e, F_s) // 128], F32) if l0 else None
    b2b = sb(em, f"{pfx}_b2b", [128, D], F32) if l0 else None
    ones_s = sb(em, f"{pfx}_ones", [1, 128], F32)
    onesc_s = sb(em, f"{pfx}_onesc", [128, 1], F32)
    idn_s = sb(em, f"{pfx}_idn", [128, 128], BF16)
    cA = sb(em, f"{pfx}_cA", [128, 1], F32)
    cB = sb(em, f"{pfx}_cB", [128, 1], F32)
    cC = sb(em, f"{pfx}_cC", [128, 1], F32)
    cD = sb(em, f"{pfx}_cD", [128, 1], F32)
    cE = sb(em, f"{pfx}_cE", [128, 1], F32)
    c11 = sb(em, f"{pfx}_c11", [1, 1], F32)
    b2row_s = sb(em, f"{pfx}_b2row", [1, D], F32) if l0 else None

    psum_h = ps(em, f"{pfx}_ph", [128, 512], F32)
    psum_h2 = ps(em, f"{pfx}_ph2", [128, 512], F32)
    psum_y = [ps(em, f"{pfx}_py{j}", [128, 512], F32) for j in range(2)]
    psum_b = ps(em, f"{pfx}_pb", [128, 512], F32)
    psum_tb = [ps(em, f"{pfx}_pt{j}", [128, 1024], BF16) for j in range(2)]
    psum_x = ps(em, f"{pfx}_px", [128, 512], F32)
    if l0:
        UPBK = [psum_h, psum_h2]
        DNB = [psum_y[0], psum_y[1]]
    else:
        UPPAIR = [(psum_h, psum_h2), (psum_y[0], psum_y[1])]
        DNB = [psum_b, psum_x]

    ld_c = em.ctr(f"{pfx}_ldc")
    ld_w = [em.ctr(f"{pfx}_ldw0"), em.ctr(f"{pfx}_ldw1")]
    ld_xb = [em.ctr(f"{pfx}_lxb0"), em.ctr(f"{pfx}_lxb1"), em.ctr(f"{pfx}_lxb2")]
    ld_xt = [em.ctr(f"{pfx}_lxt0"), em.ctr(f"{pfx}_lxt1")]
    ld_res = [em.ctr(f"{pfx}_lr0"), em.ctr(f"{pfx}_lr1")]
    ld_gy = [em.ctr(f"{pfx}_lg0"), em.ctr(f"{pfx}_lg1")]
    ld_sc = em.ctr(f"{pfx}_lsc")
    ld_bs = em.ctr(f"{pfx}_lbs")
    ld_b2 = em.ctr(f"{pfx}_lb2")
    st_t = em.ctr(f"{pfx}_stt")
    st_acc = [em.ctr(f"{pfx}_sac0"), em.ctr(f"{pfx}_sac1")]
    st_o = em.ctr(f"{pfx}_sto")
    tx = em.ctr(f"{pfx}_tx")
    tup = em.ctr(f"{pfx}_tup")
    tdn = em.ctr(f"{pfx}_tdn")
    tb_ = em.ctr(f"{pfx}_tb")

    em.dma("sync", ones_s[:, :], io["ones_f32"][:, :], ld_c)
    em.dma("sync", onesc_s[:, :], io["ones_col"][:, :], ld_c)
    em.dma("sync", idn_s[:, :], io["idn"][:, :], ld_c)
    if l0:
        em.dma("sync", bcols[:, 0:E * F_e // 128], io["b1_all_cols"][:, :], ld_c)
    n_const0 = ld_c.n

    xld = [hT[:, j * 1024:(j + 1) * 1024] for j in range(2)]
    xtc = [hT[:, (2 + j) * 1024:(3 + j) * 1024] for j in range(2)]
    xt_last = [0, 0]
    xt_uses = [0, 0]
    for i in range(ntile_l):
        slot = i % 2
        em.dma("gpsimd", xld[slot],
               io["x1_in"][i * 128:(i + 1) * 128, :], ld_xt[slot],
               wait=[(tx, xt_last[slot])] if xt_last[slot] else ())
        xt_uses[slot] += 1
        for kk in range(dtile):
            pslot = kk % 2
            w0 = [(spc, spc.n)]
            if kk == 0:
                w0 += [(ld_xt[slot], xt_uses[slot] * 16), (ld_c, n_const0)]
                if i >= 2:
                    w0.append((st_t, (i - 1) * 16))
            em.on("tensor",
                  (lambda t, slot=slot, kk=kk, pslot=pslot: t.transpose(
                      out=psum_tb[pslot][:, 0:128],
                      in_=xld[slot][:, kk * 128:(kk + 1) * 128],
                      identity=idn_s[:, :])),
                  wait=w0, inc=tx)
            em.on("scalar",
                  (lambda s, kk=kk, slot=slot, pslot=pslot: s.copy(
                      out=xtc[slot][:, kk * 128:(kk + 1) * 128],
                      in_=psum_tb[pslot][:, 0:128])),
                  wait=[(tx, tx.n)])
        xt_last[slot] = tx.n
        em.dma("sync",
               io["x1T_d"].rearrange("(dt p) t -> p dt t", p=128)[
                   :, :, i * 128:(i + 1) * 128],
               xtc[slot].rearrange("p (dt c) -> p dt c", dt=dtile),
               st_t, wait=[(spc, spc.n)])
    n_xt = st_t.n

    em.dma("sync", glp[:, :], io["g_loc_packed"][:, :], ld_sc)
    if l0:
        esB = ExitStack()
        PCK = S_tot * E // 128
        sc_pack = esB.enter_context(em.nc.sbuf_tensor(f"{pfx}_scp", [128, PCK], F32))
        msk_p = esB.enter_context(em.nc.sbuf_tensor(f"{pfx}_mskp", [128, PCK], F32))
        em.dma("sync", sc_pack[:, :], io["sc_packed"][:, :], ld_sc)
        KCAP = float(g["CAP"] * S_tot)
        lo, hi, mid, cnt = cB, cC, cD, cE
        em.on("vector", (lambda v: v.memset(lo[:], 0.0)),
              wait=[(ld_sc, 32), (ld_c, n_const0)])
        em.on("vector", (lambda v: v.memset(hi[:], 2.0)))
        for it in range(26):
            em.on("vector", (lambda v: v.tensor_tensor(
                out=mid[:], in0=lo[:], in1=hi[:], op=ALU.add)))
            em.on("vector", (lambda v: v.tensor_scalar(
                out=mid[:], in0=mid[:], scalar1=0.5, scalar2=None, op0=ALU.mult)))
            em.on("vector", (lambda v: v.tensor_scalar(
                out=msk_p[:, :], in0=sc_pack[:, :], scalar1=mid[:], scalar2=None,
                op0=ALU.is_ge)))
            em.on("vector", (lambda v: v.reduce_sum(
                out=cnt[:], in_=msk_p[:, :], axis=AX.X)))
            em.on("tensor",
                  (lambda t: t.matmul(out=psum_b[0:1, 0:1], lhsT=onesc_s[:, :],
                                      rhs=cnt[:, :], start=True, stop=True)),
                  wait=[(vpc, vpc.n), (spc, spc.n)], inc=tb_)
            em.on("scalar", (lambda s: s.copy(out=c11[:, :], in_=psum_b[0:1, 0:1])),
                  wait=[(tb_, tb_.n)])
            em.on("tensor",
                  (lambda t: t.matmul(out=psum_b[:, 1:2], lhsT=ones_s[:, :],
                                      rhs=c11[:, :], start=True, stop=True)),
                  wait=[(spc, spc.n)], inc=tb_)
            em.on("vector", (lambda v: v.tensor_copy(out=cnt[:], in_=psum_b[:, 1:2])),
                  wait=[(tb_, tb_.n)])
            em.on("vector", (lambda v: v.tensor_scalar(
                out=cnt[:], in0=cnt[:], scalar1=KCAP, scalar2=None, op0=ALU.is_ge)))
            em.on("vector", (lambda v: v.tensor_tensor(
                out=cA[:], in0=mid[:], in1=lo[:], op=ALU.subtract)))
            em.on("vector", (lambda v: v.tensor_tensor(
                out=cA[:], in0=cA[:], in1=cnt[:], op=ALU.mult)))
            em.on("vector", (lambda v: v.tensor_tensor(
                out=lo[:], in0=lo[:], in1=cA[:], op=ALU.add)))
            em.on("vector", (lambda v: v.tensor_tensor(
                out=cA[:], in0=hi[:], in1=mid[:], op=ALU.subtract)))
            em.on("vector", (lambda v: v.tensor_tensor(
                out=cA[:], in0=cA[:], in1=cnt[:], op=ALU.mult)))
            em.on("vector", (lambda v: v.tensor_tensor(
                out=hi[:], in0=mid[:], in1=cA[:], op=ALU.add)))
        em.on("vector", (lambda v: v.tensor_scalar(
            out=msk_p[:, 0:ntile_l * E], in0=glp[:, :], scalar1=lo[:],
            scalar2=None, op0=ALU.is_ge)))
        em.on("vector", (lambda v: v.tensor_tensor(
            out=glp[:, :], in0=glp[:, :], in1=msk_p[:, 0:ntile_l * E],
            op=ALU.mult)))
        em.on("vector", (lambda v: v.tensor_scalar(
            out=glp_h[:, :], in0=glp[:, :], scalar1=0.5, scalar2=None,
            op0=ALU.mult)))
        esB.close()
    else:
        em.on("vector", (lambda v: v.memset(cA[:], 0.0)),
              wait=[(ld_sc, 16), (ld_c, n_const0)])
    gate_v = vpc.n

    state = dict(gy_i=0, gy_last=[0, 0], xb_guard=[0, 0, 0], xb_n=[0, 0, 0],
                 ev={}, yev=[0, 0], wfree=[0, 0], acc_guard=(0, 0),
                 act_v=0)

    def shared_up(ft):
        return (upA, ft) if ft < SPL else (upB, ft - SPL)

    def shared_up2(ft):
        return (up2A, ft) if ft < SPL else (up2B, ft - SPL)

    def shared_dn(ft):
        return (dnA, ft) if ft < SPL else (dnB, ft - SPL)

    def load_xblk(tb, slot, extra=()):
        src = io["x1T_d"].rearrange("(dt p) t -> p dt t", p=128)[
            :, :, tb * TBLK:(tb + 1) * TBLK]
        w = [(st_t, n_xt)] + list(extra)
        if state["xb_guard"][slot]:
            w.append((tup, state["xb_guard"][slot]))
        em.dma("gpsimd",
               xblk[slot].rearrange("p (dt c) -> p dt c", dt=dtile),
               src, ld_xb[slot], wait=w)
        state["xb_n"][slot] += 1
        return (ld_xb[slot], state["xb_n"][slot] * 16)

    def up_pair_mms(ft, kt, lhs_ap, lhs2_ap, slots, xw):
        first = kt == 0
        for bi in (0, 1):
            banks = (UPBK[bi], None) if l0 else UPPAIR[bi]
            w = []
            if first:
                if bi == 0:
                    w = [xw[0], xw[1], state["wl"]]
                for key in (("u", bi), ("u2", bi)):
                    ev = state["ev"].get(key)
                    if ev:
                        w.append(ev)
            em.on("tensor",
                  (lambda t, lhs_ap=lhs_ap, banks=banks, s=slots[bi],
                   first=first, kt=kt: t.matmul(
                       out=banks[0][:, 0:TBLK],
                       lhsT=lhs_ap,
                       rhs=xblk[s][:, kt * TBLK:(kt + 1) * TBLK],
                       start=first, stop=(kt == dtile - 1))),
                  wait=w,
                  inc=(tup if (l0 and kt == dtile - 1) else None))
        if not l0:
            for bi in (0, 1):
                banks = UPPAIR[bi]
                em.on("tensor",
                      (lambda t, lhs2_ap=lhs2_ap, banks=banks, s=slots[bi],
                       first=first, kt=kt: t.matmul(
                           out=banks[1][:, 0:TBLK],
                           lhsT=lhs2_ap,
                           rhs=xblk[s][:, kt * TBLK:(kt + 1) * TBLK],
                           start=first, stop=(kt == dtile - 1))),
                      inc=(tup if kt == dtile - 1 else None))

    def chain(ft, bi, col, bias_ap, bias_ctr, bias_n, wait_t):
        dst = hT[:, col * TBLK:(col + 1) * TBLK]
        if l0:
            bank = UPBK[bi]
            free = emit_gelu_chain(em, bank, dst, TBLK, h_scr, q_scr,
                                   bias_ap, wait_t, bias_ctr, bias_n)
            state["ev"][("u", bi)] = (spc, free[0])
        else:
            banks = UPPAIR[bi]
            free = emit_glu_chain(em, banks[0], banks[1], dst, TBLK, h_scr,
                                  wait_t)
            state["ev"][("u", bi)] = (spc, free[0])
            state["ev"][("u2", bi)] = (vpc, free[1])

    def down_tile(col_of, tglob, ft_n, shared, e=None, v_i=None):
        dw = [(vpc, state["act_v"])]
        for dcp in (0, 1):
            if state["yev"][dcp]:
                dw.append((spc, state["yev"][dcp]))
        for f in range(ft_n):
            if shared:
                dns_, lf = shared_dn(f)
                lhs = dns_[lf]
            else:
                lhs = None
            for dc in (0, 1):
                n0, n1 = dc * 512, (dc + 1) * 512
                em.on("tensor",
                      (lambda t, f=f, dc=dc, n0=n0, n1=n1, tglob=tglob,
                       col_of=col_of, shared=shared: t.matmul(
                           out=DNB[dc][:, 0:512],
                           lhsT=hT[:, col_of(f) * TBLK +
                                   (tglob % TPB) * 128:
                                   col_of(f) * TBLK + (tglob % TPB) * 128 + 128],
                           rhs=(shared_dn(f)[0][shared_dn(f)[1]] if shared
                                else state["dns"][f])[:, n0:n1],
                           start=(f == 0), stop=(f == ft_n - 1))),
                      wait=(dw if (f == 0 and dc == 0) else ()),
                      inc=tdn if (f == ft_n - 1 and dc == 1) else None)

    NV = E * NHALF
    for v_i in range(NV):
        e = v_i // NHALF
        hf = v_i % NHALF
        wslot = v_i % 2
        ups = upA if wslot == 0 else upB
        up2s = (up2A if wslot == 0 else up2B) if not l0 else None
        dns = dnA if wslot == 0 else dnB
        state["dns"] = dns
        wwait = ([(tup, state["wfree"][wslot]), (tdn, tdn.n)]
                 if state["wfree"][wslot] else [])
        for kk in range(dtile):
            em.dma("gpsimd", ups[kk][:, 0:Fh],
                   io["W_up_all"][e, kk * 128:(kk + 1) * 128,
                                  hf * Fh:(hf + 1) * Fh],
                   ld_w[wslot], wait=(wwait if kk == 0 else ()))
            if not l0:
                em.dma("gpsimd", up2s[kk][:, 0:Fh],
                       io["W_up2_all"][e, kk * 128:(kk + 1) * 128,
                                       hf * Fh:(hf + 1) * Fh],
                       ld_w[wslot])
        for f in range(fth):
            em.dma("gpsimd", dns[f][:, :],
                   io["W_dn_all"][e, hf * Fh + f * 128: hf * Fh + (f + 1) * 128, :],
                   ld_w[wslot])
        state["wl"] = (ld_w[wslot], ld_w[wslot].n)
        if l0 and hf == 0:
            em.dma("sync", b2row_s[:, :], io["b2_all"][e:e + 1, :], ld_b2,
                   wait=[(tb_, tb_.n)])
            emit_bcast_into(em, b2b, b2row_s[0:1, :], D, ones_s, psum_b, tb_,
                            wait=[(ld_c, n_const0), (ld_b2, ld_b2.n)])
        b2_s = em.pc["scalar"].n

        for prx in range(NB // 2):
            tb0, tb1 = 2 * prx, 2 * prx + 1
            s0 = (2 * prx) % 3
            s1 = (2 * prx + 1) % 3
            xw0 = load_xblk(tb0, s0)
            xw1 = load_xblk(tb1, s1)
            for ft in range(fth):
                for kt in range(dtile):
                    up_pair_mms(ft, kt,
                                ups[kt][:, ft * 128:(ft + 1) * 128],
                                (up2s[kt][:, ft * 128:(ft + 1) * 128]
                                 if not l0 else None),
                                (s0, s1), (xw0, xw1))
                bia = (bcols[:, e * (F_e // 128) + hf * fth + ft:
                             e * (F_e // 128) + hf * fth + ft + 1]
                       if l0 else None)
                chain(ft, 0, ft * 2 + 0, bia, ld_c, n_const0, (tup, tup.n))
                chain(ft, 1, ft * 2 + 1, bia, ld_c, n_const0, (tup, tup.n))
            state["xb_guard"][s0] = tup.n
            state["xb_guard"][s1] = tup.n
            state["act_v"] = vpc.n
            for tt in range(2 * TPB):
                blk = tt // TPB
                a_loc = (tb0 + blk) * TPB + (tt % TPB)
                gc_col = a_loc * E + e
                state["gy_i"] ^= 1
                gslot = state["gy_i"]
                down_tile((lambda f, blk=blk: f * 2 + blk), tt, fth,
                          shared=False, e=e, v_i=v_i)
                gc = glp_h if l0 else glp
                gw = [(tdn, tdn.n), (vpc, max(gate_v, vpc.n))]
                if state["gy_last"][gslot]:
                    gw.append((st_acc[gslot], state["gy_last"][gslot]))
                for dc in (0, 1):
                    em.on("scalar",
                          (lambda s, dc=dc, gc_col=gc_col, gc=gc, gslot=gslot:
                           s.activation(
                               out=gyst[gslot][:, dc * 512:(dc + 1) * 512],
                               in_=DNB[dc][:, 0:512],
                               func=AF.Identity,
                               scale=gc[:, gc_col:gc_col + 1])),
                          wait=(gw if dc == 0 else ()))
                    state["yev"][dc] = em.pc["scalar"].n
                if l0 and hf == 0:
                    em.on("vector",
                          (lambda v, gc_col=gc_col, gslot=gslot:
                           v.scalar_tensor_tensor(
                               out=gyst[gslot][:, 0:D], in0=b2b[:, 0:D],
                               scalar=glp[:, gc_col:gc_col + 1],
                               in1=gyst[gslot][:, 0:D],
                               op0=ALU.mult, op1=ALU.add)),
                          wait=[(spc, max(spc.n, b2_s))])
                    dma_wait = [(vpc, vpc.n)]
                else:
                    dma_wait = [(spc, spc.n)]
                if v_i > 0:
                    dma_wait += [(st_acc[0], state["acc_guard"][0]),
                                 (st_acc[1], state["acc_guard"][1])]
                em.dma("gpsimd",
                       io["gy_acc"][a_loc * 128:(a_loc + 1) * 128, :],
                       gyst[gslot][:, 0:D], st_acc[gslot],
                       wait=dma_wait,
                       **({} if v_i == 0 else {"accum_op": ALU.add}))
                state["gy_last"][gslot] = st_acc[gslot].n
        state["wfree"][wslot] = tup.n
        state["acc_guard"] = (st_acc[0].n, st_acc[1].n)

    wgate = [(tup, tup.n), (tdn, tdn.n)]
    for kk in range(dtile):
        em.dma("gpsimd", upA[kk][:, 0:SPL * 128],
               io["Ws_up"][kk * 128:(kk + 1) * 128, 0:SPL * 128],
               ld_w[0], wait=(wgate if kk == 0 else ()))
        em.dma("gpsimd", upB[kk][:, 0:(fts - SPL) * 128],
               io["Ws_up"][kk * 128:(kk + 1) * 128, SPL * 128:F_s], ld_w[0])
        if not l0:
            em.dma("gpsimd", up2A[kk][:, 0:SPL * 128],
                   io["Ws_up2"][kk * 128:(kk + 1) * 128, 0:SPL * 128], ld_w[0])
            em.dma("gpsimd", up2B[kk][:, 0:(fts - SPL) * 128],
                   io["Ws_up2"][kk * 128:(kk + 1) * 128, SPL * 128:F_s], ld_w[0])
    for f in range(fts):
        dns_, lf = shared_dn(f)
        em.dma("gpsimd", dns_[lf][:, :],
               io["Ws_dn"][f * 128:(f + 1) * 128, :], ld_w[0])
    if l0:
        em.dma("sync", bcols[:, 0:fts], io["bs1_cols"][:, :], ld_bs,
               wait=[(spc, spc.n)])
        bsrow = sb(em, f"{pfx}_bsrow", [1, D], F32)
        em.dma("sync", bsrow[:, :], io["bs2_row"][:, :], ld_bs)
        emit_bcast_into(em, b2b, bsrow[0:1, :], D, ones_s, psum_b, tb_,
                        wait=[(ld_bs, ld_bs.n)])
    wl_sh = (ld_w[0], ld_w[0].n)
    n_bs = ld_bs.n
    bs_done = em.pc["scalar"].n
    state["wl"] = wl_sh
    state["ev"] = {}

    xres = [sb(em, f"{pfx}_xres{j}", [128, D], F32) for j in range(2)]
    gyres = [sb(em, f"{pfx}_gyres{j}", [128, D], F32) for j in range(2)]
    lnscr = sb(em, f"{pfx}_lnscr", [128, D], F32)
    xout_t = sb(em, f"{pfx}_xout", [128, D], F32)

    n_acc = (st_acc[0].n, st_acc[1].n)
    state["acc_total"] = n_acc

    def up_single(ft, kt, lhs_ap, lhs2_ap, slot, xw, par):
        first = kt == 0
        banks = (UPBK[par], None) if l0 else UPPAIR[par]
        w = []
        if first:
            w = [xw, state["wl"]]
            for key in (("u", par), ("u2", par)):
                ev = state["ev"].get(key)
                if ev:
                    w.append(ev)
        em.on("tensor",
              (lambda t, lhs_ap=lhs_ap, banks=banks, slot=slot, first=first,
               kt=kt: t.matmul(
                   out=banks[0][:, 0:TBLK],
                   lhsT=lhs_ap,
                   rhs=xblk[slot][:, kt * TBLK:(kt + 1) * TBLK],
                   start=first, stop=(kt == dtile - 1))),
              wait=w, inc=(tup if (l0 and kt == dtile - 1) else None))
        if not l0:
            em.on("tensor",
                  (lambda t, lhs2_ap=lhs2_ap, banks=banks, slot=slot,
                   first=first, kt=kt: t.matmul(
                       out=banks[1][:, 0:TBLK],
                       lhsT=lhs2_ap,
                       rhs=xblk[slot][:, kt * TBLK:(kt + 1) * TBLK],
                       start=first, stop=(kt == dtile - 1))),
                  inc=(tup if kt == dtile - 1 else None))

    for tb in range(NB):
        slot = tb % 3
        xw = load_xblk(tb, slot)
        for ft in range(fts):
            par = ft % 2
            ups_, lf = shared_up(ft)
            if not l0:
                u2s_, lf2 = shared_up2(ft)
            for kt in range(dtile):
                up_single(ft, kt, ups_[kt][:, lf * 128:(lf + 1) * 128],
                          (u2s_[kt][:, lf2 * 128:(lf2 + 1) * 128]
                           if not l0 else None), slot, xw, par)
            bia = bcols[:, ft:ft + 1] if l0 else None
            chain(ft, par, ft, bia, ld_bs, n_bs, (tup, tup.n))
        state["xb_guard"][slot] = tup.n
        state["act_v"] = vpc.n
        for tt in range(TPB):
            i_loc = tb * TPB + tt
            rslot = i_loc % 2
            down_tile((lambda f: f), tt, fts, shared=True)
            for dc in (0, 1):
                em.on("scalar",
                      (lambda s, dc=dc: s.activation(
                          out=gyst[0][:, dc * 512:(dc + 1) * 512],
                          in_=DNB[dc][:, 0:512],
                          func=AF.Identity, scale=0.5 if l0 else 1.0)),
                      wait=([(tdn, tdn.n), (vpc, vpc.n),
                             (st_acc[0], n_acc[0]), (st_acc[1], n_acc[1])]
                            if dc == 0 else ()))
                state["yev"][dc] = em.pc["scalar"].n
            em.dma("sync", xres[rslot][:, :],
                   io["x1_in"][i_loc * 128:(i_loc + 1) * 128, :],
                   ld_res[rslot], wait=[(st_t, n_xt), (vpc, vpc.n)])
            em.dma("gpsimd", gyres[rslot][:, :],
                   io["gy_acc"][i_loc * 128:(i_loc + 1) * 128, :],
                   ld_gy[rslot],
                   wait=[(st_acc[0], n_acc[0]), (st_acc[1], n_acc[1]),
                         (vpc, vpc.n)])
            if l0:
                em.on("vector", (lambda v: v.tensor_tensor(
                    out=gyst[0][:, 0:D], in0=gyst[0][:, 0:D], in1=b2b[:, 0:D],
                    op=ALU.add)), wait=[(spc, max(spc.n, bs_done))])
            em.on("vector",
                  (lambda v, rslot=rslot: v.tensor_tensor(
                      out=gyst[0][:, 0:D], in0=gyst[0][:, 0:D],
                      in1=gyres[rslot][:, :], op=ALU.add)),
                  wait=[(spc, spc.n), (ld_gy[rslot], ld_gy[rslot].n)])
            em.on("vector",
                  (lambda v, rslot=rslot: v.tensor_tensor(
                      out=gyst[0][:, 0:D], in0=gyst[0][:, 0:D],
                      in1=xres[rslot][:, :], op=ALU.add)),
                  wait=[(ld_res[rslot], ld_res[rslot].n)])
            emit_ln_tile(em, xout_t[:, :], gyst[0][:, 0:D], D, cA, cB, lnscr)
            em.dma("sync",
                   io["x2_out"][i_loc * 128:(i_loc + 1) * 128, :],
                   xout_t[:, :], st_o, wait=[(spc, spc.n)])
            em.on("scalar", (lambda s: s.nop()), wait=[(st_o, st_o.n)])
    return st_o


def np_bf16(x):
    import ml_dtypes
    return np.asarray(x, dtype=ml_dtypes.bfloat16)






def emit_gelu_chain(em, psum_in, dst_bf16_ap, n, h, hsq, b1col, wait_t, ld_c, nldc,
                    war=()):
    spc, vpc = em.pc["scalar"], em.pc["vector"]
    em.on("scalar",
          (lambda s: s.activation(out=h[:, 0:n], in_=psum_in[:, 0:n],
                                  func=AF.Identity, bias=b1col, scale=1.0)),
          wait=[wait_t, (ld_c, nldc), (vpc, vpc.n)] + list(war))
    hfree = spc.n
    em.on("vector", (lambda v: v.tensor_tensor(
        out=hsq[:, 0:n], in0=h[:, 0:n], in1=h[:, 0:n], op=ALU.mult)),
        wait=[(spc, spc.n)])
    em.on("vector", (lambda v: v.tensor_scalar(
        out=hsq[:, 0:n], in0=hsq[:, 0:n], scalar1=GELU_A, scalar2=1.0,
        op0=ALU.mult, op1=ALU.add)))
    em.on("vector", (lambda v: v.tensor_tensor(
        out=hsq[:, 0:n], in0=hsq[:, 0:n], in1=h[:, 0:n], op=ALU.mult)))
    em.on("scalar",
          (lambda s: s.activation(out=hsq[:, 0:n], in_=hsq[:, 0:n], func=AF.Tanh,
                                  scale=GELU_C)),
          wait=[(vpc, vpc.n)])
    em.on("vector", (lambda v: v.scalar_tensor_tensor(
        out=dst_bf16_ap, in0=hsq[:, 0:n], scalar=1.0, in1=h[:, 0:n],
        op0=ALU.add, op1=ALU.mult)),
        wait=[(spc, spc.n)])
    return (hfree, 0)


def emit_glu_chain(em, psum_g, psum_u, dst_bf16_ap, n, h, wait_t, war=()):
    spc, vpc = em.pc["scalar"], em.pc["vector"]
    em.on("scalar",
          (lambda s: s.activation(out=h[:, 0:n], in_=psum_g[:, 0:n], func=AF.Silu)),
          wait=[wait_t, (vpc, vpc.n)] + list(war))
    gfree = spc.n
    em.on("vector", (lambda v: v.tensor_tensor(
        out=dst_bf16_ap, in0=h[:, 0:n], in1=psum_u[:, 0:n], op=ALU.mult)),
        wait=[(spc, spc.n)])
    return (gfree, vpc.n)


def emit_bcast_into(em, dst, src_row_ap, ncols, ones_row, psum_pool, tbc, wait=()):
    spc, vpc = em.pc["scalar"], em.pc["vector"]
    nch = (ncols + 511) // 512
    for c in range(nch):
        c0, c1 = c * 512, min(ncols, (c + 1) * 512)
        em.on("tensor",
              (lambda t, c0=c0, c1=c1: t.matmul(
                  out=psum_pool[:, 0:c1 - c0], lhsT=ones_row[:, 0:128],
                  rhs=src_row_ap[:, c0:c1], start=True, stop=True)),
              wait=(list(wait) + [(spc, spc.n), (vpc, vpc.n)]), inc=tbc)
        em.on("scalar", (lambda s, c0=c0, c1=c1: s.copy(
            out=dst[:, c0:c1], in_=psum_pool[:, 0:c1 - c0])),
            wait=[(tbc, tbc.n)])
    return em.pc["scalar"].n


def emit_moe_ffn(em, g, io, mode):
    Tl, E, D = g["Tl"], g["E"], g["D"]
    F_e, F_s, TBLK = g["F_e"], g["F_s"], g["TBLK"]
    S_tot = g["S_tot"]
    NHALF = g.get("NHALF", 2 if mode == "l0" else 1)
    l0 = mode == "l0"
    pfx = "f0" if l0 else "f1"
    dtile = D // 128
    ntile_l = Tl // 128
    NB = Tl // TBLK
    assert NB % 2 == 0
    TPB = TBLK // 128
    Fh = F_e // NHALF
    fth = Fh // 128
    fts = F_s // 128
    NDC = (D + 511) // 512
    assert NDC == 2
    vpc, spc = em.pc["vector"], em.pc["scalar"]
    SPL = fth if NHALF > 1 else (fts + 1) // 2
    assert SPL >= fth and fts - SPL <= SPL and fts - SPL >= 0
    widthA = max(Fh, SPL * 128)
    widthB = max(Fh, (fts - SPL) * 128)

    upA = [sb(em, f"{pfx}_upA{k}", [128, widthA], BF16) for k in range(dtile)]
    upB = [sb(em, f"{pfx}_upB{k}", [128, widthB], BF16) for k in range(dtile)]
    if not l0:
        up2A = [sb(em, f"{pfx}_u2A{k}", [128, widthA], BF16) for k in range(dtile)]
        up2B = [sb(em, f"{pfx}_u2B{k}", [128, widthB], BF16) for k in range(dtile)]
    dnA = [sb(em, f"{pfx}_dnA{k}", [128, D], BF16) for k in range(SPL)]
    dnB = [sb(em, f"{pfx}_dnB{k}", [128, D], BF16)
           for k in range(max(fth, fts - SPL))]
    hT = sb(em, f"{pfx}_hT", [128, max(2 * fth, fts) * TBLK], BF16)
    xblk = [sb(em, f"{pfx}_xb{j}", [128, dtile * TBLK], BF16) for j in range(3)]
    h_scr = sb(em, f"{pfx}_hscr", [128, max(TBLK, 512)], BF16)
    q_scr = sb(em, f"{pfx}_qscr", [128, max(TBLK, 512)], BF16)
    gyst = [sb(em, f"{pfx}_gyst{j}", [128, D], F32) for j in range(2)]
    glp = sb(em, f"{pfx}_glp", [128, ntile_l * E], F32)
    glp_h = sb(em, f"{pfx}_glph", [128, ntile_l * E], F32) if l0 else None
    bcols = sb(em, f"{pfx}_b1c", [128, max(E * F_e, F_s) // 128], F32) if l0 else None
    b2b = sb(em, f"{pfx}_b2b", [128, D], F32) if l0 else None
    ones_s = sb(em, f"{pfx}_ones", [1, 128], F32)
    onesc_s = sb(em, f"{pfx}_onesc", [128, 1], F32)
    idn_s = sb(em, f"{pfx}_idn", [128, 128], BF16)
    cA = sb(em, f"{pfx}_cA", [128, 1], F32)
    cB = sb(em, f"{pfx}_cB", [128, 1], F32)
    cC = sb(em, f"{pfx}_cC", [128, 1], F32)
    cD = sb(em, f"{pfx}_cD", [128, 1], F32)
    cE = sb(em, f"{pfx}_cE", [128, 1], F32)
    c11 = sb(em, f"{pfx}_c11", [1, 1], F32)
    b2row_s = sb(em, f"{pfx}_b2row", [1, D], F32) if l0 else None

    psum_h = ps(em, f"{pfx}_ph", [128, 512], F32)
    psum_h2 = ps(em, f"{pfx}_ph2", [128, 512], F32)
    psum_y = [ps(em, f"{pfx}_py{j}", [128, 512], F32) for j in range(2)]
    psum_b = ps(em, f"{pfx}_pb", [128, 512], F32)
    psum_tb = [ps(em, f"{pfx}_pt{j}", [128, 1024], BF16) for j in range(2)]
    psum_x = ps(em, f"{pfx}_px", [128, 512], F32)
    if l0:
        UPBK = [psum_h, psum_h2]
        DNB = [psum_y[0], psum_y[1]]
    else:
        UPPAIR = [(psum_h, psum_h2), (psum_y[0], psum_y[1])]
        DNB = [psum_b, psum_x]

    ld_c = em.ctr(f"{pfx}_ldc")
    ld_w = [em.ctr(f"{pfx}_ldw0"), em.ctr(f"{pfx}_ldw1")]
    ld_xb = [em.ctr(f"{pfx}_lxb0"), em.ctr(f"{pfx}_lxb1"), em.ctr(f"{pfx}_lxb2")]
    ld_xt = [em.ctr(f"{pfx}_lxt0"), em.ctr(f"{pfx}_lxt1")]
    ld_res = [em.ctr(f"{pfx}_lr0"), em.ctr(f"{pfx}_lr1")]
    ld_gy = [em.ctr(f"{pfx}_lg0"), em.ctr(f"{pfx}_lg1")]
    ld_sc = em.ctr(f"{pfx}_lsc")
    ld_bs = em.ctr(f"{pfx}_lbs")
    ld_b2 = em.ctr(f"{pfx}_lb2")
    st_t = em.ctr(f"{pfx}_stt")
    st_acc = [em.ctr(f"{pfx}_sac0"), em.ctr(f"{pfx}_sac1")]
    st_o = em.ctr(f"{pfx}_sto")
    tx = em.ctr(f"{pfx}_tx")
    tup = em.ctr(f"{pfx}_tup")
    tdn = em.ctr(f"{pfx}_tdn")
    tb_ = em.ctr(f"{pfx}_tb")

    em.dma("sync", ones_s[:, :], io["ones_f32"][:, :], ld_c)
    em.dma("sync", onesc_s[:, :], io["ones_col"][:, :], ld_c)
    em.dma("sync", idn_s[:, :], io["idn"][:, :], ld_c)
    if l0:
        em.dma("sync", bcols[:, 0:E * F_e // 128], io["b1_all_cols"][:, :], ld_c)
    n_const0 = ld_c.n

    xld = [hT[:, j * 1024:(j + 1) * 1024] for j in range(2)]
    xtc = [hT[:, (2 + j) * 1024:(3 + j) * 1024] for j in range(2)]
    xt_last = [0, 0]
    xt_uses = [0, 0]
    for i in range(ntile_l):
        slot = i % 2
        em.dma("gpsimd", xld[slot],
               io["x1_in"][i * 128:(i + 1) * 128, :], ld_xt[slot],
               wait=[(tx, xt_last[slot])] if xt_last[slot] else ())
        xt_uses[slot] += 1
        for kk in range(dtile):
            pslot = kk % 2
            w0 = [(spc, spc.n)]
            if kk == 0:
                w0 += [(ld_xt[slot], xt_uses[slot] * 16), (ld_c, n_const0)]
                if i >= 2:
                    w0.append((st_t, (i - 1) * 16))
            em.on("tensor",
                  (lambda t, slot=slot, kk=kk, pslot=pslot: t.transpose(
                      out=psum_tb[pslot][:, 0:128],
                      in_=xld[slot][:, kk * 128:(kk + 1) * 128],
                      identity=idn_s[:, :])),
                  wait=w0, inc=tx)
            em.on("scalar",
                  (lambda s, kk=kk, slot=slot, pslot=pslot: s.copy(
                      out=xtc[slot][:, kk * 128:(kk + 1) * 128],
                      in_=psum_tb[pslot][:, 0:128])),
                  wait=[(tx, tx.n)])
        xt_last[slot] = tx.n
        em.dma("sync",
               io["x1T_d"].rearrange("(dt p) t -> p dt t", p=128)[
                   :, :, i * 128:(i + 1) * 128],
               xtc[slot].rearrange("p (dt c) -> p dt c", dt=dtile),
               st_t, wait=[(spc, spc.n)])
    n_xt = st_t.n

    em.dma("sync", glp[:, :], io["g_loc_packed"][:, :], ld_sc)
    if l0:
        esB = ExitStack()
        PCK = S_tot * E // 128
        sc_pack = esB.enter_context(em.nc.sbuf_tensor(f"{pfx}_scp", [128, PCK], F32))
        msk_p = esB.enter_context(em.nc.sbuf_tensor(f"{pfx}_mskp", [128, PCK], F32))
        em.dma("sync", sc_pack[:, :], io["sc_packed"][:, :], ld_sc)
        KCAP = float(g["CAP"] * S_tot)
        lo, hi, mid, cnt = cB, cC, cD, cE
        em.on("vector", (lambda v: v.memset(lo[:], 0.0)),
              wait=[(ld_sc, 32), (ld_c, n_const0)])
        em.on("vector", (lambda v: v.memset(hi[:], 2.0)))
        for it in range(26):
            em.on("vector", (lambda v: v.tensor_tensor(
                out=mid[:], in0=lo[:], in1=hi[:], op=ALU.add)))
            em.on("vector", (lambda v: v.tensor_scalar(
                out=mid[:], in0=mid[:], scalar1=0.5, scalar2=None, op0=ALU.mult)))
            em.on("vector", (lambda v: v.tensor_scalar(
                out=msk_p[:, :], in0=sc_pack[:, :], scalar1=mid[:], scalar2=None,
                op0=ALU.is_ge)))
            em.on("vector", (lambda v: v.reduce_sum(
                out=cnt[:], in_=msk_p[:, :], axis=AX.X)))
            em.on("tensor",
                  (lambda t: t.matmul(out=psum_b[0:1, 0:1], lhsT=onesc_s[:, :],
                                      rhs=cnt[:, :], start=True, stop=True)),
                  wait=[(vpc, vpc.n), (spc, spc.n)], inc=tb_)
            em.on("scalar", (lambda s: s.copy(out=c11[:, :], in_=psum_b[0:1, 0:1])),
                  wait=[(tb_, tb_.n)])
            em.on("tensor",
                  (lambda t: t.matmul(out=psum_b[:, 1:2], lhsT=ones_s[:, :],
                                      rhs=c11[:, :], start=True, stop=True)),
                  wait=[(spc, spc.n)], inc=tb_)
            em.on("vector", (lambda v: v.tensor_copy(out=cnt[:], in_=psum_b[:, 1:2])),
                  wait=[(tb_, tb_.n)])
            em.on("vector", (lambda v: v.tensor_scalar(
                out=cnt[:], in0=cnt[:], scalar1=KCAP, scalar2=None, op0=ALU.is_ge)))
            em.on("vector", (lambda v: v.tensor_tensor(
                out=cA[:], in0=mid[:], in1=lo[:], op=ALU.subtract)))
            em.on("vector", (lambda v: v.tensor_tensor(
                out=cA[:], in0=cA[:], in1=cnt[:], op=ALU.mult)))
            em.on("vector", (lambda v: v.tensor_tensor(
                out=lo[:], in0=lo[:], in1=cA[:], op=ALU.add)))
            em.on("vector", (lambda v: v.tensor_tensor(
                out=cA[:], in0=hi[:], in1=mid[:], op=ALU.subtract)))
            em.on("vector", (lambda v: v.tensor_tensor(
                out=cA[:], in0=cA[:], in1=cnt[:], op=ALU.mult)))
            em.on("vector", (lambda v: v.tensor_tensor(
                out=hi[:], in0=mid[:], in1=cA[:], op=ALU.add)))
        em.on("vector", (lambda v: v.tensor_scalar(
            out=msk_p[:, 0:ntile_l * E], in0=glp[:, :], scalar1=lo[:],
            scalar2=None, op0=ALU.is_ge)))
        em.on("vector", (lambda v: v.tensor_tensor(
            out=glp[:, :], in0=glp[:, :], in1=msk_p[:, 0:ntile_l * E],
            op=ALU.mult)))
        em.on("vector", (lambda v: v.tensor_scalar(
            out=glp_h[:, :], in0=glp[:, :], scalar1=0.5, scalar2=None,
            op0=ALU.mult)))
        esB.close()
    else:
        em.on("vector", (lambda v: v.memset(cA[:], 0.0)),
              wait=[(ld_sc, 16), (ld_c, n_const0)])
    gate_v = vpc.n

    state = dict(gy_i=0, gy_last=[0, 0], xb_guard=[0, 0, 0], xb_n=[0, 0, 0],
                 ev={}, yev=[0, 0], wfree=[0, 0], acc_guard=(0, 0),
                 act_v=0)

    def shared_up(ft):
        return (upA, ft) if ft < SPL else (upB, ft - SPL)

    def shared_up2(ft):
        return (up2A, ft) if ft < SPL else (up2B, ft - SPL)

    def shared_dn(ft):
        return (dnA, ft) if ft < SPL else (dnB, ft - SPL)

    def load_xblk(tb, slot, extra=()):
        src = io["x1T_d"].rearrange("(dt p) t -> p dt t", p=128)[
            :, :, tb * TBLK:(tb + 1) * TBLK]
        w = [(st_t, n_xt)] + list(extra)
        if state["xb_guard"][slot]:
            w.append((tup, state["xb_guard"][slot]))
        em.dma("gpsimd",
               xblk[slot].rearrange("p (dt c) -> p dt c", dt=dtile),
               src, ld_xb[slot], wait=w)
        state["xb_n"][slot] += 1
        return (ld_xb[slot], state["xb_n"][slot] * 16)

    def up_pair_mms(ft, kt, lhs_ap, lhs2_ap, slots, xw):
        first = kt == 0
        for bi in (0, 1):
            banks = (UPBK[bi], None) if l0 else UPPAIR[bi]
            w = []
            if first:
                if bi == 0:
                    w = [xw[0], xw[1], state["wl"]]
                for key in (("u", bi), ("u2", bi)):
                    ev = state["ev"].get(key)
                    if ev:
                        w.append(ev)
            em.on("tensor",
                  (lambda t, lhs_ap=lhs_ap, banks=banks, s=slots[bi],
                   first=first, kt=kt: t.matmul(
                       out=banks[0][:, 0:TBLK],
                       lhsT=lhs_ap,
                       rhs=xblk[s][:, kt * TBLK:(kt + 1) * TBLK],
                       start=first, stop=(kt == dtile - 1))),
                  wait=w,
                  inc=(tup if (l0 and kt == dtile - 1) else None))
        if not l0:
            for bi in (0, 1):
                banks = UPPAIR[bi]
                em.on("tensor",
                      (lambda t, lhs2_ap=lhs2_ap, banks=banks, s=slots[bi],
                       first=first, kt=kt: t.matmul(
                           out=banks[1][:, 0:TBLK],
                           lhsT=lhs2_ap,
                           rhs=xblk[s][:, kt * TBLK:(kt + 1) * TBLK],
                           start=first, stop=(kt == dtile - 1))),
                      inc=(tup if kt == dtile - 1 else None))

    def chain(ft, bi, col, bias_ap, bias_ctr, bias_n, wait_t):
        dst = hT[:, col * TBLK:(col + 1) * TBLK]
        if l0:
            bank = UPBK[bi]
            free = emit_gelu_chain(em, bank, dst, TBLK, h_scr, q_scr,
                                   bias_ap, wait_t, bias_ctr, bias_n)
            state["ev"][("u", bi)] = (spc, free[0])
        else:
            banks = UPPAIR[bi]
            free = emit_glu_chain(em, banks[0], banks[1], dst, TBLK, h_scr,
                                  wait_t)
            state["ev"][("u", bi)] = (spc, free[0])
            state["ev"][("u2", bi)] = (vpc, free[1])

    def down_tile(col_of, tglob, ft_n, shared, e=None, v_i=None):
        dw = [(vpc, state["act_v"])]
        for dcp in (0, 1):
            if state["yev"][dcp]:
                dw.append((spc, state["yev"][dcp]))
        for f in range(ft_n):
            if shared:
                dns_, lf = shared_dn(f)
                lhs = dns_[lf]
            else:
                lhs = None
            for dc in (0, 1):
                n0, n1 = dc * 512, (dc + 1) * 512
                em.on("tensor",
                      (lambda t, f=f, dc=dc, n0=n0, n1=n1, tglob=tglob,
                       col_of=col_of, shared=shared: t.matmul(
                           out=DNB[dc][:, 0:512],
                           lhsT=hT[:, col_of(f) * TBLK +
                                   (tglob % TPB) * 128:
                                   col_of(f) * TBLK + (tglob % TPB) * 128 + 128],
                           rhs=(shared_dn(f)[0][shared_dn(f)[1]] if shared
                                else state["dns"][f])[:, n0:n1],
                           start=(f == 0), stop=(f == ft_n - 1))),
                      wait=(dw if (f == 0 and dc == 0) else ()),
                      inc=tdn if (f == ft_n - 1 and dc == 1) else None)

    NV = E * NHALF
    for v_i in range(NV):
        e = v_i // NHALF
        hf = v_i % NHALF
        wslot = v_i % 2
        ups = upA if wslot == 0 else upB
        up2s = (up2A if wslot == 0 else up2B) if not l0 else None
        dns = dnA if wslot == 0 else dnB
        state["dns"] = dns
        wwait = ([(tup, state["wfree"][wslot]), (tdn, tdn.n)]
                 if state["wfree"][wslot] else [])
        for kk in range(dtile):
            em.dma("gpsimd", ups[kk][:, 0:Fh],
                   io["W_up_all"][e, kk * 128:(kk + 1) * 128,
                                  hf * Fh:(hf + 1) * Fh],
                   ld_w[wslot], wait=(wwait if kk == 0 else ()))
            if not l0:
                em.dma("gpsimd", up2s[kk][:, 0:Fh],
                       io["W_up2_all"][e, kk * 128:(kk + 1) * 128,
                                       hf * Fh:(hf + 1) * Fh],
                       ld_w[wslot])
        for f in range(fth):
            em.dma("gpsimd", dns[f][:, :],
                   io["W_dn_all"][e, hf * Fh + f * 128: hf * Fh + (f + 1) * 128, :],
                   ld_w[wslot])
        state["wl"] = (ld_w[wslot], ld_w[wslot].n)
        if l0 and hf == 0:
            em.dma("sync", b2row_s[:, :], io["b2_all"][e:e + 1, :], ld_b2,
                   wait=[(tb_, tb_.n)])
            emit_bcast_into(em, b2b, b2row_s[0:1, :], D, ones_s, psum_b, tb_,
                            wait=[(ld_c, n_const0), (ld_b2, ld_b2.n)])
        b2_s = em.pc["scalar"].n

        for prx in range(NB // 2):
            tb0, tb1 = 2 * prx, 2 * prx + 1
            s0 = (2 * prx) % 3
            s1 = (2 * prx + 1) % 3
            xw0 = load_xblk(tb0, s0)
            xw1 = load_xblk(tb1, s1)
            for ft in range(fth):
                for kt in range(dtile):
                    up_pair_mms(ft, kt,
                                ups[kt][:, ft * 128:(ft + 1) * 128],
                                (up2s[kt][:, ft * 128:(ft + 1) * 128]
                                 if not l0 else None),
                                (s0, s1), (xw0, xw1))
                bia = (bcols[:, e * (F_e // 128) + hf * fth + ft:
                             e * (F_e // 128) + hf * fth + ft + 1]
                       if l0 else None)
                chain(ft, 0, ft * 2 + 0, bia, ld_c, n_const0, (tup, tup.n))
                chain(ft, 1, ft * 2 + 1, bia, ld_c, n_const0, (tup, tup.n))
            state["xb_guard"][s0] = tup.n
            state["xb_guard"][s1] = tup.n
            state["act_v"] = vpc.n
            for tt in range(2 * TPB):
                blk = tt // TPB
                a_loc = (tb0 + blk) * TPB + (tt % TPB)
                gc_col = a_loc * E + e
                state["gy_i"] ^= 1
                gslot = state["gy_i"]
                down_tile((lambda f, blk=blk: f * 2 + blk), tt, fth,
                          shared=False, e=e, v_i=v_i)
                gc = glp_h if l0 else glp
                gw = [(tdn, tdn.n), (vpc, max(gate_v, vpc.n))]
                if state["gy_last"][gslot]:
                    gw.append((st_acc[gslot], state["gy_last"][gslot]))
                for dc in (0, 1):
                    em.on("scalar",
                          (lambda s, dc=dc, gc_col=gc_col, gc=gc, gslot=gslot:
                           s.activation(
                               out=gyst[gslot][:, dc * 512:(dc + 1) * 512],
                               in_=DNB[dc][:, 0:512],
                               func=AF.Identity,
                               scale=gc[:, gc_col:gc_col + 1])),
                          wait=(gw if dc == 0 else ()))
                    state["yev"][dc] = em.pc["scalar"].n
                if l0 and hf == 0:
                    em.on("vector",
                          (lambda v, gc_col=gc_col, gslot=gslot:
                           v.scalar_tensor_tensor(
                               out=gyst[gslot][:, 0:D], in0=b2b[:, 0:D],
                               scalar=glp[:, gc_col:gc_col + 1],
                               in1=gyst[gslot][:, 0:D],
                               op0=ALU.mult, op1=ALU.add)),
                          wait=[(spc, max(spc.n, b2_s))])
                    dma_wait = [(vpc, vpc.n)]
                else:
                    dma_wait = [(spc, spc.n)]
                if v_i > 0:
                    dma_wait += [(st_acc[0], state["acc_guard"][0]),
                                 (st_acc[1], state["acc_guard"][1])]
                em.dma("gpsimd",
                       io["gy_acc"][a_loc * 128:(a_loc + 1) * 128, :],
                       gyst[gslot][:, 0:D], st_acc[gslot],
                       wait=dma_wait,
                       **({} if v_i == 0 else {"accum_op": ALU.add}))
                state["gy_last"][gslot] = st_acc[gslot].n
        state["wfree"][wslot] = tup.n
        state["acc_guard"] = (st_acc[0].n, st_acc[1].n)

    wgate = [(tup, tup.n), (tdn, tdn.n)]
    for kk in range(dtile):
        em.dma("gpsimd", upA[kk][:, 0:SPL * 128],
               io["Ws_up"][kk * 128:(kk + 1) * 128, 0:SPL * 128],
               ld_w[0], wait=(wgate if kk == 0 else ()))
        em.dma("gpsimd", upB[kk][:, 0:(fts - SPL) * 128],
               io["Ws_up"][kk * 128:(kk + 1) * 128, SPL * 128:F_s], ld_w[0])
        if not l0:
            em.dma("gpsimd", up2A[kk][:, 0:SPL * 128],
                   io["Ws_up2"][kk * 128:(kk + 1) * 128, 0:SPL * 128], ld_w[0])
            em.dma("gpsimd", up2B[kk][:, 0:(fts - SPL) * 128],
                   io["Ws_up2"][kk * 128:(kk + 1) * 128, SPL * 128:F_s], ld_w[0])
    for f in range(fts):
        dns_, lf = shared_dn(f)
        em.dma("gpsimd", dns_[lf][:, :],
               io["Ws_dn"][f * 128:(f + 1) * 128, :], ld_w[0])
    if l0:
        em.dma("sync", bcols[:, 0:fts], io["bs1_cols"][:, :], ld_bs,
               wait=[(spc, spc.n)])
        bsrow = sb(em, f"{pfx}_bsrow", [1, D], F32)
        em.dma("sync", bsrow[:, :], io["bs2_row"][:, :], ld_bs)
        emit_bcast_into(em, b2b, bsrow[0:1, :], D, ones_s, psum_b, tb_,
                        wait=[(ld_bs, ld_bs.n)])
    wl_sh = (ld_w[0], ld_w[0].n)
    n_bs = ld_bs.n
    bs_done = em.pc["scalar"].n
    state["wl"] = wl_sh
    state["ev"] = {}

    xres = [sb(em, f"{pfx}_xres{j}", [128, D], F32) for j in range(2)]
    gyres = [sb(em, f"{pfx}_gyres{j}", [128, D], F32) for j in range(2)]
    lnscr = sb(em, f"{pfx}_lnscr", [128, D], F32)
    xout_t = sb(em, f"{pfx}_xout", [128, D], F32)

    n_acc = (st_acc[0].n, st_acc[1].n)
    state["acc_total"] = n_acc

    for prx in range(NB // 2):
        tb0 = 2 * prx
        s0, s1 = (2 * prx) % 3, (2 * prx + 1) % 3
        xw0 = load_xblk(tb0, s0)
        xw1 = load_xblk(tb0 + 1, s1)
        for ft in range(fts):
            ups_, lf = shared_up(ft)
            if not l0:
                u2s_, lf2 = shared_up2(ft)
            for kt in range(dtile):
                up_pair_mms(ft, kt,
                            ups_[kt][:, lf * 128:(lf + 1) * 128],
                            (u2s_[kt][:, lf2 * 128:(lf2 + 1) * 128]
                             if not l0 else None),
                            (s0, s1), (xw0, xw1))
            bia = bcols[:, ft:ft + 1] if l0 else None
            chain(ft, 0, (ft * 2) % fts, bia, ld_bs, n_bs, (tup, tup.n))
            chain(ft, 1, (ft * 2 + 1) % fts, bia, ld_bs, n_bs, (tup, tup.n))
        state["xb_guard"][s0] = tup.n
        state["xb_guard"][s1] = tup.n
        state["act_v"] = vpc.n
        for tt in range(2 * TPB):
            blk = tt // TPB
            i_loc = (tb0 + blk) * TPB + (tt % TPB)
            rslot = i_loc % 2
            down_tile((lambda f, blk=blk: (f * 2 + blk) % fts), tt, fts,
                      shared=True)
            for dc in (0, 1):
                em.on("scalar",
                      (lambda s, dc=dc: s.activation(
                          out=gyst[0][:, dc * 512:(dc + 1) * 512],
                          in_=DNB[dc][:, 0:512],
                          func=AF.Identity, scale=0.5 if l0 else 1.0)),
                      wait=([(tdn, tdn.n), (vpc, vpc.n),
                             (st_acc[0], n_acc[0]), (st_acc[1], n_acc[1])]
                            if dc == 0 else ()))
                state["yev"][dc] = em.pc["scalar"].n
            em.dma("sync", xres[rslot][:, :],
                   io["x1_in"][i_loc * 128:(i_loc + 1) * 128, :],
                   ld_res[rslot], wait=[(st_t, n_xt), (vpc, vpc.n)])
            em.dma("gpsimd", gyres[rslot][:, :],
                   io["gy_acc"][i_loc * 128:(i_loc + 1) * 128, :],
                   ld_gy[rslot],
                   wait=[(st_acc[0], n_acc[0]), (st_acc[1], n_acc[1]),
                         (vpc, vpc.n)])
            if l0:
                em.on("vector", (lambda v: v.tensor_tensor(
                    out=gyst[0][:, 0:D], in0=gyst[0][:, 0:D], in1=b2b[:, 0:D],
                    op=ALU.add)), wait=[(spc, max(spc.n, bs_done))])
            em.on("vector",
                  (lambda v, rslot=rslot: v.tensor_tensor(
                      out=gyst[0][:, 0:D], in0=gyst[0][:, 0:D],
                      in1=gyres[rslot][:, :], op=ALU.add)),
                  wait=[(spc, spc.n), (ld_gy[rslot], ld_gy[rslot].n)])
            em.on("vector",
                  (lambda v, rslot=rslot: v.tensor_tensor(
                      out=gyst[0][:, 0:D], in0=gyst[0][:, 0:D],
                      in1=xres[rslot][:, :], op=ALU.add)),
                  wait=[(ld_res[rslot], ld_res[rslot].n)])
            emit_ln_tile(em, xout_t[:, :], gyst[0][:, 0:D], D, cA, cB, lnscr)
            em.dma("sync",
                   io["x2_out"][i_loc * 128:(i_loc + 1) * 128, :],
                   xout_t[:, :], st_o, wait=[(spc, spc.n)])
            em.on("scalar", (lambda s: s.nop()), wait=[(st_o, st_o.n)])
    return st_o


def np_bf16(x):
    import ml_dtypes
    return np.asarray(x, dtype=ml_dtypes.bfloat16)






def emit_gelu_chain(em, psum_in, dst_bf16_ap, n, h, hsq, b1col, wait_t, ld_c, nldc,
                    war=()):
    spc, vpc = em.pc["scalar"], em.pc["vector"]
    em.on("scalar",
          (lambda s: s.activation(out=h[:, 0:n], in_=psum_in[:, 0:n],
                                  func=AF.Identity, bias=b1col, scale=1.0)),
          wait=[wait_t, (ld_c, nldc), (vpc, vpc.n)] + list(war))
    hfree = spc.n
    em.on("vector", (lambda v: v.tensor_tensor(
        out=hsq[:, 0:n], in0=h[:, 0:n], in1=h[:, 0:n], op=ALU.mult)),
        wait=[(spc, spc.n)])
    em.on("vector", (lambda v: v.tensor_scalar(
        out=hsq[:, 0:n], in0=hsq[:, 0:n], scalar1=GELU_A, scalar2=1.0,
        op0=ALU.mult, op1=ALU.add)))
    em.on("vector", (lambda v: v.tensor_tensor(
        out=hsq[:, 0:n], in0=hsq[:, 0:n], in1=h[:, 0:n], op=ALU.mult)))
    em.on("scalar",
          (lambda s: s.activation(out=hsq[:, 0:n], in_=hsq[:, 0:n], func=AF.Tanh,
                                  scale=GELU_C)),
          wait=[(vpc, vpc.n)])
    em.on("vector", (lambda v: v.scalar_tensor_tensor(
        out=dst_bf16_ap, in0=hsq[:, 0:n], scalar=1.0, in1=h[:, 0:n],
        op0=ALU.add, op1=ALU.mult)),
        wait=[(spc, spc.n)])
    return (hfree, 0)


def emit_glu_chain(em, psum_g, psum_u, dst_bf16_ap, n, h, wait_t, war=()):
    spc, vpc = em.pc["scalar"], em.pc["vector"]
    em.on("scalar",
          (lambda s: s.activation(out=h[:, 0:n], in_=psum_g[:, 0:n], func=AF.Silu)),
          wait=[wait_t, (vpc, vpc.n)] + list(war))
    gfree = spc.n
    em.on("vector", (lambda v: v.tensor_tensor(
        out=dst_bf16_ap, in0=h[:, 0:n], in1=psum_u[:, 0:n], op=ALU.mult)),
        wait=[(spc, spc.n)])
    return (gfree, vpc.n)


def emit_bcast_into(em, dst, src_row_ap, ncols, ones_row, psum_pool, tbc, wait=()):
    spc, vpc = em.pc["scalar"], em.pc["vector"]
    nch = (ncols + 511) // 512
    for c in range(nch):
        c0, c1 = c * 512, min(ncols, (c + 1) * 512)
        em.on("tensor",
              (lambda t, c0=c0, c1=c1: t.matmul(
                  out=psum_pool[:, 0:c1 - c0], lhsT=ones_row[:, 0:128],
                  rhs=src_row_ap[:, c0:c1], start=True, stop=True)),
              wait=(list(wait) + [(spc, spc.n), (vpc, vpc.n)]), inc=tbc)
        em.on("scalar", (lambda s, c0=c0, c1=c1: s.copy(
            out=dst[:, c0:c1], in_=psum_pool[:, 0:c1 - c0])),
            wait=[(tbc, tbc.n)])
    return em.pc["scalar"].n


def emit_moe_ffn(em, g, io, mode):
    Tl, E, D = g["Tl"], g["E"], g["D"]
    F_e, F_s, TBLK = g["F_e"], g["F_s"], g["TBLK"]
    S_tot = g["S_tot"]
    NHALF = g.get("NHALF", 2 if mode == "l0" else 1)
    l0 = mode == "l0"
    pfx = "f0" if l0 else "f1"
    dtile = D // 128
    ntile_l = Tl // 128
    NB = Tl // TBLK
    TPB = TBLK // 128
    Fh = F_e // NHALF
    fth = Fh // 128
    fts = F_s // 128
    NDC = (D + 511) // 512
    vpc, spc = em.pc["vector"], em.pc["scalar"]
    SPL = fth if NHALF > 1 else (fts + 1) // 2
    assert SPL >= fth and fts - SPL <= SPL and fts - SPL >= 0
    widthA = max(Fh, SPL * 128)
    widthB = max(Fh, (fts - SPL) * 128)

    upA = [sb(em, f"{pfx}_upA{k}", [128, widthA], BF16) for k in range(dtile)]
    upB = [sb(em, f"{pfx}_upB{k}", [128, widthB], BF16) for k in range(dtile)]
    if not l0:
        up2A = [sb(em, f"{pfx}_u2A{k}", [128, widthA], BF16) for k in range(dtile)]
        up2B = [sb(em, f"{pfx}_u2B{k}", [128, widthB], BF16) for k in range(dtile)]
    dnA = [sb(em, f"{pfx}_dnA{k}", [128, D], BF16) for k in range(SPL)]
    dnB = [sb(em, f"{pfx}_dnB{k}", [128, D], BF16)
           for k in range(max(fth, fts - SPL))]
    hT = sb(em, f"{pfx}_hT", [128, fts * TBLK], BF16)
    xblk = [sb(em, f"{pfx}_xb{j}", [128, dtile * TBLK], BF16) for j in range(2)]
    h_scr = sb(em, f"{pfx}_hscr", [128, max(TBLK, 512)], BF16)
    q_scr = sb(em, f"{pfx}_qscr", [128, max(TBLK, 512)], BF16)
    gyst = [sb(em, f"{pfx}_gyst{j}", [128, D], F32) for j in range(2)]
    glp = sb(em, f"{pfx}_glp", [128, ntile_l * E], F32)
    glp_h = sb(em, f"{pfx}_glph", [128, ntile_l * E], F32) if l0 else None
    bcols = sb(em, f"{pfx}_b1c", [128, max(E * F_e, F_s) // 128], F32) if l0 else None
    b2b = sb(em, f"{pfx}_b2b", [128, D], F32) if l0 else None
    ones_s = sb(em, f"{pfx}_ones", [1, 128], F32)
    onesc_s = sb(em, f"{pfx}_onesc", [128, 1], F32)
    idn_s = sb(em, f"{pfx}_idn", [128, 128], BF16)
    cA = sb(em, f"{pfx}_cA", [128, 1], F32)
    cB = sb(em, f"{pfx}_cB", [128, 1], F32)
    cC = sb(em, f"{pfx}_cC", [128, 1], F32)
    cD = sb(em, f"{pfx}_cD", [128, 1], F32)
    cE = sb(em, f"{pfx}_cE", [128, 1], F32)
    c11 = sb(em, f"{pfx}_c11", [1, 1], F32)
    b2row_s = sb(em, f"{pfx}_b2row", [1, D], F32) if l0 else None

    psum_h = ps(em, f"{pfx}_ph", [128, 512], F32)
    psum_h2 = ps(em, f"{pfx}_ph2", [128, 512], F32)
    psum_y = [ps(em, f"{pfx}_py{j}", [128, 512], F32) for j in range(2)]
    psum_b = ps(em, f"{pfx}_pb", [128, 512], F32)
    psum_tb = [ps(em, f"{pfx}_pt{j}", [128, 1024], BF16) for j in range(2)]
    if l0:
        UPB = [(psum_h, None), (psum_h2, None)]
        DNB = [psum_y[0], psum_y[1]]
        psum_x = ps(em, f"{pfx}_px", [128, 512], F32)
    else:
        UPB = [(psum_h, psum_h2), (psum_y[0], psum_y[1])]
        psum_x = ps(em, f"{pfx}_px", [128, 512], F32)
        DNB = [psum_b, psum_x]

    ld_c = em.ctr(f"{pfx}_ldc")
    ld_w = [em.ctr(f"{pfx}_ldw0"), em.ctr(f"{pfx}_ldw1")]
    ld_xb = [em.ctr(f"{pfx}_lxb0"), em.ctr(f"{pfx}_lxb1")]
    ld_xt = [em.ctr(f"{pfx}_lxt0"), em.ctr(f"{pfx}_lxt1")]
    ld_res = [em.ctr(f"{pfx}_lr0"), em.ctr(f"{pfx}_lr1")]
    ld_gy = [em.ctr(f"{pfx}_lg0"), em.ctr(f"{pfx}_lg1")]
    ld_sc = em.ctr(f"{pfx}_lsc")
    ld_bs = em.ctr(f"{pfx}_lbs")
    st_t = em.ctr(f"{pfx}_stt")
    st_acc = [em.ctr(f"{pfx}_sac0"), em.ctr(f"{pfx}_sac1")]
    st_o = em.ctr(f"{pfx}_sto")
    tx = em.ctr(f"{pfx}_tx")
    tup = em.ctr(f"{pfx}_tup")
    tdn = em.ctr(f"{pfx}_tdn")
    tb_ = em.ctr(f"{pfx}_tb")
    ld_b2 = em.ctr(f"{pfx}_lb2")

    em.dma("sync", ones_s[:, :], io["ones_f32"][:, :], ld_c)
    em.dma("sync", onesc_s[:, :], io["ones_col"][:, :], ld_c)
    em.dma("sync", idn_s[:, :], io["idn"][:, :], ld_c)
    if l0:
        em.dma("sync", bcols[:, 0:E * F_e // 128], io["b1_all_cols"][:, :], ld_c)
    n_const0 = ld_c.n

    esA = ExitStack()
    xld = [esA.enter_context(em.nc.sbuf_tensor(f"{pfx}_xld{j}", [128, D], BF16))
           for j in range(2)]
    xtc = [esA.enter_context(em.nc.sbuf_tensor(f"{pfx}_xtc{j}", [128, dtile * 128],
                                               BF16)) for j in range(2)]
    xt_last = [0, 0]
    xt_uses = [0, 0]
    for i in range(ntile_l):
        slot = i % 2
        em.dma("gpsimd", xld[slot][:, :],
               io["x1_in"][i * 128:(i + 1) * 128, :], ld_xt[slot],
               wait=[(tx, xt_last[slot])] if xt_last[slot] else ())
        xt_uses[slot] += 1
        for k in range(dtile):
            pslot = k % 2
            w0 = [(spc, spc.n)]
            if k == 0:
                w0 += [(ld_xt[slot], xt_uses[slot] * 16), (ld_c, n_const0)]
                if i >= 2:
                    w0.append((st_t, (i - 1) * 16))
            em.on("tensor",
                  (lambda t, slot=slot, k=k, pslot=pslot: t.transpose(
                      out=psum_tb[pslot][:, 0:128],
                      in_=xld[slot][:, k * 128:(k + 1) * 128],
                      identity=idn_s[:, :])),
                  wait=w0, inc=tx)
            em.on("scalar",
                  (lambda s, k=k, slot=slot, pslot=pslot: s.copy(
                      out=xtc[slot][:, k * 128:(k + 1) * 128],
                      in_=psum_tb[pslot][:, 0:128])),
                  wait=[(tx, tx.n)])
        xt_last[slot] = tx.n
        em.dma("sync",
               io["x1T_d"].rearrange("(dt p) t -> p dt t", p=128)[
                   :, :, i * 128:(i + 1) * 128],
               xtc[slot].rearrange("p (dt c) -> p dt c", dt=dtile),
               st_t, wait=[(spc, spc.n)])
    n_xt = st_t.n

    em.dma("sync", glp[:, :], io["g_loc_packed"][:, :], ld_sc)
    if l0:
        esB = ExitStack()
        PCK = S_tot * E // 128
        sc_pack = esB.enter_context(em.nc.sbuf_tensor(f"{pfx}_scp", [128, PCK], F32))
        msk_p = esB.enter_context(em.nc.sbuf_tensor(f"{pfx}_mskp", [128, PCK], F32))
        em.dma("sync", sc_pack[:, :], io["sc_packed"][:, :], ld_sc)
        KCAP = float(g["CAP"] * S_tot)
        lo, hi, mid, cnt = cB, cC, cD, cE
        em.on("vector", (lambda v: v.memset(lo[:], 0.0)),
              wait=[(ld_sc, 32), (ld_c, n_const0)])
        em.on("vector", (lambda v: v.memset(hi[:], 2.0)))
        for it in range(26):
            em.on("vector", (lambda v: v.tensor_tensor(
                out=mid[:], in0=lo[:], in1=hi[:], op=ALU.add)))
            em.on("vector", (lambda v: v.tensor_scalar(
                out=mid[:], in0=mid[:], scalar1=0.5, scalar2=None, op0=ALU.mult)))
            em.on("vector", (lambda v: v.tensor_scalar(
                out=msk_p[:, :], in0=sc_pack[:, :], scalar1=mid[:], scalar2=None,
                op0=ALU.is_ge)))
            em.on("vector", (lambda v: v.reduce_sum(
                out=cnt[:], in_=msk_p[:, :], axis=AX.X)))
            em.on("tensor",
                  (lambda t: t.matmul(out=psum_b[0:1, 0:1], lhsT=onesc_s[:, :],
                                      rhs=cnt[:, :], start=True, stop=True)),
                  wait=[(vpc, vpc.n), (spc, spc.n)], inc=tb_)
            em.on("scalar", (lambda s: s.copy(out=c11[:, :], in_=psum_b[0:1, 0:1])),
                  wait=[(tb_, tb_.n)])
            em.on("tensor",
                  (lambda t: t.matmul(out=psum_b[:, 1:2], lhsT=ones_s[:, :],
                                      rhs=c11[:, :], start=True, stop=True)),
                  wait=[(spc, spc.n)], inc=tb_)
            em.on("vector", (lambda v: v.tensor_copy(out=cnt[:], in_=psum_b[:, 1:2])),
                  wait=[(tb_, tb_.n)])
            em.on("vector", (lambda v: v.tensor_scalar(
                out=cnt[:], in0=cnt[:], scalar1=KCAP, scalar2=None, op0=ALU.is_ge)))
            em.on("vector", (lambda v: v.tensor_tensor(
                out=cA[:], in0=mid[:], in1=lo[:], op=ALU.subtract)))
            em.on("vector", (lambda v: v.tensor_tensor(
                out=cA[:], in0=cA[:], in1=cnt[:], op=ALU.mult)))
            em.on("vector", (lambda v: v.tensor_tensor(
                out=lo[:], in0=lo[:], in1=cA[:], op=ALU.add)))
            em.on("vector", (lambda v: v.tensor_tensor(
                out=cA[:], in0=hi[:], in1=mid[:], op=ALU.subtract)))
            em.on("vector", (lambda v: v.tensor_tensor(
                out=cA[:], in0=cA[:], in1=cnt[:], op=ALU.mult)))
            em.on("vector", (lambda v: v.tensor_tensor(
                out=hi[:], in0=mid[:], in1=cA[:], op=ALU.add)))
        em.on("vector", (lambda v: v.tensor_scalar(
            out=msk_p[:, 0:ntile_l * E], in0=glp[:, :], scalar1=lo[:],
            scalar2=None, op0=ALU.is_ge)))
        em.on("vector", (lambda v: v.tensor_tensor(
            out=glp[:, :], in0=glp[:, :], in1=msk_p[:, 0:ntile_l * E],
            op=ALU.mult)))
        em.on("vector", (lambda v: v.tensor_scalar(
            out=glp_h[:, :], in0=glp[:, :], scalar1=0.5, scalar2=None,
            op0=ALU.mult)))
        esB.close()
    else:
        em.on("vector", (lambda v: v.memset(cA[:], 0.0)),
              wait=[(ld_sc, 16), (ld_c, n_const0)])
    gate_v = vpc.n

    state = dict(gy_i=0, gy_last=[0, 0], xb_guard=[0, 0],
                 ev=[[0, 0], [0, 0]], yev=[0, 0], wfree=[0, 0],
                 acc_guard=(0, 0))

    def expert_sets(slot):
        if l0:
            return (upA if slot == 0 else upB), None, (dnA if slot == 0 else dnB)
        return ((upA if slot == 0 else upB),
                (up2A if slot == 0 else up2B),
                (dnA if slot == 0 else dnB))

    def shared_up(ft):
        return (upA, ft) if ft < SPL else (upB, ft - SPL)

    def shared_up2(ft):
        return (up2A, ft) if ft < SPL else (up2B, ft - SPL)

    def shared_dn(ft):
        return (dnA, ft) if ft < SPL else (dnB, ft - SPL)

    def emit_block(tb, ups, up2s, dns, ft_n, wl, bias_of, shared):
        slot = tb % 2
        src = io["x1T_d"].rearrange("(dt p) t -> p dt t", p=128)[
            :, :, tb * TBLK:(tb + 1) * TBLK]
        em.dma("gpsimd",
               xblk[slot].rearrange("p (dt c) -> p dt c", dt=dtile),
               src, ld_xb[slot],
               wait=[(st_t, n_xt)] + ([(tup, state["xb_guard"][slot])]
                                      if state["xb_guard"][slot] else []))
        state.setdefault("xb_n", [0, 0])
        state["xb_n"][slot] += 1
        xb_w = (ld_xb[slot], state["xb_n"][slot] * 16)
        for ft in range(ft_n):
            par = ft % 2
            banks = UPB[par]
            evs, evv = state["ev"][par]
            w_first = [xb_w, wl]
            if evs:
                w_first.append((spc, evs))
            if evv:
                w_first.append((vpc, evv))
            if shared:
                ups_, lf = shared_up(ft)
            else:
                ups_, lf = ups, ft
            for kt in range(dtile):
                em.on("tensor",
                      (lambda t, kt=kt, slot=slot, banks=banks, ups_=ups_,
                       lf=lf: t.matmul(
                           out=banks[0][:, 0:TBLK],
                           lhsT=ups_[kt][:, lf * 128:(lf + 1) * 128],
                           rhs=xblk[slot][:, kt * TBLK:(kt + 1) * TBLK],
                           start=(kt == 0), stop=(kt == dtile - 1))),
                      wait=(w_first if kt == 0 else ()),
                      inc=(tup if (l0 and kt == dtile - 1) else None))
            if not l0:
                if shared:
                    u2s_, lf2 = shared_up2(ft)
                else:
                    u2s_, lf2 = up2s, ft
                for kt in range(dtile):
                    em.on("tensor",
                          (lambda t, kt=kt, slot=slot, banks=banks, u2s_=u2s_,
                           lf2=lf2: t.matmul(
                               out=banks[1][:, 0:TBLK],
                               lhsT=u2s_[kt][:, lf2 * 128:(lf2 + 1) * 128],
                               rhs=xblk[slot][:, kt * TBLK:(kt + 1) * TBLK],
                               start=(kt == 0), stop=(kt == dtile - 1))),
                          inc=(tup if kt == dtile - 1 else None))
            dst = hT[:, ft * TBLK:(ft + 1) * TBLK]
            if l0:
                bc, bn = bias_of(ft)
                free = emit_gelu_chain(em, banks[0], dst, TBLK, h_scr, q_scr,
                                       bc, (tup, tup.n),
                                       ld_c if not shared else ld_bs, bn)
            else:
                free = emit_glu_chain(em, banks[0], banks[1], dst, TBLK, h_scr,
                                      (tup, tup.n))
            state["ev"][par] = list(free)
        state["xb_guard"][slot] = tup.n

    def emit_down(tt, tb, dns, ft_n, shared, v_i=None, e=None, hf=None):
        a_loc = tb * TPB + tt
        for dc in range(NDC):
            n0, n1 = dc * 512, min(D, (dc + 1) * 512)
            ybank = DNB[dc % 2]
            dw = [(vpc, state["act_v"])]
            if state["yev"][dc % 2]:
                dw.append((spc, state["yev"][dc % 2]))
            for f in range(ft_n):
                if shared:
                    dns_, lf = shared_dn(f)
                else:
                    dns_, lf = dns, f
                em.on("tensor",
                      (lambda t, tt=tt, f=f, n0=n0, n1=n1, ybank=ybank,
                       dns_=dns_, lf=lf: t.matmul(
                           out=ybank[:, 0:n1 - n0],
                           lhsT=hT[:, f * TBLK + tt * 128:
                                   f * TBLK + (tt + 1) * 128],
                           rhs=dns_[lf][:, n0:n1],
                           start=(f == 0), stop=(f == ft_n - 1))),
                      wait=(dw if f == 0 else ()),
                      inc=tdn if f == ft_n - 1 else None)
            if shared:
                em.on("scalar",
                      (lambda s, ybank=ybank, n0=n0, n1=n1: s.activation(
                          out=gyst[0][:, n0:n1], in_=ybank[:, 0:n1 - n0],
                          func=AF.Identity, scale=0.5 if l0 else 1.0)),
                      wait=[(tdn, tdn.n), (vpc, vpc.n),
                            (st_acc[0], state["acc_total"][0]),
                            (st_acc[1], state["acc_total"][1])])
                state["yev"][dc % 2] = em.pc["scalar"].n
            else:
                gslot = state["gy_i"]
                gc = glp_h if l0 else glp
                gc_col = a_loc * E + e
                gw = [(tdn, tdn.n), (vpc, max(gate_v, vpc.n))]
                if state["gy_last"][gslot]:
                    gw.append((st_acc[gslot], state["gy_last"][gslot]))
                em.on("scalar",
                      (lambda s, ybank=ybank, n0=n0, n1=n1, gc_col=gc_col,
                       gc=gc, gslot=gslot: s.activation(
                           out=gyst[gslot][:, n0:n1],
                           in_=ybank[:, 0:n1 - n0],
                           func=AF.Identity,
                           scale=gc[:, gc_col:gc_col + 1])),
                      wait=gw)
                state["yev"][dc % 2] = em.pc["scalar"].n
                state["gy_last"][gslot] = 0

    NV = E * NHALF
    for v_i in range(NV):
        e = v_i // NHALF
        hf = v_i % NHALF
        wslot = v_i % 2
        ups, up2s, dns = expert_sets(wslot)
        wwait = ([(tup, state["wfree"][wslot]), (tdn, tdn.n)]
                 if state["wfree"][wslot] else [])
        for k in range(dtile):
            em.dma("gpsimd", ups[k][:, 0:Fh],
                   io["W_up_all"][e, k * 128:(k + 1) * 128,
                                  hf * Fh:(hf + 1) * Fh],
                   ld_w[wslot], wait=(wwait if k == 0 else ()))
            if not l0:
                em.dma("gpsimd", up2s[k][:, 0:Fh],
                       io["W_up2_all"][e, k * 128:(k + 1) * 128,
                                       hf * Fh:(hf + 1) * Fh],
                       ld_w[wslot])
        for f in range(fth):
            em.dma("gpsimd", dns[f][:, :],
                   io["W_dn_all"][e, hf * Fh + f * 128: hf * Fh + (f + 1) * 128, :],
                   ld_w[wslot])
        wl = (ld_w[wslot], ld_w[wslot].n)
        if l0 and hf == 0:
            em.dma("sync", b2row_s[:, :], io["b2_all"][e:e + 1, :], ld_b2,
                   wait=[(tb_, tb_.n)])
            emit_bcast_into(em, b2b, b2row_s[0:1, :], D, ones_s,
                            psum_b, tb_, wait=[(ld_c, n_const0),
                                               (ld_b2, ld_b2.n)])
        b2_s = em.pc["scalar"].n

        def bias_of(ft, e=e, hf=hf):
            return (bcols[:, e * (F_e // 128) + hf * fth + ft:
                          e * (F_e // 128) + hf * fth + ft + 1], n_const0)

        for tb in range(NB):
            emit_block(tb, ups, up2s, dns, fth, wl, bias_of, shared=False)
            state["act_v"] = vpc.n
            for tt in range(TPB):
                a_loc = tb * TPB + tt
                gc_col = a_loc * E + e
                state["gy_i"] ^= 1
                gslot = state["gy_i"]
                emit_down(tt, tb, dns, fth, shared=False, v_i=v_i, e=e, hf=hf)
                if l0 and hf == 0:
                    em.on("vector",
                          (lambda v, gc_col=gc_col, gslot=gslot:
                           v.scalar_tensor_tensor(
                               out=gyst[gslot][:, 0:D], in0=b2b[:, 0:D],
                               scalar=glp[:, gc_col:gc_col + 1],
                               in1=gyst[gslot][:, 0:D],
                               op0=ALU.mult, op1=ALU.add)),
                          wait=[(spc, max(spc.n, b2_s))])
                    dma_wait = [(vpc, vpc.n)]
                else:
                    dma_wait = [(spc, spc.n)]
                if v_i > 0:
                    dma_wait += [(st_acc[0], state["acc_guard"][0]),
                                 (st_acc[1], state["acc_guard"][1])]
                em.dma("gpsimd",
                       io["gy_acc"][a_loc * 128:(a_loc + 1) * 128, :],
                       gyst[gslot][:, 0:D], st_acc[gslot],
                       wait=dma_wait,
                       **({} if v_i == 0 else {"accum_op": ALU.add}))
                state["gy_last"][gslot] = st_acc[gslot].n
        state["wfree"][wslot] = tup.n
        state["acc_guard"] = (st_acc[0].n, st_acc[1].n)
    esA.close()

    wgate = [(tup, tup.n), (tdn, tdn.n)]
    for k in range(dtile):
        em.dma("gpsimd", upA[k][:, 0:SPL * 128],
               io["Ws_up"][k * 128:(k + 1) * 128, 0:SPL * 128],
               ld_w[0], wait=(wgate if k == 0 else ()))
        em.dma("gpsimd", upB[k][:, 0:(fts - SPL) * 128],
               io["Ws_up"][k * 128:(k + 1) * 128, SPL * 128:F_s], ld_w[0])
        if not l0:
            em.dma("gpsimd", up2A[k][:, 0:SPL * 128],
                   io["Ws_up2"][k * 128:(k + 1) * 128, 0:SPL * 128], ld_w[0])
            em.dma("gpsimd", up2B[k][:, 0:(fts - SPL) * 128],
                   io["Ws_up2"][k * 128:(k + 1) * 128, SPL * 128:F_s], ld_w[0])
    for f in range(fts):
        dns_, lf = shared_dn(f)
        em.dma("gpsimd", dns_[lf][:, :],
               io["Ws_dn"][f * 128:(f + 1) * 128, :], ld_w[0])
    if l0:
        em.dma("sync", bcols[:, 0:fts], io["bs1_cols"][:, :], ld_bs,
               wait=[(spc, spc.n)])
        bsrow = sb(em, f"{pfx}_bsrow", [1, D], F32)
        em.dma("sync", bsrow[:, :], io["bs2_row"][:, :], ld_bs)
        emit_bcast_into(em, b2b, bsrow[0:1, :], D, ones_s, psum_b, tb_,
                        wait=[(ld_bs, ld_bs.n)])
    wl_sh = (ld_w[0], ld_w[0].n)
    n_bs = ld_bs.n
    bs_done = em.pc["scalar"].n

    xres = [sb(em, f"{pfx}_xres{j}", [128, D], F32) for j in range(2)]
    gyres = [sb(em, f"{pfx}_gyres{j}", [128, D], F32) for j in range(2)]
    lnscr = sb(em, f"{pfx}_lnscr", [128, D], F32)
    xout_t = sb(em, f"{pfx}_xout", [128, D], F32)

    def bias_sh(ft):
        return (bcols[:, ft:ft + 1], n_bs)

    n_acc = (st_acc[0].n, st_acc[1].n)
    state["acc_total"] = n_acc
    for tb in range(NB):
        emit_block(tb, None, None, None, fts, wl_sh, bias_sh, shared=True)
        state["act_v"] = vpc.n
        for tt in range(TPB):
            i_loc = tb * TPB + tt
            rslot = i_loc % 2
            emit_down(tt, tb, None, fts, shared=True)
            em.dma("sync", xres[rslot][:, :],
                   io["x1_in"][i_loc * 128:(i_loc + 1) * 128, :],
                   ld_res[rslot], wait=[(st_t, n_xt), (vpc, vpc.n)])
            em.dma("gpsimd", gyres[rslot][:, :],
                   io["gy_acc"][i_loc * 128:(i_loc + 1) * 128, :],
                   ld_gy[rslot],
                   wait=[(st_acc[0], n_acc[0]), (st_acc[1], n_acc[1]),
                         (vpc, vpc.n)])
            if l0:
                em.on("vector", (lambda v: v.tensor_tensor(
                    out=gyst[0][:, 0:D], in0=gyst[0][:, 0:D], in1=b2b[:, 0:D],
                    op=ALU.add)), wait=[(spc, max(spc.n, bs_done))])
            em.on("vector",
                  (lambda v, rslot=rslot: v.tensor_tensor(
                      out=gyst[0][:, 0:D], in0=gyst[0][:, 0:D],
                      in1=gyres[rslot][:, :], op=ALU.add)),
                  wait=[(spc, spc.n), (ld_gy[rslot], ld_gy[rslot].n)])
            em.on("vector",
                  (lambda v, rslot=rslot: v.tensor_tensor(
                      out=gyst[0][:, 0:D], in0=gyst[0][:, 0:D],
                      in1=xres[rslot][:, :], op=ALU.add)),
                  wait=[(ld_res[rslot], ld_res[rslot].n)])
            emit_ln_tile(em, xout_t[:, :], gyst[0][:, 0:D], D, cA, cB, lnscr)
            em.dma("sync",
                   io["x2_out"][i_loc * 128:(i_loc + 1) * 128, :],
                   xout_t[:, :], st_o, wait=[(spc, spc.n)])
            em.on("scalar", (lambda s: s.nop()), wait=[(st_o, st_o.n)])
    return st_o


def build_attn_program(g, mode):
    nc = bass.Bass("TRN2", target_bir_lowering=False, debug=False)
    B_loc, T, D = g["B_loc"], g["T"], g["D"]
    io = {}
    io["x_in"] = nc.dram_tensor("x_in", [B_loc * T, D], F32, kind="ExternalInput").ap()
    io["W_attn"] = nc.dram_tensor("W_attn", [D, 3 * D], F32, kind="ExternalInput").ap()
    io["b_attn_cols"] = nc.dram_tensor("b_attn_cols", [128, 3 * D // 128], F32,
                                       kind="ExternalInput").ap()
    io["W_proj"] = nc.dram_tensor("W_proj", [D, D], F32, kind="ExternalInput").ap()
    io["b_proj_row"] = nc.dram_tensor("b_proj_row", [1, D], F32, kind="ExternalInput").ap()
    io["idn"] = nc.dram_tensor("idn", [128, 128], BF16, kind="ExternalInput").ap()
    io["ones_f32"] = nc.dram_tensor("ones_f32", [1, 128], F32, kind="ExternalInput").ap()
    io["idn32"] = nc.dram_tensor("idn32", [128, 128], F32, kind="ExternalInput").ap()
    io["gwT"] = nc.dram_tensor("gwT", [D, 8], F32, kind="ExternalInput").ap()
    if mode == "l0":
        io["gain11"] = nc.dram_tensor("gain11", [1, 1], F32, kind="ExternalInput").ap()
        io["trimT"] = nc.dram_tensor("trimT", [128, 2048], BF16, kind="ExternalInput").ap()
        io["scores_out"] = nc.dram_tensor("scores_out", [B_loc * T, 8], F32,
                                          kind="ExternalOutput").ap()
    else:
        for nm in ("cosq", "sinq"):
            io[nm] = nc.dram_tensor(nm, [128, T], F32, kind="ExternalInput").ap()
        io["wdense_out"] = nc.dram_tensor("wdense_out", [B_loc * T, 8], F32,
                                          kind="ExternalOutput").ap()
    io["x1_out"] = nc.dram_tensor("x1_out", [B_loc * T, D], F32, kind="ExternalOutput").ap()
    es = ExitStack()
    with es:
        em = Em(nc, es)
        block = es.enter_context(nc.Block())
        st = emit_attn(em, g, io, mode)
        em.on("gpsimd", (lambda e: e.nop()), wait=[(st, st.n)])
        em.finalize(block)
    return nc


def attn_consts(D):
    tri = np.tril(np.ones((128, 128), np.float32))
    idn = np_bf16(np.eye(128, dtype=np.float32))
    idn32 = np.eye(128, dtype=np.float32)
    ones_f32 = np.ones((1, 128), np.float32)
    r = np.arange(128)[:, None]
    c = np.arange(512)[None, :]
    trimT = np.concatenate(
        [(c >= j * 128 + r).astype(np.float32) for j in range(4)], axis=1)
    return dict(tri=tri, idn=idn, idn32=idn32, ones_f32=ones_f32,
                trimT=np_bf16(trimT))


def rope_consts(T, half=32, theta=500.0, kscale=0.125):
    freqs = 1.0 / theta ** (np.arange(half, dtype=np.float64) / half)
    ang = np.arange(T, dtype=np.float64)[:, None] * freqs[None, :]
    cos = np.cos(ang).astype(np.float32)
    sin = np.sin(ang).astype(np.float32)
    cosT = np.empty((128, T), np.float32)
    sinT = np.empty((128, T), np.float32)
    for r in range(128):
        f = r % half
        cosT[r] = cos[:, f]
        sgn = -1.0 if (r % 64) < half else 1.0
        sinT[r] = sgn * sin[:, f]
    return dict(cosq=cosT, sinq=sinT,
                cosk=np.float32(kscale) * cosT, sink=np.float32(kscale) * sinT)


def build_ffn_program(g, mode):
    nc = bass.Bass("TRN2", target_bir_lowering=False, debug=False)
    Tl, E, D = g["Tl"], g["E"], g["D"]
    F_e, F_s, S_tot = g["F_e"], g["F_s"], g["S_tot"]
    l0 = mode == "l0"
    ntile_l = Tl // 128
    io = {}
    def ein(name, shape):
        io[name] = nc.dram_tensor(name, shape, F32, kind="ExternalInput").ap()
    ein("x1_in", [Tl, D])
    ein("g_loc_packed", [128, ntile_l * E])
    if l0:
        ein("sc_packed", [128, S_tot * E // 128])
        ein("b1_all_cols", [128, E * F_e // 128]); ein("b2_all", [E, D])
        ein("bs1_cols", [128, F_s // 128]); ein("bs2_row", [1, D])
    ein("W_up_all", [E, D, F_e]); ein("W_dn_all", [E, F_e, D])
    ein("Ws_up", [D, F_s]); ein("Ws_dn", [F_s, D])
    if not l0:
        ein("W_up2_all", [E, D, F_e]); ein("Ws_up2", [D, F_s])
    ein("ones_f32", [1, 128]); ein("ones_col", [128, 1])
    io["idn"] = nc.dram_tensor("idn", [128, 128], BF16, kind="ExternalInput").ap()
    io["x2_out"] = nc.dram_tensor("x2_out", [Tl, D], F32, kind="ExternalOutput").ap()
    io["x1T_d"] = nc.dram_tensor("x1T_d", [D, Tl], BF16).ap()
    io["gy_acc"] = nc.dram_tensor("gy_acc", [Tl, D], F32).ap()
    es = ExitStack()
    with es:
        em = Em(nc, es)
        block = es.enter_context(nc.Block())
        st = emit_moe_ffn(em, g, io, mode)
        em.on("gpsimd", (lambda e: e.nop()), wait=[(st, st.n)])
        em.finalize(block)
    return nc



_B, _T, _D, _H, _E = 16, 2048, 1024, 16, 8


def _cols(b):
    return np.ascontiguousarray(np.asarray(b).reshape(-1, 128).T)


def _pack_gate(s, E):
    a = s.shape[0] // 128
    return np.ascontiguousarray(
        np.asarray(s).reshape(a, 128, E).transpose(1, 0, 2).reshape(128, a * E))


def kernel(**inputs):
    ii = {k: np.asarray(v) for k, v in inputs.items()}
    B, T, D, H, E = _B, _T, _D, _H, _E
    R = N_CORES
    B_loc = B // R
    Tl = B_loc * T
    S_tot = B * T
    cores = list(range(R))
    cst = attn_consts(D)
    ones_col = np.ones((128, 1), np.float32)
    g_at = dict(B_loc=B_loc, T=T, D=D, H=H)

    x_sh = [np.ascontiguousarray(ii["L0_attn_x"][2 * c:2 * c + 2].reshape(Tl, D))
            for c in range(R)]

    nc1 = build_attn_program(g_at, "l0")
    common1 = dict(
        W_attn=ii["L0_attn_W_attn"],
        b_attn_cols=_cols(ii["L0_attn_b_attn"]),
        W_proj=ii["L0_attn_W_proj"],
        b_proj_row=ii["L0_attn_b_proj"].reshape(1, D),
        gain11=ii["L0_attn_score_gain"].reshape(1, 1),
        gwT=np.ascontiguousarray(ii["L0_ffn_gate_weight"].T),
        trimT=cst["trimT"], idn=cst["idn"], idn32=cst["idn32"],
        ones_f32=cst["ones_f32"],
    )
    res1 = run_bass_kernel_spmd(
        nc1, [dict(common1, x_in=x_sh[c]) for c in range(R)], core_ids=cores)
    x1_sh = [res1.results[c]["x1_out"] for c in range(R)]
    sc_all = np.concatenate([res1.results[c]["scores_out"] for c in range(R)], 0)
    sc_packed = _pack_gate(sc_all, E)

    g_f0 = dict(Tl=Tl, E=E, D=D, F_e=4096, F_s=4096, TBLK=256, CAP=2,
                S_tot=S_tot)
    nc2 = build_ffn_program(g_f0, "l0")
    maps2 = []
    for c in range(R):
        maps2.append(dict(
            x1_in=x1_sh[c],
            g_loc_packed=_pack_gate(sc_all[c * Tl:(c + 1) * Tl], E),
            sc_packed=sc_packed,
            W_up_all=ii["L0_ffn_W1"], W_dn_all=ii["L0_ffn_W2"],
            b1_all_cols=_cols(ii["L0_ffn_b1"]),
            b2_all=ii["L0_ffn_b2"],
            Ws_up=ii["L0_ffn_Ws1"], Ws_dn=ii["L0_ffn_Ws2"],
            bs1_cols=_cols(ii["L0_ffn_bs1"]),
            bs2_row=ii["L0_ffn_bs2"].reshape(1, D),
            ones_f32=cst["ones_f32"], ones_col=ones_col, idn=cst["idn"]))
    res2 = run_bass_kernel_spmd(nc2, maps2, core_ids=cores)
    x2_sh = [res2.results[c]["x2_out"] for c in range(R)]

    nc3 = build_attn_program(g_at, "l1")
    rc = rope_consts(T, 32, 500.0, 1.0)
    common3 = dict(
        W_attn=ii["L1_attn_w_qkv"],
        b_attn_cols=_cols(ii["L1_attn_b_qkv"]),
        W_proj=ii["L1_attn_w_out"],
        b_proj_row=ii["L1_attn_b_out"].reshape(1, D),
        gwT=np.ascontiguousarray(ii["L1_ffn_gate_w"].T),
        cosq=rc["cosq"], sinq=rc["sinq"],
        idn=cst["idn"], idn32=cst["idn32"], ones_f32=cst["ones_f32"],
    )
    res3 = run_bass_kernel_spmd(
        nc3, [dict(common3, x_in=x2_sh[c]) for c in range(R)], core_ids=cores)
    x3_sh = [res3.results[c]["x1_out"] for c in range(R)]
    wd_sh = [res3.results[c]["wdense_out"] for c in range(R)]

    g_f1 = dict(Tl=Tl, E=E, D=D, F_e=1408, F_s=2816, TBLK=256, CAP=0,
                S_tot=S_tot)
    nc4 = build_ffn_program(g_f1, "l1")
    maps4 = []
    for c in range(R):
        maps4.append(dict(
            x1_in=x3_sh[c],
            g_loc_packed=_pack_gate(wd_sh[c], E),
            W_up_all=ii["L1_ffn_wg"], W_up2_all=ii["L1_ffn_wu"],
            W_dn_all=ii["L1_ffn_wd"],
            Ws_up=ii["L1_ffn_swg"], Ws_up2=ii["L1_ffn_swu"],
            Ws_dn=ii["L1_ffn_swd"],
            ones_f32=cst["ones_f32"], ones_col=ones_col, idn=cst["idn"]))
    res4 = run_bass_kernel_spmd(nc4, maps4, core_ids=cores)
    out = np.concatenate([res4.results[c]["x2_out"] for c in range(R)], 0)
    return out.reshape(B, T, D).astype(np.float32)
